# version 03
# baseline (speedup 1.0000x reference)
import contextlib
import math
import numpy as np
import ml_dtypes
import concourse.bass as bass
import concourse.mybir as mybir
from concourse.bass_utils import run_bass_kernel_spmd

F32 = mybir.dt.float32
BF16 = mybir.dt.bfloat16
I32 = mybir.dt.int32
AF = mybir.ActivationFunctionType
ALU = mybir.AluOpType
AX = mybir.AxisListType
BF = ml_dtypes.bfloat16

B, S, D, H = 2, 8192, 1024, 4
DK = DV = 256
CH = 32
TPC = 2048
DH = 4096
NCORE = 8

ENGS = ("sync", "gpsimd", "tensor", "vector", "scalar")

_nc_cache = {}


class Ctr:
    def __init__(self, p, name):
        self.s = p.stack.enter_context(p.nc.semaphore(name))
        self.n = 0
        self.name = name


class Prog:
    def __init__(self, name):
        self.nc = bass.Bass("TRN2", target_bir_lowering=False)
        self.stack = contextlib.ExitStack()
        self.name = name
        self.sch = []
        self.ctr = {e: Ctr(self, "c_" + e) for e in ENGS}

    def sb(self, name, shape, dt=F32):
        return self.stack.enter_context(self.nc.sbuf_tensor(name, shape, dt))

    def ps(self, name, shape, dt=F32):
        return self.stack.enter_context(self.nc.psum_tensor(name, shape, dt))

    def din(self, name, shape, dt=F32):
        return self.nc.dram_tensor(name, shape, dt, kind="ExternalInput").ap()

    def dout(self, name, shape, dt=F32):
        return self.nc.dram_tensor(name, shape, dt, kind="ExternalOutput").ap()

    def ctr_new(self, name):
        return Ctr(self, name)

    def _emit_deps(self, eng, deps):
        for d in deps:
            if d is None:
                continue
            dc, tgt = d
            if isinstance(dc, str):
                dc = self.ctr[dc]
            if tgt is None or tgt <= 0:
                continue
            self.sch.append((eng, lambda e, s=dc.s, t=tgt: e.wait_ge(s, t)))

    def do(self, eng, fn, deps=()):
        c = self.ctr[eng]
        self._emit_deps(eng, deps)
        c.n += 1
        tgt = c.n
        self.sch.append((eng, lambda e, fn=fn, s=c.s: fn(e).then_inc(s, 1)))
        return (eng, tgt)

    def dma(self, eng, dst, src, ctr, deps=(), slow=False):
        self._emit_deps(eng, deps)
        kw = {"allow_slow_non_contiguous": True} if slow else {}
        ctr.n += 16
        self.sch.append((eng, lambda e, d=dst, s=src: e.dma_start(
            d, s, **kw).then_inc(ctr.s, 16)))
        return (ctr, ctr.n)

    def wait(self, eng, *deps):
        self._emit_deps(eng, deps)

    def finish(self):
        with self.stack:
            with self.nc.Block() as block:
                for engname in ENGS:
                    if not any(e == engname for e, _ in self.sch):
                        continue

                    def make(engname):
                        def body(eng):
                            for e, fn in self.sch:
                                if e == engname:
                                    fn(eng)
                        return body

                    getattr(block, engname)(make(engname))
        return self.nc


def r8(x, p=128):
    return x.rearrange("(a p) t -> p a t", p=p)


def s8(t, ntiles):
    return t[:].rearrange("p (a t) -> p a t", a=ntiles)


class PP:

    def __init__(self, tiles):
        self.t = tiles
        self.free_at = [None] * len(tiles)
        self.i = 0

    def acq(self):
        i = self.i % len(self.t)
        self.i += 1
        return i, self.t[i], self.free_at[i]

    def rel(self, i, tok):
        self.free_at[i] = tok


def mm_group(p, psum_ap, pairs, deps=()):
    n = len(pairs)
    tok = None
    for idx, (lhsT, rhs) in enumerate(pairs):
        d = deps if idx == 0 else ()
        tok = p.do("tensor", lambda e, l=lhsT, r=rhs, idx=idx: e.matmul(
            psum_ap, l, r, start=(idx == 0), stop=(idx == n - 1),
            skip_group_check=True), deps=d)
    return tok



def build_l0norm():
    p = Prog("l0norm")
    xF = p.din("xF", [D, TPC])
    onesr = p.din("onesr", [1, 128])
    onescl = p.din("onescl", [128, 1])
    identc = p.din("identc", [128, 128])

    o_xnT = p.dout("xnT", [TPC, D], BF16)
    o_xnF = p.dout("xnF", [D, TPC], BF16)

    sxF = p.sb("sxF", [128, 8 * TPC])
    sxn = p.sb("sxn", [128, 8 * TPC], BF16)
    sxb = p.sb("sxb", [128, 8 * 512], BF16)
    sxsq = p.sb("sxsq", [128, 8 * 512], BF16)
    sonesf = p.sb("sonesf", [1, 128])
    sonescol = p.sb("sonescol", [128, 1], BF16)
    sident = p.sb("sident", [128, 128], BF16)
    srow = p.sb("srow", [1, 3 * 512])
    smean = p.sb("smean", [1, 512])
    srstd = p.sb("srstd", [1, 512])
    sxnTb = [p.sb(f"sxnTb{i}", [128, D], BF16) for i in range(2)]
    stmp = p.sb("stmp", [128, 8 * 512])

    pr1 = p.ps("pr1", [1, 512])
    pr2 = p.ps("pr2", [1, 512])
    pbc1 = p.ps("pbc1", [128, 512])
    pbc2 = p.ps("pbc2", [128, 512])
    ptr0 = p.ps("ptr0", [128, 128], BF16)
    ptr1 = p.ps("ptr1", [128, 128], BF16)

    ld = p.ctr_new("ld")
    lg = p.ctr_new("lg")
    stt = [p.ctr_new(f"stt{i}") for i in range(2)]
    stf = p.ctr_new("stf")

    p.dma("sync", s8(sxF, 8), r8(xF), ld)
    p.dma("sync", sonesf[:], onesr, ld)
    p.dma("gpsimd", sident[:], identc, lg)
    p.dma("gpsimd", sonescol[:], onescl, lg)
    lds = [(ld, ld.n), (lg, lg.n)]

    NB = TPC // 512
    t_stats_prev = None
    v_apply_last = None
    for nb in range(NB):
        c0 = nb * 512
        v_cast = None
        s_sq = None
        for kt in range(8):
            v_cast = p.do("vector", lambda e, kt=kt, c0=c0: e.tensor_copy(
                sxb[:, kt * 512:(kt + 1) * 512],
                sxF[:, kt * TPC + c0: kt * TPC + c0 + 512]),
                deps=(lds + [t_stats_prev]) if kt == 0 else ())
        for kt in range(8):
            s_sq = p.do("scalar", lambda e, kt=kt, c0=c0: e.activation(
                sxsq[:, kt * 512:(kt + 1) * 512],
                sxF[:, kt * TPC + c0: kt * TPC + c0 + 512], AF.Square),
                deps=(lds + [t_stats_prev]) if kt == 0 else ())
        mm_group(p, pr1[:], [(sonescol[:], sxb[:, kt * 512:(kt + 1) * 512])
                             for kt in range(8)],
                 deps=[v_cast, s_sq, v_apply_last] + lds)
        t_stats = mm_group(p, pr2[:],
                           [(sonescol[:], sxsq[:, kt * 512:(kt + 1) * 512])
                            for kt in range(8)])
        t_stats_prev = t_stats

        v_mean = p.do("vector", lambda e: e.tensor_scalar(
            smean[:], pr1[:], 1.0 / D, None, ALU.mult), deps=[t_stats])
        v_msq = p.do("vector", lambda e: e.tensor_tensor(
            srow[:, 0:512], smean[:], smean[:], ALU.mult), deps=[v_mean])
        v_var = p.do("vector", lambda e: e.scalar_tensor_tensor(
            srow[:, 512:1024], pr2[:], 1.0 / D, srow[:, 0:512], ALU.mult,
            ALU.subtract), deps=[v_msq])
        v_eps = p.do("vector", lambda e: e.tensor_scalar(
            srow[:, 512:1024], srow[:, 512:1024], 1e-5, None, ALU.add),
            deps=[v_var])
        s_std = p.do("scalar", lambda e: e.activation(
            srow[:, 1024:1536], srow[:, 512:1024], AF.Sqrt), deps=[v_eps])
        v_rst = p.do("vector", lambda e: e.reciprocal(
            srstd[:], srow[:, 1024:1536]), deps=[s_std])

        p.do("tensor", lambda e: e.matmul(
            pbc1[:], sonesf[:], smean[:], start=True, stop=True),
            deps=[v_rst, v_apply_last])
        t_bc = p.do("tensor", lambda e: e.matmul(
            pbc2[:], sonesf[:], srstd[:], start=True, stop=True))

        v_apply_last = None
        for kt in range(8):
            v_sub = p.do("vector", lambda e, kt=kt, c0=c0: e.tensor_tensor(
                stmp[:, kt * 512:(kt + 1) * 512],
                sxF[:, kt * TPC + c0: kt * TPC + c0 + 512], pbc1[:],
                ALU.subtract), deps=[t_bc] if kt == 0 else ())
            v_apply_last = p.do("vector", lambda e, kt=kt, c0=c0: e.tensor_tensor(
                sxn[:, kt * TPC + c0: kt * TPC + c0 + 512],
                stmp[:, kt * 512:(kt + 1) * 512], pbc2[:], ALU.mult),
                deps=[v_sub])
    v_xn = v_apply_last

    p.dma("sync", r8(o_xnF), s8(sxn, 8), stf, deps=[v_xn])

    ptr = PP([ptr0, ptr1])
    sbuf_free = [None, None]
    for nt in range(16):
        j = nt % 2
        v_tr = None
        for kt in range(8):
            i, pt, freeat = ptr.acq()
            t_tr = p.do("tensor", lambda e, nt=nt, kt=kt, pt=pt: e.transpose(
                pt[:], sxn[:, kt * TPC + nt * 128: kt * TPC + nt * 128 + 128],
                sident[:]), deps=[v_xn, freeat])
            v_tr = p.do("vector", lambda e, kt=kt, j=j, pt=pt: e.tensor_copy(
                sxnTb[j][:, kt * 128:(kt + 1) * 128], pt[:]),
                deps=[t_tr] + ([sbuf_free[j]] if kt == 0 else []))
            ptr.rel(i, v_tr)
        sbuf_free[j] = p.dma("sync", r8(o_xnT)[:, nt:nt + 1, :],
                             sxnTb[j][:].rearrange("p (a t) -> p a t", a=1),
                             stt[j], deps=[v_tr])
    p.wait("sync", (stf, stf.n), (stt[0], stt[0].n), (stt[1], stt[1].n))
    return p.finish()



def build_l0qk():
    p = Prog("l0qk")
    xnF = p.din("xnF", [D, TPC], BF16)
    W1 = p.din("W1", [D, D])
    W2 = p.din("W2", [D, D])
    b1c = p.din("b1c", [128, 8])
    b2c = p.din("b2c", [128, 8])
    onescl = p.din("onescl", [128, 1])

    o_qT = p.dout("qT", [D, TPC], BF16)
    o_kT = p.dout("kT", [D, TPC], BF16)
    o_brow = p.dout("brow", [1, TPC])

    sxn = p.sb("sxn", [128, 8 * TPC], BF16)
    sW1 = p.sb("sW1", [128, 8 * D], BF16)
    sW2 = p.sb("sW2", [128, 8 * D], BF16)
    sb1 = p.sb("sb1", [128, 8])
    sb2 = p.sb("sb2", [128, 8])
    sonescol = p.sb("sonescol", [128, 1], BF16)
    sqk = p.sb("sqk", [128, 8 * TPC], BF16)
    ssq = p.sb("ssq", [128, 8 * 512], BF16)
    sqss = p.sb("sqss", [1, TPC])
    skss = p.sb("skss", [1, TPC])
    skmax = p.sb("skmax", [1, 1])
    sbrow = p.sb("sbrow", [1, TPC])

    pr1 = p.ps("pr1", [1, 512])
    pk0 = p.ps("pk0", [128, 512])
    pk1 = p.ps("pk1", [128, 512])

    ld = p.ctr_new("ld")
    lg = p.ctr_new("lg")
    st = p.ctr_new("st")

    p.dma("sync", s8(sxn, 8), r8(xnF), ld)
    p.dma("sync", sb1[:], b1c, ld)
    p.dma("sync", sb2[:], b2c, ld)
    p.dma("gpsimd", s8(sW1, 8), r8(W1), lg)
    p.dma("gpsimd", s8(sW2, 8), r8(W2), lg)
    p.dma("gpsimd", sonescol[:], onescl, lg)
    lds = [(ld, ld.n), (lg, lg.n)]

    NB = TPC // 512
    pp = PP([pk0, pk1])
    st_qk = {}
    v_pr_cons = None
    for which, (sW, sbias, srow_out, o_out) in enumerate(
            ((sW1, sb1, sqss, o_qT), (sW2, sb2, skss, o_kT))):
        t_ssq_cons = None
        v_store_dep = None
        for nb in range(NB):
            c0 = nb * 512
            for mt in range(8):
                i, pkt, freeat = pp.acq()
                deps = [freeat]
                if nb == 0 and mt == 0:
                    deps += lds if which == 0 else [st_qk[0]]
                t_mm = mm_group(
                    p, pkt[:],
                    [(sW[:, kt * D + mt * 128: kt * D + mt * 128 + 128],
                      sxn[:, kt * TPC + c0: kt * TPC + c0 + 512])
                     for kt in range(8)], deps=deps)
                v_ev = p.do("vector", lambda e, mt=mt, c0=c0, sbias=sbias,
                            pkt=pkt: e.tensor_scalar(
                    sqk[:, mt * TPC + c0: mt * TPC + c0 + 512], pkt[:],
                    sbias[:, mt:mt + 1], None, ALU.add), deps=[t_mm])
                pp.rel(i, v_ev)
                v_store_dep = v_ev
            v_sq = None
            for kt in range(8):
                v_sq = p.do("vector", lambda e, kt=kt, c0=c0: e.tensor_tensor(
                    ssq[:, kt * 512:(kt + 1) * 512],
                    sqk[:, kt * TPC + c0: kt * TPC + c0 + 512],
                    sqk[:, kt * TPC + c0: kt * TPC + c0 + 512], ALU.mult),
                    deps=[t_ssq_cons, v_store_dep] if kt == 0 else ())
            t_ssq_cons = mm_group(
                p, pr1[:], [(sonescol[:], ssq[:, kt * 512:(kt + 1) * 512])
                            for kt in range(8)], deps=[v_sq, v_pr_cons])
            v_pr_cons = p.do("vector", lambda e, c0=c0, srow_out=srow_out:
                             e.tensor_copy(srow_out[:, c0:c0 + 512], pr1[:]),
                             deps=[t_ssq_cons])
        st_qk[which] = p.dma("sync", r8(o_out), s8(sqk, 8), st,
                             deps=[v_store_dep, v_pr_cons])

    v_red = p.do("vector", lambda e: e.tensor_reduce(
        skmax[:], skss[:], AX.X, ALU.max), deps=[v_pr_cons])
    v_brow = p.do("vector", lambda e: e.tensor_scalar(
        sbrow[:], sqss[:], skmax[:], -0.5, ALU.add, ALU.mult), deps=[v_red])
    p.dma("sync", o_brow, sbrow[:], st, deps=[v_brow])
    p.wait("sync", (st, st.n))
    return p.finish()



def build_l0flash():
    p = Prog("l0flash")
    qT = p.din("qT", [D, TPC], BF16)
    browi = p.din("brow", [1, TPC])
    kTt = p.din("kTt", [64, 8, 128, 128], BF16)
    xnTt = p.din("xnTt", [8, 64, 128, 128], BF16)
    W3 = p.din("W3", [D, D])
    W4 = p.din("W4", [D, D])
    b3c = p.din("b3c", [128, 8])
    b4c = p.din("b4c", [128, 8])
    xF = p.din("xF", [D, TPC])
    onesr = p.din("onesr", [1, 128])
    ones128 = p.din("ones128", [128, 128])

    o_acc = p.dout("acc", [D, TPC])

    sqT = p.sb("sqT", [128, 8 * TPC], BF16)
    sxnTe = [p.sb(f"sxnTe{i}", [128, 64 * 128], BF16) for i in range(2)]
    skc = [p.sb(f"skc{i}", [128, 8 * 128], BF16) for i in range(3)]
    sW3 = p.sb("sW3", [128, 8 * D], BF16)
    sW4 = p.sb("sW4", [128, 8 * D], BF16)
    sb3 = p.sb("sb3", [128, 8])
    sb4 = p.sb("sb4", [128, 8])
    sbrow = p.sb("sbrow", [1, TPC], BF16)
    sonesrb = p.sb("sonesrb", [1, 128], BF16)
    sones128 = p.sb("sones128", [128, 128], BF16)
    sexp = p.sb("sexp", [128, 64 * 256], BF16)
    sbrc = p.sb("sbrc", [128, 256])
    sZn = p.sb("sZn", [128, 8 * 256], BF16)
    so3 = p.sb("so3", [128, 8 * 256], BF16)
    sxFb = p.sb("sxFb", [128, 8 * 256])
    sacc = p.sb("sacc", [128, 8 * 256])

    psc = [p.ps("psc0", [128, 512]), p.ps("psc1", [128, 512])]
    pden = p.ps("pden", [128, 512])
    pz = [p.ps("pz0", [128, 512]), p.ps("pz1", [128, 512])]
    po = [p.ps("po0", [128, 512]), p.ps("po1", [128, 512])]

    ld = p.ctr_new("ld")
    lg = p.ctr_new("lg")
    lk = [p.ctr_new(f"lk{i}") for i in range(3)]
    le = [p.ctr_new(f"le{i}") for i in range(2)]
    lf = p.ctr_new("lf")
    st = p.ctr_new("st")

    p.dma("sync", s8(sqT, 8), r8(qT), ld)
    p.dma("sync", sb3[:], b3c, ld)
    p.dma("sync", sb4[:], b4c, ld)
    p.dma("gpsimd", s8(sW3, 8), r8(W3), lg)
    p.dma("gpsimd", s8(sW4, 8), r8(W4), lg)
    p.dma("gpsimd", sbrow[:], browi, lg)
    p.dma("gpsimd", sonesrb[:], onesr, lg)
    p.dma("gpsimd", sones128[:], ones128, lg)
    lds = [(ld, ld.n), (lg, lg.n)]

    NKT = S // 128
    NQB = TPC // 256

    k_loaded = {}
    t_kdone = {}
    s_expdone = {}
    t_z_last = None
    pzpp = PP(pz)
    opp = PP(po)
    xnbuf_free = [None, None]
    xn_le = [None, None]
    v_zn_last = None
    v_rec_prev = None
    v_acc_prev = None
    st_acc_prev = None

    for qb in range(NQB):
        q0 = qb * 256
        lf_prev = p.dma("sync", s8(sxFb, 8), r8(xF)[:, :, q0:q0 + 256], lf,
                        deps=[v_acc_prev])

        for kt in range(NKT):
            idx = qb * NKT + kt
            j = idx % 3
            k_loaded[idx] = p.dma(
                "gpsimd", s8(skc[j], 8), kTt[kt].rearrange("a p t -> p a t"),
                lk[j], deps=[t_kdone.get(idx - 3)])

            par = idx % 2
            deps = [k_loaded[idx], s_expdone.get(idx - 2)]
            if idx == 0:
                deps += lds
            pairs = [(sonesrb[:], sbrow[:, q0:q0 + 256])]
            pairs += [(skc[j][:, et * 128:(et + 1) * 128],
                       sqT[:, et * TPC + q0: et * TPC + q0 + 256])
                      for et in range(8)]
            t_sc = mm_group(p, psc[par][:, 0:256], pairs, deps=deps)
            t_kdone[idx] = t_sc

            s_expdone[idx] = p.do("scalar", lambda e, kt=kt, par=par: e.activation(
                sexp[:, kt * 256:(kt + 1) * 256], psc[par][:, 0:256], AF.Exp,
                scale=1.0 / 32.0),
                deps=[t_sc] + ([t_z_last] if kt == 0 else []))

        t_den = mm_group(p, pden[:, 0:256],
                         [(sones128[:], sexp[:, kt * 256:(kt + 1) * 256])
                          for kt in range(NKT)],
                         deps=[s_expdone[qb * NKT + NKT - 1], v_rec_prev])
        v_rec = p.do("vector", lambda e: e.reciprocal(sbrc[:], pden[:, 0:256]),
                     deps=[t_den, v_zn_last])
        v_rec_prev = v_rec

        for et in range(2):
            jj = et % 2
            xn_le[jj] = p.dma("sync", s8(sxnTe[jj], 64),
                              xnTt[et].rearrange("a p t -> p a t"), le[jj],
                              deps=[xnbuf_free[jj]])
        for et in range(8):
            jj = et % 2
            i, pzt, freeat = pzpp.acq()
            t_z = mm_group(p, pzt[:, 0:256],
                           [(sxnTe[jj][:, kt * 128:(kt + 1) * 128],
                             sexp[:, kt * 256:(kt + 1) * 256])
                            for kt in range(NKT)],
                           deps=[xn_le[jj], freeat])
            t_z_last = t_z
            xnbuf_free[jj] = t_z
            if et + 2 < 8:
                xn_le[jj] = p.dma("sync", s8(sxnTe[jj], 64),
                                  xnTt[et + 2].rearrange("a p t -> p a t"),
                                  le[jj], deps=[t_z])
            v_zn_last = p.do("vector", lambda e, et=et, pzt=pzt: e.tensor_tensor(
                sZn[:, et * 256:(et + 1) * 256], pzt[:, 0:256], sbrc[:],
                ALU.mult), deps=[t_z, v_rec])
            pzpp.rel(i, v_zn_last)

        v_so3_last = None
        for mt in range(8):
            i, pot, freeat = opp.acq()
            t_mm = mm_group(p, pot[:, 0:256],
                            [(sW3[:, kt * D + mt * 128: kt * D + mt * 128 + 128],
                              sZn[:, kt * 256:(kt + 1) * 256])
                             for kt in range(8)], deps=[v_zn_last, freeat])
            v_so3_last = p.do("vector", lambda e, mt=mt, pot=pot: e.tensor_scalar(
                so3[:, mt * 256:(mt + 1) * 256], pot[:, 0:256],
                sb3[:, mt:mt + 1], None, ALU.add), deps=[t_mm])
            opp.rel(i, v_so3_last)

        for mt in range(8):
            i, pot, freeat = pzpp.acq()
            t_mm = mm_group(p, pot[:, 0:256],
                            [(sW4[:, kt * D + mt * 128: kt * D + mt * 128 + 128],
                              so3[:, kt * 256:(kt + 1) * 256])
                             for kt in range(8)], deps=[v_so3_last, freeat])
            v_b4 = p.do("vector", lambda e, mt=mt, pot=pot: e.tensor_scalar(
                sacc[:, mt * 256:(mt + 1) * 256], pot[:, 0:256],
                sb4[:, mt:mt + 1], None, ALU.add),
                deps=[t_mm, lf_prev, st_acc_prev])
            v_acc_prev = p.do("vector", lambda e, mt=mt: e.tensor_tensor(
                sacc[:, mt * 256:(mt + 1) * 256],
                sacc[:, mt * 256:(mt + 1) * 256],
                sxFb[:, mt * 256:(mt + 1) * 256], ALU.add), deps=[v_b4])
            pzpp.rel(i, v_acc_prev)

        st_acc_prev = p.dma("sync", r8(o_acc)[:, :, q0:q0 + 256], s8(sacc, 8),
                            st, deps=[v_acc_prev])

    p.wait("sync", (st, st.n))
    return p.finish()



def build_wstat():
    p = Prog("wstat")
    wins = [p.din("wsl0", [512, 1024]), p.din("wsl1", [128, 4096]),
            p.din("wsl2", [512, 1024]), p.din("wsl3", [128, 4096])]
    onescl = p.din("onescl", [128, 1])
    o_part = p.dout("part", [1, 4])

    sw = p.sb("sw", [128, 4 * 4096])
    scol = p.sb("scol", [128, 4])
    sacc = p.sb("sacc", [128, 1])
    sonescf = p.sb("sonescf", [128, 1])
    srow = p.sb("srow", [1, 32])
    sout = p.sb("sout", [1, 4])
    ptr = p.ps("ptr", [1, 512])

    ld = p.ctr_new("ld")
    lg = p.ctr_new("lg")
    st = p.ctr_new("st")

    p.dma("sync", sonescf[:], onescl, ld)
    ldtoks = []
    for j, w in enumerate(wins):
        na = 4 if w.tensor.shape[0] == 512 else 1
        ldtoks.append(p.dma(
            "sync", sw[:, j * 4096:(j + 1) * 4096].rearrange(
                "p (a t) -> p a t", a=na), r8(w), ld))

    prev = None
    for j, w in enumerate(wins):
        na = 4 if w.tensor.shape[0] == 512 else 1
        base = j * 4096
        width = 4096 // na
        r = None
        for a in range(na):
            r = p.do("vector", lambda e, a=a, base=base, width=width, j=j:
                     e.tensor_reduce(
                scol[:, a:a + 1], sw[:, base + a * width: base + (a + 1) * width],
                AX.X, ALU.add, apply_absolute_value=True),
                deps=[(ld, ld.n), prev] if a == 0 else ())
        if na == 4:
            r = p.do("vector", lambda e: e.tensor_tensor(
                scol[:, 0:1], scol[:, 0:1], scol[:, 1:2], ALU.add), deps=[r])
            r = p.do("vector", lambda e: e.tensor_tensor(
                scol[:, 2:3], scol[:, 2:3], scol[:, 3:4], ALU.add), deps=[r])
            r = p.do("vector", lambda e: e.tensor_tensor(
                sacc[:], scol[:, 0:1], scol[:, 2:3], ALU.add), deps=[r])
        else:
            r = p.do("vector", lambda e: e.tensor_copy(
                sacc[:], scol[:, 0:1]), deps=[r])
        t = p.do("tensor", lambda e: e.matmul(
            ptr[0:1, 0:1], sonescf[:], sacc[:], start=True, stop=True),
            deps=[r, prev])
        prev = p.do("vector", lambda e, j=j: e.tensor_copy(
            sout[:, j:j + 1], ptr[0:1, 0:1]), deps=[t])
    p.dma("sync", o_part, sout[:], st, deps=[prev])
    p.wait("sync", (st, st.n))
    return p.finish()



def build_wquant():
    p = Prog("wquant")
    wins = [p.din("wsl0", [512, 1024]), p.din("wsl1", [128, 4096]),
            p.din("wsl2", [512, 1024]), p.din("wsl3", [128, 4096])]
    parts = p.din("parts", [8, 4])
    identc = p.din("identc", [128, 128])
    onesr = p.din("onesr", [1, 128])

    outs = [p.dout("wq0", [1024, 512], BF16), p.dout("wq1", [4096, 128], BF16),
            p.dout("wq2", [1024, 512], BF16), p.dout("wq3", [4096, 128], BF16)]
    o_invw = p.dout("invw", [1, 4])

    sw = p.sb("sw", [128, 4 * 4096])
    swq = p.sb("swq", [128, 4096], BF16)
    swi = p.sb("swi", [128, 4096], I32)
    swt = p.sb("swt", [128, 32 * 128], BF16)
    sparts = p.sb("sparts", [8, 4])
    sident = p.sb("sident", [128, 128])
    sidentb = p.sb("sidentb", [128, 128], BF16)
    sonesf = p.sb("sonesf", [1, 128])
    sr4 = p.sb("sr4", [4, 8])
    sc4 = p.sb("sc4", [4, 1])
    ssc = p.sb("ssc", [1, 4])
    sinv = p.sb("sinv", [1, 4])
    sscol = p.sb("sscol", [128, 4])

    ptr = p.ps("ptr", [1, 512])
    ptrb = [p.ps(f"ptrb{i}", [128, 128], BF16) for i in range(2)]
    pbc = p.ps("pbc", [128, 512])

    ld = p.ctr_new("ld")
    lg = p.ctr_new("lg")
    lw = p.ctr_new("lw")
    st = p.ctr_new("st")

    p.dma("sync", sparts[:], parts, ld)
    p.dma("sync", sonesf[:], onesr, ld)
    p.dma("gpsimd", sident[:], identc, lg)
    p.dma("gpsimd", sidentb[:], identc, lg)
    lds = [(ld, ld.n), (lg, lg.n)]

    t = p.do("tensor", lambda e: e.transpose(pbc[0:4, 0:8], sparts[:],
                                             sident[0:8, 0:8]), deps=lds)
    v = p.do("vector", lambda e: e.tensor_copy(sr4[:], pbc[0:4, 0:8]), deps=[t])
    v = p.do("vector", lambda e: e.tensor_reduce(
        sc4[:], sr4[:], AX.X, ALU.add), deps=[v])
    v = p.do("vector", lambda e: e.tensor_scalar(
        sc4[:], sc4[:], 1.0 / (DH * D), 1e-5, ALU.mult, ALU.max), deps=[v])
    t2 = p.do("tensor", lambda e: e.transpose(pbc[0:1, 0:4], sc4[:],
                                              sident[0:4, 0:4]), deps=[v])
    v_inv = p.do("vector", lambda e: e.tensor_copy(sinv[:], pbc[0:1, 0:4]),
                 deps=[t2])
    p.dma("sync", o_invw, sinv[:], st, deps=[v_inv])
    v_rec = p.do("vector", lambda e: e.reciprocal(ssc[:], sinv[:]),
                 deps=[v_inv])
    t3 = p.do("tensor", lambda e: e.matmul(
        pbc[:, 0:4], sonesf[:], ssc[:], start=True, stop=True), deps=[v_rec])
    v_sc = p.do("vector", lambda e: e.tensor_copy(sscol[:], pbc[:, 0:4]),
                deps=[t3])

    wtoks = []
    for j, w in enumerate(wins):
        na = 4 if w.tensor.shape[0] == 512 else 1
        wtoks.append(p.dma("sync", sw[:, j * 4096:(j + 1) * 4096].rearrange(
            "p (a t) -> p a t", a=na), r8(w), lw))

    ptrpp = PP(ptrb)
    v_tr = None
    st_prev = None
    for j, (w, o) in enumerate(zip(wins, outs)):
        tall = w.tensor.shape[0] == 512
        na = 4 if tall else 1
        ncol = 4096 // na
        base = j * 4096
        for a in range(na):
            v_q = p.do("vector", lambda e, a=a, base=base, ncol=ncol, j=j:
                       e.tensor_scalar(
                swi[:, 0:ncol], sw[:, base + a * ncol: base + (a + 1) * ncol],
                sscol[:, j:j + 1], None, ALU.mult),
                deps=[(lw, lw.n), v_sc, v_tr])
            v_q = p.do("vector", lambda e, ncol=ncol: e.tensor_scalar(
                swq[:, 0:ncol], swi[:, 0:ncol], 1.0, -1.0, ALU.min, ALU.max),
                deps=[v_q])
            for cc in range(ncol // 128):
                i, pt, freeat = ptrpp.acq()
                t_tr = p.do("tensor", lambda e, cc=cc, pt=pt: e.transpose(
                    pt[:], swq[:, cc * 128:(cc + 1) * 128], sidentb[:]),
                    deps=[v_q, freeat, (lg, lg.n)])
                v_tr = p.do("vector", lambda e, cc=cc, a=a, na=na, pt=pt:
                            e.tensor_copy(
                    swt[:, (cc * na + a) * 128:(cc * na + a) * 128 + 128],
                    pt[:]), deps=[t_tr, st_prev if (a == 0 and cc == 0) else None])
                ptrpp.rel(i, v_tr)
        ntc = ncol // 128
        st_prev = p.dma("sync", r8(o), swt[:, 0:ntc * na * 128].rearrange(
            "p (a t) -> p a t", a=ntc), st, deps=[v_tr])
    p.wait("sync", (st, st.n))
    return p.finish()



def build_ffn():
    p = Prog("ffn")
    xF = p.din("xF", [D, TPC])
    xhl = p.din("xhl", [D, 1])
    xhr = p.din("xhr", [D, 1])
    accIn = p.din("accIn", [D, TPC])
    w1qT = p.din("w1qT", [D, DH], BF16)
    w2qTt = p.din("w2qTt", [8, 32, 128, 128], BF16)
    invw1 = p.din("invw1", [1, 1])
    invw2 = p.din("invw2", [1, 1])
    cwc = p.din("cwc", [DH, 3])
    cbc = p.din("cbc", [128, 32])
    onesr = p.din("onesr", [1, 128])
    onescl = p.din("onescl", [128, 1])
    identc = p.din("identc", [128, 128])

    o_x = p.dout("xout", [D, TPC])

    NBL = 8
    W = 258

    sw1 = p.sb("sw1", [128, 8 * DH], BF16)
    sw2b = [p.sb(f"sw2b{i}", [128, 32 * 128], BF16) for i in range(2)]
    sdiag = p.sb("sdiag", [128, 96 * 128], BF16)
    scw = p.sb("scw", [128, 96])
    scb = p.sb("scb", [128, 32])
    sidentb = p.sb("sidentb", [128, 128], BF16)
    sident = p.sb("sident", [128, 128])
    sonesf = p.sb("sonesf", [1, 128])
    sonescol = p.sb("sonescol", [128, 1], BF16)
    sinv1 = p.sb("sinv1", [1, 1])
    sinv2 = p.sb("sinv2", [1, 1])
    sxfb = p.sb("sxfb", [128, 8 * W])
    sacb = p.sb("sacb", [128, 8 * 256])
    sxbb = p.sb("sxbb", [128, 8 * W], BF16)
    sxsqb = p.sb("sxsqb", [128, 8 * W], BF16)
    sxnb = p.sb("sxnb", [128, 8 * W])
    sxq = p.sb("sxq", [128, 8 * W], BF16)
    swiq = p.sb("swiq", [128, W], I32)
    shh = p.sb("shh", [128, 2 * W], BF16)
    sy = p.sb("sy", [128, 32 * 256], BF16)
    sxq2 = p.sb("sxq2", [128, 32 * 256], BF16)
    swiq2 = p.sb("swiq2", [128, 256], I32)
    sam = p.sb("sam", [128, W])
    sfold = p.sb("sfold", [128, W])
    samrow = p.sb("samrow", [1, 4 * 512])
    srows = p.sb("srows", [1, 4 * 512])
    sB = p.sb("sB", [128, W])
    dB = p.sb("dB", [128, W])
    s2B = p.sb("s2B", [128, 256])
    d2B = p.sb("d2B", [128, 256])
    sxo = p.sb("sxo", [128, 8 * 256])

    pm = [p.ps("pm0", [128, 512]), p.ps("pm1", [128, 512])]
    pe_ = [p.ps("pe0", [128, 512]), p.ps("pe1", [128, 512])]
    pbc1 = p.ps("pbc1", [128, 512])
    pbc2 = p.ps("pbc2", [128, 512])
    prow = p.ps("prow", [1, 512])
    prow2 = p.ps("prow2", [128, 512])

    ld = p.ctr_new("ld")
    lg = p.ctr_new("lg")
    lf = p.ctr_new("lf")
    la = p.ctr_new("la")
    lw2 = [p.ctr_new(f"lw2{i}") for i in range(2)]
    st = p.ctr_new("st")

    p.dma("sync", sonesf[:], onesr, ld)
    p.dma("sync", scw[:].rearrange("p (a t) -> p a t", a=32),
          r8(cwc), ld)
    p.dma("sync", scb[:], cbc, ld)
    p.dma("sync", sident[:], identc, ld)
    p.dma("sync", sinv1[:], invw1, ld)
    p.dma("sync", sinv2[:], invw2, ld)
    p.dma("gpsimd", s8(sw1, 8), r8(w1qT), lg)
    p.dma("gpsimd", sidentb[:], identc, lg)
    p.dma("gpsimd", sonescol[:], onescl, lg)
    lds = [(ld, ld.n), (lg, lg.n)]

    v_diag = None
    for mt in range(32):
        for k in range(3):
            v_diag = p.do("vector", lambda e, mt=mt, k=k: e.tensor_scalar(
                sdiag[:, (mt * 3 + k) * 128:(mt * 3 + k) * 128 + 128],
                sidentb[:], scw[:, mt * 3 + k: mt * 3 + k + 1], None,
                ALU.mult), deps=lds if (mt == 0 and k == 0) else ())

    v_apply = None
    s_sq_prev = None
    s_silu_last = None
    v_xo_last = None
    st_prev = None
    t_row_cons = None
    v_row_cons = None
    t_conv_prev = [None, None]
    s_silu_prev = [None, None]
    v_pe_cons = [None, None]
    w2_tok = [None, None]
    t_w2_cons = [None, None]

    for nb in range(NBL):
        t0 = nb * 256
        fdeps = [v_apply, s_sq_prev]
        if nb == 0:
            lf_t = p.dma("sync", s8(sxfb, 8)[:, :, 1:258],
                         r8(xF)[:, :, 0:257], lf, deps=fdeps)
            lf_t = p.dma("sync", s8(sxfb, 8)[:, :, 0:1], r8(xhl), lf, slow=True)
        elif nb == NBL - 1:
            lf_t = p.dma("sync", s8(sxfb, 8)[:, :, 0:257],
                         r8(xF)[:, :, t0 - 1:TPC], lf, deps=fdeps)
            lf_t = p.dma("sync", s8(sxfb, 8)[:, :, 257:258], r8(xhr), lf, slow=True)
        else:
            lf_t = p.dma("sync", s8(sxfb, 8), r8(xF)[:, :, t0 - 1:t0 + 257],
                         lf, deps=fdeps)
        la_t = p.dma("sync", s8(sacb, 8), r8(accIn)[:, :, t0:t0 + 256], la,
                     deps=[v_xo_last])

        v_cast = None
        s_sq = None
        for kt in range(8):
            v_cast = p.do("vector", lambda e, kt=kt: e.tensor_copy(
                sxbb[:, kt * W:(kt + 1) * W], sxfb[:, kt * W:(kt + 1) * W]),
                deps=([(lf, lf.n), t_row_cons] if kt == 0 else ()))
        for kt in range(8):
            s_sq = p.do("scalar", lambda e, kt=kt: e.activation(
                sxsqb[:, kt * W:(kt + 1) * W], sxfb[:, kt * W:(kt + 1) * W],
                AF.Square), deps=([(lf, lf.n), t_row_cons] if kt == 0 else ()))
        t_sum = mm_group(p, prow[0:1, 0:W],
                         [(sonescol[:], sxbb[:, kt * W:(kt + 1) * W])
                          for kt in range(8)], deps=[v_cast, s_sq, v_row_cons])
        v_mean = p.do("vector", lambda e: e.tensor_scalar(
            srows[:, 0:W], prow[0:1, 0:W], 1.0 / D, None, ALU.mult),
            deps=[t_sum])
        t_ssq = mm_group(p, prow[0:1, 0:W],
                         [(sonescol[:], sxsqb[:, kt * W:(kt + 1) * W])
                          for kt in range(8)], deps=[v_mean])
        v_msq = p.do("vector", lambda e: e.tensor_tensor(
            srows[:, 512:512 + W], srows[:, 0:W], srows[:, 0:W], ALU.mult),
            deps=[t_ssq])
        v_var = p.do("vector", lambda e: e.scalar_tensor_tensor(
            srows[:, 1024:1024 + W], prow[0:1, 0:W], 1.0 / D,
            srows[:, 512:512 + W], ALU.mult, ALU.subtract), deps=[v_msq])
        v_eps = p.do("vector", lambda e: e.tensor_scalar(
            srows[:, 1024:1024 + W], srows[:, 1024:1024 + W], 1e-5, None,
            ALU.add), deps=[v_var])
        s_std = p.do("scalar", lambda e: e.activation(
            srows[:, 1536:1536 + W], srows[:, 1024:1024 + W], AF.Sqrt),
            deps=[v_eps])
        v_rst = p.do("vector", lambda e: e.reciprocal(
            srows[:, 512:512 + W], srows[:, 1536:1536 + W]), deps=[s_std])
        t_row_cons = t_ssq
        s_sq_prev = s_sq
        v_row_cons = v_var

        t_bc1 = p.do("tensor", lambda e: e.matmul(
            pbc1[:, 0:W], sonesf[:], srows[:, 0:W], start=True, stop=True),
            deps=[v_rst])
        t_bc2 = p.do("tensor", lambda e: e.matmul(
            pbc2[:, 0:W], sonesf[:], srows[:, 512:512 + W], start=True,
            stop=True))
        v_apply = None
        for kt in range(8):
            v_s = p.do("vector", lambda e, kt=kt: e.tensor_tensor(
                sxnb[:, kt * W:(kt + 1) * W], sxfb[:, kt * W:(kt + 1) * W],
                pbc1[:, 0:W], ALU.subtract),
                deps=[t_bc2] if kt == 0 else ())
            v_apply = p.do("vector", lambda e, kt=kt: e.tensor_tensor(
                sxnb[:, kt * W:(kt + 1) * W], sxnb[:, kt * W:(kt + 1) * W],
                pbc2[:, 0:W], ALU.mult), deps=[v_s])

        v_am = p.do("vector", lambda e: e.scalar_tensor_tensor(
            sam[:], sxnb[:, 0:W], -1.0, sxnb[:, 0:W], ALU.mult, ALU.max),
            deps=[v_apply])
        for kt in range(1, 8):
            v_am = p.do("vector", lambda e, kt=kt: e.scalar_tensor_tensor(
                sfold[:, 0:W], sxnb[:, kt * W:(kt + 1) * W], -1.0,
                sxnb[:, kt * W:(kt + 1) * W], ALU.mult, ALU.max),
                deps=[v_am])
            v_am = p.do("vector", lambda e: e.tensor_tensor(
                sam[:], sam[:], sfold[:, 0:W], ALU.max), deps=[v_am])
        v_am = p.do("vector", lambda e: e.tensor_copy(
            sfold[0:64, 0:W], sam[64:128, :]), deps=[v_am])
        v_am = p.do("vector", lambda e: e.tensor_tensor(
            sam[0:64, :], sam[0:64, :], sfold[0:64, 0:W], ALU.max),
            deps=[v_am])
        v_am = p.do("vector", lambda e: e.tensor_copy(
            sfold[0:32, 0:W], sam[32:64, :]), deps=[v_am])
        v_am = p.do("vector", lambda e: e.tensor_tensor(
            sam[0:32, :], sam[0:32, :], sfold[0:32, 0:W], ALU.max),
            deps=[v_am])
        v_row = None
        for cc, wid in ((0, 128), (1, 128), (2, 2)):
            t_t1 = p.do("tensor", lambda e, cc=cc, wid=wid: e.transpose(
                prow2[0:wid, 0:32], sam[0:32, cc * 128:cc * 128 + wid],
                sident[0:32, 0:32]), deps=[v_am, v_row])
            v_c1 = p.do("vector", lambda e, cc=cc, wid=wid: e.tensor_reduce(
                sam[0:wid, 0:1], prow2[0:wid, 0:32], AX.X, ALU.max),
                deps=[t_t1])
            t_t2 = p.do("tensor", lambda e, cc=cc, wid=wid: e.transpose(
                prow2[0:1, 0:wid], sam[0:wid, 0:1], sident[0:wid, 0:wid]),
                deps=[v_c1])
            v_row = p.do("vector", lambda e, cc=cc, wid=wid: e.tensor_copy(
                samrow[0:1, cc * 128:cc * 128 + wid], prow2[0:1, 0:wid]),
                deps=[t_t2])
        v_r1 = p.do("vector", lambda e: e.tensor_scalar(
            samrow[0:1, 512:512 + W], samrow[0:1, 0:W], 1e-5, None, ALU.max),
            deps=[v_row])
        v_r2 = p.do("vector", lambda e: e.reciprocal(
            samrow[0:1, 1024:1024 + W], samrow[0:1, 512:512 + W]),
            deps=[v_r1])
        v_r3 = p.do("vector", lambda e: e.tensor_scalar(
            samrow[0:1, 1024:1024 + W], samrow[0:1, 1024:1024 + W], 127.0,
            None, ALU.mult), deps=[v_r2])
        v_r4 = p.do("vector", lambda e: e.tensor_scalar(
            samrow[0:1, 1536:1536 + W], samrow[0:1, 512:512 + W],
            sinv1[0:1, 0:1], 1.0 / 127.0, ALU.mult, ALU.mult), deps=[v_r3])
        t_b1 = p.do("tensor", lambda e: e.matmul(
            pbc1[:, 0:W], sonesf[:], samrow[0:1, 1024:1024 + W], start=True,
            stop=True), deps=[v_r4, v_apply])
        t_b2 = p.do("tensor", lambda e: e.matmul(
            pbc2[:, 0:W], sonesf[:], samrow[0:1, 1536:1536 + W], start=True,
            stop=True))
        v_sB = p.do("vector", lambda e: e.tensor_copy(sB[:], pbc1[:, 0:W]),
                    deps=[t_b2])
        v_dB = p.do("vector", lambda e: e.tensor_copy(dB[:], pbc2[:, 0:W]),
                    deps=[v_sB])

        v_xq = None
        for kt in range(8):
            v_i = p.do("vector", lambda e, kt=kt: e.tensor_tensor(
                swiq[:], sxnb[:, kt * W:(kt + 1) * W], sB[:], ALU.mult),
                deps=[v_dB] if kt == 0 else [v_xq])
            v_xq = p.do("vector", lambda e, kt=kt: e.tensor_tensor(
                sxq[:, kt * W:(kt + 1) * W], swiq[:], dB[:], ALU.mult),
                deps=[v_i])

        for mt in range(32):
            par = mt % 2
            t_mm1 = mm_group(
                p, pm[par][:, 0:W],
                [(sw1[:, kt * DH + mt * 128: kt * DH + mt * 128 + 128],
                  sxq[:, kt * W:(kt + 1) * W]) for kt in range(8)],
                deps=[v_xq, s_silu_prev[par]])
            v_h = p.do("vector", lambda e, par=par: e.tensor_copy(
                shh[:, par * W:par * W + W], pm[par][:, 0:W]),
                deps=[t_mm1, t_conv_prev[par]])
            t_conv = mm_group(
                p, pm[par][:, 0:256],
                [(sdiag[:, (mt * 3 + k) * 128:(mt * 3 + k) * 128 + 128],
                  shh[:, par * W + k: par * W + k + 256]) for k in range(3)],
                deps=[v_h, v_diag])
            t_conv_prev[par] = t_conv
            s_silu = p.do("scalar", lambda e, mt=mt, par=par: e.activation(
                sy[:, mt * 256:(mt + 1) * 256], pm[par][:, 0:256], AF.Silu,
                bias=scb[:, mt:mt + 1]), deps=[t_conv])
            s_silu_prev[par] = s_silu
            s_silu_last = s_silu

        v_am2 = p.do("vector", lambda e: e.scalar_tensor_tensor(
            sam[:, 0:256], sy[:, 0:256], -1.0, sy[:, 0:256], ALU.mult,
            ALU.max), deps=[s_silu_last])
        for mt in range(1, 32):
            v_am2 = p.do("vector", lambda e, mt=mt: e.scalar_tensor_tensor(
                sfold[:, 0:256], sy[:, mt * 256:(mt + 1) * 256], -1.0,
                sy[:, mt * 256:(mt + 1) * 256], ALU.mult, ALU.max),
                deps=[v_am2])
            v_am2 = p.do("vector", lambda e: e.tensor_tensor(
                sam[:, 0:256], sam[:, 0:256], sfold[:, 0:256], ALU.max),
                deps=[v_am2])
        v_am2 = p.do("vector", lambda e: e.tensor_copy(
            sfold[0:64, 0:256], sam[64:128, 0:256]), deps=[v_am2])
        v_am2 = p.do("vector", lambda e: e.tensor_tensor(
            sam[0:64, 0:256], sam[0:64, 0:256], sfold[0:64, 0:256], ALU.max),
            deps=[v_am2])
        v_am2 = p.do("vector", lambda e: e.tensor_copy(
            sfold[0:32, 0:256], sam[32:64, 0:256]), deps=[v_am2])
        v_am2 = p.do("vector", lambda e: e.tensor_tensor(
            sam[0:32, 0:256], sam[0:32, 0:256], sfold[0:32, 0:256], ALU.max),
            deps=[v_am2])
        v_row2 = None
        for cc in range(2):
            t_t1 = p.do("tensor", lambda e, cc=cc: e.transpose(
                prow2[0:128, 0:32], sam[0:32, cc * 128:(cc + 1) * 128],
                sident[0:32, 0:32]), deps=[v_am2, v_row2])
            v_c1 = p.do("vector", lambda e: e.tensor_reduce(
                sam[0:128, 0:1], prow2[0:128, 0:32], AX.X, ALU.max),
                deps=[t_t1])
            t_t2 = p.do("tensor", lambda e: e.transpose(
                prow2[0:1, 0:128], sam[0:128, 0:1], sident[:]), deps=[v_c1])
            v_row2 = p.do("vector", lambda e, cc=cc: e.tensor_copy(
                samrow[0:1, cc * 128:(cc + 1) * 128], prow2[0:1, 0:128]),
                deps=[t_t2])
        v_r1 = p.do("vector", lambda e: e.tensor_scalar(
            samrow[0:1, 512:512 + 256], samrow[0:1, 0:256], 1e-5, None,
            ALU.max), deps=[v_row2])
        v_r2 = p.do("vector", lambda e: e.reciprocal(
            samrow[0:1, 1024:1024 + 256], samrow[0:1, 512:512 + 256]),
            deps=[v_r1])
        v_r3 = p.do("vector", lambda e: e.tensor_scalar(
            samrow[0:1, 1024:1024 + 256], samrow[0:1, 1024:1024 + 256],
            127.0, None, ALU.mult), deps=[v_r2])
        v_r4 = p.do("vector", lambda e: e.tensor_scalar(
            samrow[0:1, 1536:1536 + 256], samrow[0:1, 512:512 + 256],
            sinv2[0:1, 0:1], 1.0 / 127.0, ALU.mult, ALU.mult), deps=[v_r3])
        t_b1 = p.do("tensor", lambda e: e.matmul(
            pbc1[:, 0:256], sonesf[:], samrow[0:1, 1024:1024 + 256],
            start=True, stop=True), deps=[v_r4, v_xq])
        t_b2 = p.do("tensor", lambda e: e.matmul(
            pbc2[:, 0:256], sonesf[:], samrow[0:1, 1536:1536 + 256],
            start=True, stop=True))
        v_s2B = p.do("vector", lambda e: e.tensor_copy(s2B[:], pbc1[:, 0:256]),
                     deps=[t_b2])
        v_d2B = p.do("vector", lambda e: e.tensor_copy(d2B[:], pbc2[:, 0:256]),
                     deps=[v_s2B])

        v_xq2 = None
        for mt in range(32):
            v_i = p.do("vector", lambda e, mt=mt: e.tensor_tensor(
                swiq2[:], sy[:, mt * 256:(mt + 1) * 256], s2B[:], ALU.mult),
                deps=[v_d2B] if mt == 0 else [v_xq2])
            v_xq2 = p.do("vector", lambda e, mt=mt: e.tensor_copy(
                sxq2[:, mt * 256:(mt + 1) * 256], swiq2[:]), deps=[v_i])

        for et in range(2):
            jj = et % 2
            w2_tok[jj] = p.dma("gpsimd", s8(sw2b[jj], 32),
                               w2qTt[et].rearrange("a p t -> p a t"),
                               lw2[jj], deps=[t_w2_cons[jj]])
        for et in range(8):
            jj = et % 2
            par = et % 2
            t_mm2 = mm_group(
                p, pe_[par][:, 0:256],
                [(sw2b[jj][:, kt * 128:(kt + 1) * 128],
                  sxq2[:, kt * 256:(kt + 1) * 256]) for kt in range(32)],
                deps=[v_xq2, w2_tok[jj], v_pe_cons[par]])
            t_w2_cons[jj] = t_mm2
            if et + 2 < 8:
                w2_tok[jj] = p.dma("gpsimd", s8(sw2b[jj], 32),
                                   w2qTt[et + 2].rearrange("a p t -> p a t"),
                                   lw2[jj], deps=[t_mm2])
            v_o1 = p.do("vector", lambda e, et=et, par=par: e.tensor_tensor(
                sxo[:, et * 256:(et + 1) * 256], pe_[par][:, 0:256], d2B[:],
                ALU.mult), deps=[t_mm2, (la, la.n), st_prev])
            v_xo_last = p.do("vector", lambda e, et=et: e.tensor_tensor(
                sxo[:, et * 256:(et + 1) * 256],
                sxo[:, et * 256:(et + 1) * 256],
                sacb[:, et * 256:(et + 1) * 256], ALU.add), deps=[v_o1])
            v_pe_cons[par] = v_xo_last

        st_prev = p.dma("sync", r8(o_x)[:, :, t0:t0 + 256], s8(sxo, 8), st,
                        deps=[v_xo_last])

    p.wait("sync", (st, st.n))
    return p.finish()



L = S
NCH = L // CH
NTB = 16


def build_l1head(debug=False):
    p = Prog("l1head")
    nc = p.nc
    xn1 = p.din("xn1", [D, L], BF16)
    Wq = p.din("Wq", [D, DK])
    Wk = p.din("Wk", [D, DK])
    Wv = p.din("Wv", [D, DV])
    Wb = p.din("Wb", [D, 1])
    cq = p.din("cq", [DK, 4])
    ck = p.din("ck", [DK, 4])
    cv = p.din("cv", [DV, 4])
    firs = p.din("firs", [DV, 3])
    firl = p.din("firl", [DV, 31])
    negI = p.din("negI", [32, 32])
    mslneg = p.din("mslneg", [32, 512])
    mtriu = p.din("mtriu", [32, 512])
    onesr = p.din("onesr", [1, 128])
    onescl = p.din("onescl", [128, 1])
    identc = p.din("identc", [128, 128])

    o_v = p.dout("vF", [DV, L], BF16)
    o_fs = p.dout("fsF", [DV, L], BF16)
    o_fl = p.dout("flF", [DV, L], BF16)
    o_dF = p.dout("dF", [DV, L], BF16)
    o_stats = p.dout("stats", [6, L])
    kind = "ExternalOutput" if debug else "Internal"
    uTs = nc.dram_tensor("uTs", [L, DV], BF16, kind=kind).ap()
    kTs = nc.dram_tensor("kTs", [L, DK], BF16, kind=kind).ap()
    vTs = nc.dram_tensor("vTs", [L, DV], BF16, kind=kind).ap()
    wTs = nc.dram_tensor("wTs", [NCH // 4, 128, 256], BF16, kind=kind).ap()
    if debug:
        o_TT = p.dout("dbg_TT", [32, NCH * 32], BF16)
        o_ATd = p.dout("dbg_AT", [32, NCH * 32], BF16)
        o_qs = p.dout("dbg_qs", [DK, L], BF16)
        o_ks = p.dout("dbg_ks", [DK, L], BF16)

    PAD = 32
    LP = L + 2 * PAD
    SEG = 1024
    NSEG = L // SEG

    sq = p.sb("sq", [128, 2 * L], BF16)
    sk = p.sb("sk", [128, 2 * L], BF16)
    spre = p.sb("spre", [128, 2 * LP], BF16)
    sWbc = p.sb("sWbc", [128, 8], BF16)
    scc = p.sb("scc", [128, 2 * 4])
    sfir = p.sb("sfir", [128, 2 * 34])
    snegI = p.sb("snegI", [32, 32], BF16)
    smsl = p.sb("smsl", [32, 512], BF16)
    smtr = p.sb("smtr", [32, 512], BF16)
    sonesf = p.sb("sonesf", [1, 128])
    sonesrb = p.sb("sonesrb", [1, 128], BF16)
    sonescol = p.sb("sonescol", [128, 1], BF16)
    sident = p.sb("sident", [128, 128])
    sidentb = p.sb("sidentb", [128, 128], BF16)
    sbeta = p.sb("sbeta", [1, L], BF16)
    srseg = p.sb("srseg", [1, 2 * SEG])
    sstt = p.sb("sstt", [1, 2 * SEG])
    sbc = p.sb("sbc", [128, 512], BF16)
    ssq2 = p.sb("ssq2", [128, 2 * 512], BF16)
    schk = p.sb("schk", [32, 256], BF16)
    schk2 = p.sb("schk2", [32, 256], BF16)
    subat = p.sb("subat", [32, 4 * 256], BF16)
    skbat = p.sb("skbat", [32, 4 * 256], BF16)
    sSb = [p.sb(f"sSb{i}", [128, 256], BF16) for i in range(2)]
    su2 = p.sb("su2", [32, 256], BF16)
    sDel = p.sb("sDel", [32, 256], BF16)

    pA = p.ps("pA", [128, 512])
    pB = p.ps("pB", [128, 512])
    pM = [p.ps("pM0", [32, 512]), p.ps("pM1", [32, 512])]
    ptb = [p.ps("ptb0", [128, 256], BF16), p.ps("ptb1", [128, 256], BF16)]
    pS = [p.ps("pS0", [128, 512]), p.ps("pS1", [128, 512])]

    ld = p.ctr_new("ld")
    lg = p.ctr_new("lg")
    lx = [p.ctr_new(f"lx{i}") for i in range(2)]
    lo = p.ctr_new("lo")
    lu = [p.ctr_new(f"lu{i}") for i in range(2)]
    lk2 = [p.ctr_new(f"lk2{i}") for i in range(2)]
    lv2 = [p.ctr_new(f"lv2{i}") for i in range(2)]
    lqu = p.ctr_new("lqu")
    lqk = p.ctr_new("lqk")
    lqw = p.ctr_new("lqw")
    lqv = p.ctr_new("lqv")
    lsh = p.ctr_new("lsh")

    p.dma("sync", sonesf[:], onesr, ld)
    p.dma("sync", sident[:], identc, ld)

    p.dma("sync", sfir[:, 0:6].rearrange("p (a t) -> p a t", a=2), r8(firs), ld)
    p.dma("sync", sfir[:, 6:68].rearrange("p (a t) -> p a t", a=2), r8(firl), ld)
    p.dma("gpsimd", smsl[:], mslneg, lg)
    p.dma("gpsimd", smtr[:], mtriu, lg)
    p.dma("gpsimd", sidentb[:], identc, lg)
    p.dma("gpsimd", sonesrb[:], onesr, lg)
    p.dma("gpsimd", sonescol[:], onescl, lg)
    p.dma("gpsimd", snegI[:], negI, lg)
    p.dma("gpsimd", sWbc[:].rearrange("p (a t) -> p a t", a=8), r8(Wb), lg, slow=True)
    lds = [(ld, ld.n), (lg, lg.n)]

    tpp = PP(ptb)

    scope1 = contextlib.ExitStack()
    sv = scope1.enter_context(nc.sbuf_tensor("sv", [128, 2 * LP], BF16))
    sxnb = [scope1.enter_context(nc.sbuf_tensor(f"sxnb{i}", [128, 8 * 512],
                                                BF16)) for i in range(2)]
    sWp = scope1.enter_context(nc.sbuf_tensor("sWp", [128, 8 * 2 * 128],
                                              BF16))

    v_pad = None
    for t_, w_ in ((sv, LP), (spre, LP)):
        for mt in range(2):
            v_pad = p.do("vector", lambda e, t_=t_, w_=w_, mt=mt: e.memset(
                t_[:, mt * w_: mt * w_ + PAD], 0.0), deps=[v_pad])
            v_pad = p.do("vector", lambda e, t_=t_, w_=w_, mt=mt: e.memset(
                t_[:, (mt + 1) * w_ - PAD: (mt + 1) * w_], 0.0), deps=[v_pad])

    ppA = PP([pA, pB])
    xguard = [None, None]
    beta_done = None
    prev_mm_last = None
    prev_silu = None
    for ti, (Wt, ct, dst, off, width) in enumerate((
            (Wq, cq, sq, 0, L), (Wk, ck, sk, 0, L), (Wv, cv, sv, PAD, LP))):
        wtok = p.dma("gpsimd", sWp[:].rearrange("p (a t) -> p a t", a=8),
                     r8(Wt), lg, deps=[prev_mm_last])
        ctok = p.dma("sync", scc[:].rearrange("p (a t) -> p a t", a=2),
                     r8(ct), ld, deps=[prev_silu])
        v_evl = None
        for nb in range(NTB):
            c0 = nb * 512
            jj = nb % 2
            xtok = p.dma("sync", s8(sxnb[jj], 8), r8(xn1)[:, :, c0:c0 + 512],
                         lx[jj], deps=[xguard[jj]])
            for mt in range(2):
                i2, pkt, freeat = ppA.acq()
                t_mm = mm_group(
                    p, pkt[:],
                    [(sWp[:, kt * 256 + mt * 128: kt * 256 + mt * 128 + 128],
                      sxnb[jj][:, kt * 512:(kt + 1) * 512])
                     for kt in range(8)],
                    deps=[wtok, xtok, freeat] + (lds if (ti == 0 and nb == 0
                                                         and mt == 0) else []))
                v_evl = p.do("vector", lambda e, mt=mt, c0=c0, pkt=pkt:
                             e.tensor_copy(
                    spre[:, mt * LP + PAD + c0: mt * LP + PAD + c0 + 512],
                    pkt[:]), deps=[t_mm] + ([v_pad] if (ti == 0 and nb == 0)
                                            else []))
                ppA.rel(i2, v_evl)
                prev_mm_last = t_mm
            xguard[jj] = prev_mm_last
            if ti == 0:
                t_b = mm_group(p, pM[0][0:1, 0:512],
                               [(sWbc[:, kt:kt + 1],
                                 sxnb[jj][:, kt * 512:(kt + 1) * 512])
                                for kt in range(8)],
                               deps=[(lg, lg.n), beta_done])
                beta_done = p.do("vector", lambda e, c0=c0: e.tensor_copy(
                    sbeta[:, c0:c0 + 512], pM[0][0:1, 0:512]), deps=[t_b])
        s_silu = None
        for mt in range(2):
            po = mt * LP + PAD
            do = mt * width + off
            v_cv = p.do("vector", lambda e, po=po, do=do, dst=dst, mt=mt:
                        e.tensor_scalar(
                dst[:, do: do + L], spre[:, po - 3: po - 3 + L],
                scc[:, mt * 4: mt * 4 + 1], None, ALU.mult),
                deps=[v_evl, ctok, s_silu])
            for k in range(1, 4):
                v_cv = p.do("vector", lambda e, po=po, do=do, k=k, dst=dst,
                            mt=mt: e.scalar_tensor_tensor(
                    dst[:, do: do + L], spre[:, po - 3 + k: po - 3 + k + L],
                    scc[:, mt * 4 + k: mt * 4 + k + 1],
                    dst[:, do: do + L], ALU.mult, ALU.add), deps=[v_cv])
            s_silu = p.do("scalar", lambda e, do=do, dst=dst: e.activation(
                dst[:, do: do + L], dst[:, do: do + L], AF.Silu),
                deps=[v_cv])
        prev_silu = s_silu

    s_qkv = prev_silu
    s_beta = p.do("scalar", lambda e: e.activation(
        sbeta[:], sbeta[:], AF.Sigmoid), deps=[beta_done])

    v_scale = None
    for which, src in ((0, sq), (1, sk)):
        for sg in range(NSEG):
            g0 = sg * SEG
            last = None
            for nb in range(SEG // 512):
                c0 = g0 + nb * 512
                v_sq = None
                for mt in range(2):
                    v_sq = p.do("vector", lambda e, mt=mt, c0=c0, src=src:
                                e.tensor_tensor(
                        ssq2[:, mt * 512:(mt + 1) * 512],
                        src[:, mt * L + c0: mt * L + c0 + 512],
                        src[:, mt * L + c0: mt * L + c0 + 512], ALU.mult),
                        deps=[s_qkv if (which == 0 and sg == 0 and nb == 0)
                              else None, last, v_scale] if mt == 0 else ())
                t_r = mm_group(p, pM[1][0:1, 0:512],
                               [(sonescol[:], ssq2[:, mt * 512:(mt + 1) * 512])
                                for mt in range(2)], deps=[v_sq, last])
                last = p.do("vector", lambda e, nb=nb: e.tensor_copy(
                    srseg[:, nb * 512: nb * 512 + 512], pM[1][0:1, 0:512]),
                    deps=[t_r])
            v_e = p.do("vector", lambda e: e.tensor_scalar(
                srseg[:, 0:SEG], srseg[:, 0:SEG], 1e-6, None, ALU.add),
                deps=[last])
            s_s = p.do("scalar", lambda e: e.activation(
                srseg[:, SEG:2 * SEG], srseg[:, 0:SEG], AF.Sqrt), deps=[v_e])
            v_r = p.do("vector", lambda e: e.reciprocal(
                srseg[:, 0:SEG], srseg[:, SEG:2 * SEG]), deps=[s_s])
            for nb in range(SEG // 512):
                c0 = g0 + nb * 512
                t_bc = p.do("tensor", lambda e, nb=nb: e.matmul(
                    pA[:], sonesf[:], srseg[:, nb * 512: nb * 512 + 512],
                    start=True, stop=True), deps=[v_r, v_scale])
                v_bc = p.do("vector", lambda e: e.tensor_copy(
                    sbc[:, 0:512], pA[:]), deps=[t_bc])
                for mt in range(2):
                    v_scale = p.do("vector", lambda e, mt=mt, c0=c0, src=src:
                                   e.tensor_tensor(
                        src[:, mt * L + c0: mt * L + c0 + 512],
                        src[:, mt * L + c0: mt * L + c0 + 512],
                        sbc[:, 0:512], ALU.mult), deps=[v_bc])
    v_qkscaled = v_scale

    v_store = p.dma("sync", r8(o_v),
                    sv[:].rearrange("p (m t) -> p m t", m=2)[:, :, PAD:PAD + L],
                    lo, deps=[s_qkv], slow=True)
    stat_toks = []
    v_fir_prev = v_qkscaled
    for oo, (o_t, foff, ntap) in enumerate(((o_fs, 0, 3), (o_fl, 6, 31))):
        v_f = None
        for mt in range(2):
            po = mt * LP + PAD
            v_f = p.do("vector", lambda e, po=po, foff=foff, mt=mt,
                       ntap=ntap: e.tensor_scalar(
                spre[:, po: po + L],
                sv[:, po - ntap + 1: po - ntap + 1 + L],
                sfir[:, foff + mt * ntap: foff + mt * ntap + 1], None,
                ALU.mult), deps=[v_fir_prev] if mt == 0 else [v_f])
            for k in range(1, ntap):
                v_f = p.do("vector", lambda e, po=po, foff=foff, k=k,
                           mt=mt, ntap=ntap: e.scalar_tensor_tensor(
                    spre[:, po: po + L],
                    sv[:, po - ntap + 1 + k: po - ntap + 1 + k + L],
                    sfir[:, foff + mt * ntap + k: foff + mt * ntap + k + 1],
                    spre[:, po: po + L], ALU.mult, ALU.add), deps=[v_f])
        st_f = p.dma("sync", r8(o_t),
                     spre[:].rearrange("p (m t) -> p m t", m=2)
                     [:, :, PAD:PAD + L], lo, deps=[v_f], slow=True)
        last_st = None
        for sg in range(NSEG):
            g0 = sg * SEG
            last = None
            for nb in range(SEG // 512):
                c0 = g0 + nb * 512
                v_sq = None
                for mt in range(2):
                    v_sq = p.do("vector", lambda e, mt=mt, c0=c0:
                                e.tensor_tensor(
                        ssq2[:, mt * 512:(mt + 1) * 512],
                        spre[:, mt * LP + PAD + c0: mt * LP + PAD + c0 + 512],
                        spre[:, mt * LP + PAD + c0: mt * LP + PAD + c0 + 512],
                        ALU.mult), deps=[v_f, last] if mt == 0 else ())
                t_r = mm_group(p, pM[1][0:1, 0:512],
                               [(sonescol[:], ssq2[:, mt * 512:(mt + 1) * 512])
                                for mt in range(2)], deps=[v_sq, last])
                last = p.do("vector", lambda e, nb=nb: e.tensor_copy(
                    srseg[:, nb * 512: nb * 512 + 512],
                    pM[1][0:1, 0:512]), deps=[t_r])
                t_r2 = mm_group(p, pM[1][0:1, 0:512],
                                [(sonescol[:],
                                  spre[:, mt * LP + PAD + c0: mt * LP + PAD + c0 + 512])
                                 for mt in range(2)], deps=[last])
                last = p.do("vector", lambda e, nb=nb: e.tensor_copy(
                    sstt[:, nb * 512: nb * 512 + 512], pM[1][0:1, 0:512]),
                    deps=[t_r2])
            v1 = p.do("vector", lambda e: e.tensor_scalar(
                sstt[:, 0:SEG], sstt[:, 0:SEG], 1.0 / DV, None, ALU.mult),
                deps=[last])
            v2 = p.do("vector", lambda e: e.tensor_tensor(
                sstt[:, SEG:2 * SEG], sstt[:, 0:SEG], sstt[:, 0:SEG],
                ALU.mult), deps=[v1])
            v3 = p.do("vector", lambda e: e.scalar_tensor_tensor(
                sstt[:, SEG:2 * SEG], srseg[:, 0:SEG],
                1.0 / DV, sstt[:, SEG:2 * SEG], ALU.mult, ALU.subtract),
                deps=[v2])
            v4 = p.do("vector", lambda e: e.tensor_scalar(
                sstt[:, SEG:2 * SEG], sstt[:, SEG:2 * SEG], 1e-30, None,
                ALU.max), deps=[v3])
            v5 = p.do("scalar", lambda e: e.activation(
                sstt[:, SEG:2 * SEG], sstt[:, SEG:2 * SEG], AF.Sqrt),
                deps=[v4])
            p.dma("sync", o_stats[2 * oo:2 * oo + 1, g0:g0 + SEG],
                  sstt[0:1, 0:SEG], lo, deps=[v5])
            last_st = p.dma("sync", o_stats[2 * oo + 1:2 * oo + 2, g0:g0 + SEG],
                            sstt[0:1, SEG:2 * SEG], lo, deps=[v5])
        stat_toks.append(last_st)
        v_fir_prev = (lo, lo.n)
    v_vb = None
    for sg in range(NSEG):
        g0 = sg * SEG
        for nb in range(SEG // 512):
            c0 = g0 + nb * 512
            t_bc = p.do("tensor", lambda e, c0=c0: e.matmul(
                pA[:], sonesrb[:], sbeta[:, c0:c0 + 512],
                start=True, stop=True),
                deps=[s_beta, v_vb,
                      (lo, lo.n) if sg == 0 and nb == 0 else None,
                      v_fir_prev if sg == 0 and nb == 0 else None])
            v_bc = p.do("vector", lambda e: e.tensor_copy(
                sbc[:, 0:512], pA[:]), deps=[t_bc])
            for mt in range(2):
                v_vb = p.do("vector", lambda e, mt=mt, c0=c0: e.tensor_tensor(
                    sv[:, mt * LP + PAD + c0: mt * LP + PAD + c0 + 512],
                    sv[:, mt * LP + PAD + c0: mt * LP + PAD + c0 + 512],
                    sbc[:, 0:512], ALU.mult), deps=[v_bc])

    lv_prev = None
    v_vt_ev = None
    for c in range(NCH):
        cb = c % 4
        for mt in range(2):
            i2, pt, freeat = tpp.acq()
            t_tr = p.do("tensor", lambda e, c=c, mt=mt, pt=pt: e.transpose(
                pt[0:32, 0:128],
                sv[:, mt * LP + PAD + c * 32: mt * LP + PAD + c * 32 + 32],
                sidentb[:]), deps=[v_vb if c == 0 and mt == 0 else None,
                                   freeat])
            v_vt_ev = p.do("vector", lambda e, mt=mt, cb=cb, pt=pt:
                           e.tensor_copy(
                subat[0:32, cb * 256 + mt * 128: cb * 256 + mt * 128 + 128],
                pt[0:32, 0:128]),
                deps=[t_tr, lv_prev if (cb == 0 and mt == 0) else None])
            tpp.rel(i2, v_vt_ev)
        if cb == 3:
            c4 = c // 4
            lv_prev = p.dma(
                "sync", vTs[c4 * 128:(c4 + 1) * 128, :].rearrange(
                    "(c i) d -> i c d", i=32), s8(subat, 4), lqv,
                deps=[v_vt_ev])
    v_vt_done = lv_prev

    scope1.close()
    scope_late = contextlib.ExitStack()
    sAT = scope_late.enter_context(nc.sbuf_tensor("sAT", [32, NCH * 32], BF16))
    swbat = scope_late.enter_context(nc.sbuf_tensor("swbat", [128, 256], BF16))
    swstr = [scope_late.enter_context(nc.sbuf_tensor(f"swstr{i}", [128, 256],
                                                     BF16)) for i in range(2)]
    scopeT = contextlib.ExitStack()
    sTT = scopeT.enter_context(nc.sbuf_tensor("sTT", [32, NCH * 32], BF16))

    v_kb = None
    for sg in range(NSEG):
        g0 = sg * SEG
        for nb in range(SEG // 512):
            c0 = g0 + nb * 512
            t_bc = p.do("tensor", lambda e, c0=c0: e.matmul(
                pA[:], sonesrb[:], sbeta[:, c0:c0 + 512],
                start=True, stop=True),
                deps=[s_beta, v_kb, (lo, lo.n) if sg == 0 and nb == 0 else None,
                      v_vt_done if sg == 0 and nb == 0 else None])
            v_bc = p.do("vector", lambda e: e.tensor_copy(
                sbc[:, 0:512], pA[:]), deps=[t_bc])
            for mt in range(2):
                v_kb = p.do("vector", lambda e, mt=mt, c0=c0: e.tensor_tensor(
                    spre[:, mt * LP + PAD + c0: mt * LP + PAD + c0 + 512],
                    sk[:, mt * L + c0: mt * L + c0 + 512], sbc[:, 0:512],
                    ALU.mult), deps=[v_bc])
    skb = spre

    scope2 = contextlib.ExitStack()
    sMev = scope2.enter_context(nc.sbuf_tensor("sMev", [32, NCH * 32], BF16))
    sTch = [scope2.enter_context(nc.sbuf_tensor(f"sTch{i}", [128, 1024], F32))
            for i in range(2)]
    sTchb = scope2.enter_context(nc.sbuf_tensor("sTchb", [128, 1024], BF16))

    v_mev = None
    for bat in range(NCH // 16):
        par = bat % 2
        t_mev_prev = None
        for ci in range(16):
            c = bat * 16 + ci
            t_mev_prev = mm_group(
                p, pM[par][0:32, ci * 32:(ci + 1) * 32],
                [(skb[:, mt * LP + PAD + c * 32: mt * LP + PAD + c * 32 + 32],
                  sk[:, mt * L + c * 32: mt * L + c * 32 + 32])
                 for mt in range(2)],
                deps=[v_kb, v_mev] if ci == 0 else ())
        v_mev = p.do("vector", lambda e, bat=bat, par=par: e.tensor_tensor(
            sMev[:, bat * 512:(bat + 1) * 512], pM[par][0:32, 0:512],
            smsl[:], ALU.mult), deps=[t_mev_prev])
    for i in range(32):
        for half in range(2):
            p.dma("gpsimd",
                  sTch[half][:, i * 32:(i + 1) * 32],
                  sMev[i:i + 1, :].rearrange("p (c j) -> p c j", j=32)
                  [:, half * 128:(half + 1) * 128, :],
                  lsh, deps=[v_mev] if (i == 0 and half == 0) else ())
    v_t = None
    for half in range(2):
        for i in range(1, 32):
            for jjj in range(i):
                v_t = p.do("vector", lambda e, half=half, i=i, jjj=jjj:
                           e.scalar_tensor_tensor(
                    sTch[half][:, i * 32:(i + 1) * 32],
                    sTch[half][:, jjj * 32:(jjj + 1) * 32],
                    sTch[half][:, i * 32 + jjj: i * 32 + jjj + 1],
                    sTch[half][:, i * 32:(i + 1) * 32], ALU.mult, ALU.add),
                    deps=[v_t] if v_t is not None else [(lsh, lsh.n)])
    for half in range(2):
        v_t = p.do("vector", lambda e, half=half: e.tensor_scalar(
            sTch[half][:, 0:1024:33], sTch[half][:, 0:1024:33], 1.0, None,
            ALU.add), deps=[v_t])
    v_tt = None
    v_cast = None
    for half in range(2):
        v_cast = p.do("vector", lambda e, half=half: e.tensor_copy(
            sTchb[:], sTch[half][:]), deps=[v_t, v_tt])
        for i in range(32):
            ii, pt, freeat = tpp.acq()
            t_tr = p.do("tensor", lambda e, i=i, pt=pt: e.transpose(
                pt[0:32, 0:128], sTchb[:, i * 32:(i + 1) * 32], sidentb[:]),
                deps=[v_cast, freeat])
            v_tt = p.do("vector", lambda e, i=i, half=half, pt=pt:
                        e.tensor_copy(
                sTT[0:32, half * 4096 + i: half * 4096 + i + 4065: 32],
                pt[0:32, 0:128]), deps=[t_tr])
            tpp.rel(ii, v_tt)

    scope2.close()
    scope3 = contextlib.ExitStack()
    svstr = [scope3.enter_context(nc.sbuf_tensor(f"svstr{i}", [32, 4 * 256],
                                                 BF16)) for i in range(2)]

    vstr_tok = [None, None]
    v_u_ev = None
    v_w_ev = None
    v_at_ev = None
    lq_u_prev = None
    lq_k_prev = None
    lq_w_prev = None
    for g in range(2):
        vstr_tok[g] = p.dma("sync", s8(svstr[g], 4),
                            vTs[g * 128:(g + 1) * 128, :].rearrange(
                                "(c i) d -> i c d", i=32), lv2[g],
                            deps=[v_vt_done, v_tt] if g == 0 else [v_vt_done])
    for c in range(NCH):
        g = c // 4
        j = g % 2
        cb = c % 4
        t_u = mm_group(p, pM[0][0:32, 0:256],
                       [(sTT[0:32, c * 32:(c + 1) * 32],
                         svstr[j][0:32, cb * 256:(cb + 1) * 256])],
                       deps=[vstr_tok[j], v_tt if c == 0 else None,
                             v_u_ev])
        v_u_ev = p.do("vector", lambda e, cb=cb: e.tensor_copy(
            subat[0:32, cb * 256:(cb + 1) * 256], pM[0][0:32, 0:256]),
            deps=[t_u, lq_u_prev if cb == 0 else None, v_vt_done if c == 0 else None])
        if cb == 3:
            c4 = c // 4
            lq_u_prev = p.dma(
                "sync", uTs[c4 * 128:(c4 + 1) * 128, :].rearrange(
                    "(c i) d -> i c d", i=32), s8(subat, 4), lqu,
                deps=[v_u_ev])
            if g + 2 < NCH // 4:
                vstr_tok[j] = p.dma("sync", s8(svstr[j], 4),
                                    vTs[(g + 2) * 128:(g + 3) * 128, :]
                                    .rearrange("(c i) d -> i c d", i=32),
                                    lv2[j], deps=[t_u])
        v_ch2 = None
        for mt in range(2):
            i2, pt, freeat = tpp.acq()
            t_tr = p.do("tensor", lambda e, c=c, mt=mt, pt=pt: e.transpose(
                pt[0:32, 0:128],
                skb[:, mt * LP + PAD + c * 32: mt * LP + PAD + c * 32 + 32],
                sidentb[:]), deps=[freeat])
            v_ch2 = p.do("vector", lambda e, mt=mt, pt=pt: e.tensor_copy(
                schk2[0:32, mt * 128:(mt + 1) * 128], pt[0:32, 0:128]),
                deps=[t_tr, v_w_ev if mt == 0 else None])
            tpp.rel(i2, v_ch2)
        for mt in range(2):
            t_w = mm_group(p, pA[0:128, mt * 32:(mt + 1) * 32],
                           [(schk2[0:32, mt * 128:(mt + 1) * 128],
                             sTT[0:32, c * 32:(c + 1) * 32])],
                           deps=[v_ch2, v_w_ev])
            v_w_ev = p.do("vector", lambda e, cb=cb, mt=mt: e.tensor_copy(
                swbat[:, cb * 64 + mt * 32: cb * 64 + mt * 32 + 32],
                pA[0:128, mt * 32:(mt + 1) * 32]),
                deps=[t_w, lq_w_prev if (cb == 0 and mt == 0) else None])
        if cb == 3:
            lq_w_prev = p.dma("sync", wTs[c // 4], swbat[:], lqw,
                              deps=[v_w_ev])
        v_k2 = None
        for mt in range(2):
            i2, pt, freeat = tpp.acq()
            t_tr = p.do("tensor", lambda e, c=c, mt=mt, pt=pt: e.transpose(
                pt[0:32, 0:128], sk[:, mt * L + c * 32: mt * L + c * 32 + 32],
                sidentb[:]), deps=[freeat])
            v_k2 = p.do("vector", lambda e, mt=mt, cb=cb, pt=pt: e.tensor_copy(
                skbat[0:32, cb * 256 + mt * 128: cb * 256 + mt * 128 + 128],
                pt[0:32, 0:128]),
                deps=[t_tr, lq_k_prev if (cb == 0 and mt == 0) else None])
            tpp.rel(i2, v_k2)
        if cb == 3:
            c4 = c // 4
            lq_k_prev = p.dma(
                "sync", kTs[c4 * 128:(c4 + 1) * 128, :].rearrange(
                    "(c i) d -> i c d", i=32), s8(skbat, 4), lqk,
                deps=[v_k2])
        t_at = mm_group(p, pM[1][0:32, (c % 16) * 32:(c % 16) * 32 + 32],
                        [(sk[:, mt * L + c * 32: mt * L + c * 32 + 32],
                          sq[:, mt * L + c * 32: mt * L + c * 32 + 32])
                         for mt in range(2)],
                        deps=[v_at_ev if c % 16 == 0 else None,
                              v_qkscaled if c == 0 else None])
        if c % 16 == 15:
            v_at_ev = p.do("vector", lambda e, c=c: e.tensor_tensor(
                sAT[0:32, (c - 15) * 32:(c + 1) * 32], pM[1][0:32, 0:512],
                smtr[:], ALU.mult), deps=[t_at])

    if debug:
        p.dma("sync", o_TT, sTT[:], lo, deps=[v_tt])
        p.dma("sync", o_ATd, sAT[:], lo, deps=[v_at_ev])
        p.dma("sync", r8(o_qs), s8(sq, 2), lo, deps=[v_qkscaled])
        p.dma("sync", r8(o_ks), s8(sk, 2), lo, deps=[v_qkscaled])
    scope3.close()
    scopeT.close()
    scope4 = contextlib.ExitStack()
    sustr = [scope4.enter_context(nc.sbuf_tensor(f"sustr{i}", [32, 4 * 256],
                                                 BF16)) for i in range(2)]
    skstr = [scope4.enter_context(nc.sbuf_tensor(f"skstr{i}", [32, 4 * 256],
                                                 BF16)) for i in range(2)]
    v_s0 = p.do("vector", lambda e: e.memset(sSb[0][:], 0.0), deps=[v_at_ev])
    v_s0 = p.do("vector", lambda e: e.memset(sSb[1][:], 0.0), deps=[v_s0])
    ustr_tok = [None, None]
    kstr_tok = [None, None]
    wstr_tok = [None, None]
    lw2 = [p.ctr_new(f"lw2{i}") for i in range(2)]
    for g in range(2):
        ustr_tok[g] = p.dma("sync", s8(sustr[g], 4),
                            uTs[g * 128:(g + 1) * 128, :].rearrange(
                                "(c i) d -> i c d", i=32), lu[g],
                            deps=[lq_u_prev])
        kstr_tok[g] = p.dma("sync", s8(skstr[g], 4),
                            kTs[g * 128:(g + 1) * 128, :].rearrange(
                                "(c i) d -> i c d", i=32), lk2[g],
                            deps=[lq_k_prev])
        wstr_tok[g] = p.dma("sync", swstr[g][:], wTs[g], lw2[g],
                            deps=[lq_w_prev])
    v_sb_ref = [v_s0, v_s0]
    v_u2 = None
    v_del_ev = None
    sDF = spre
    for c in range(NCH):
        g = c // 4
        j = g % 2
        cb = c % 4
        pairs = [(snegI[:], sustr[j][0:32, cb * 256:(cb + 1) * 256])]
        pairs += [(swstr[j][:, cb * 64 + mt * 32: cb * 64 + mt * 32 + 32],
                   sSb[mt][:]) for mt in range(2)]
        t_p1 = mm_group(p, pM[0][0:32, 0:256], pairs,
                        deps=[ustr_tok[j], wstr_tok[j], v_sb_ref[0],
                              v_sb_ref[1], v_u2])
        v_u2 = p.do("vector", lambda e: e.tensor_scalar(
            su2[:], pM[0][0:32, 0:256], -1.0, None, ALU.mult), deps=[t_p1])
        pairs = [(sq[:, mt * L + c * 32: mt * L + c * 32 + 32], sSb[mt][:])
                 for mt in range(2)]
        pairs += [(sAT[0:32, c * 32:(c + 1) * 32], su2[:])]
        t_p2 = mm_group(p, pM[1][0:32, 0:256], pairs,
                        deps=[v_u2, v_del_ev])
        t_ds = None
        for mt in range(2):
            t_ds = p.do("tensor", lambda e, mt=mt, j=j, cb=cb, c=c: e.matmul(
                pS[mt][:, 0:256],
                skstr[j][0:32, cb * 256 + mt * 128: cb * 256 + mt * 128 + 128],
                su2[:], start=(c == 0), stop=(c == NCH - 1),
                skip_group_check=True),
                deps=[kstr_tok[j]] if mt == 0 else ())
        if cb == 3 and g + 2 < NCH // 4:
            ustr_tok[j] = p.dma("sync", s8(sustr[j], 4),
                                uTs[(g + 2) * 128:(g + 3) * 128, :].rearrange(
                                    "(c i) d -> i c d", i=32), lu[j],
                                deps=[t_p1])
            kstr_tok[j] = p.dma("sync", s8(skstr[j], 4),
                                kTs[(g + 2) * 128:(g + 3) * 128, :].rearrange(
                                    "(c i) d -> i c d", i=32), lk2[j],
                                deps=[t_ds])
            wstr_tok[j] = p.dma("sync", swstr[j][:], wTs[g + 2], lw2[j],
                                deps=[t_p1])
        v_sb_ref = [
            p.do("vector", lambda e: e.tensor_copy(sSb[0][:], pS[0][:, 0:256]),
                 deps=[t_ds]),
            p.do("vector", lambda e: e.tensor_copy(sSb[1][:], pS[1][:, 0:256]),
                 deps=[t_ds]),
        ]
        v_del = p.do("vector", lambda e: e.tensor_copy(
            sDel[:], pM[1][0:32, 0:256]),
            deps=[t_p2, v_kb if c == 0 else None])
        v_del_ev = v_del
        for mt in range(2):
            i2, pt, freeat = tpp.acq()
            t_tr = p.do("tensor", lambda e, mt=mt, pt=pt: e.transpose(
                pt[0:128, 0:32], sDel[0:32, mt * 128:(mt + 1) * 128],
                sidentb[0:32, 0:32]), deps=[v_del, freeat])
            v_df = p.do("vector", lambda e, c=c, mt=mt, pt=pt: e.tensor_copy(
                sDF[:, mt * LP + PAD + c * 32: mt * LP + PAD + c * 32 + 32],
                pt[0:128, 0:32]), deps=[t_tr])
            tpp.rel(i2, v_df)
            v_del_ev = v_df
    v_scan_done = v_del_ev

    st_df = p.dma("sync", r8(o_dF),
                  sDF[:].rearrange("p (m t) -> p m t", m=2)[:, :, PAD:PAD + L],
                  lo, deps=[v_scan_done], slow=True)
    last_st = None
    for sg in range(NSEG):
        g0 = sg * SEG
        last = None
        for nb in range(SEG // 512):
            c0 = g0 + nb * 512
            v_sq = None
            for mt in range(2):
                v_sq = p.do("vector", lambda e, mt=mt, c0=c0: e.tensor_tensor(
                    ssq2[:, mt * 512:(mt + 1) * 512],
                    sDF[:, mt * LP + PAD + c0: mt * LP + PAD + c0 + 512],
                    sDF[:, mt * LP + PAD + c0: mt * LP + PAD + c0 + 512],
                    ALU.mult), deps=[v_scan_done, last] if mt == 0 else ())
            t_r = mm_group(p, pM[1][0:1, 0:512],
                           [(sonescol[:], ssq2[:, mt * 512:(mt + 1) * 512])
                            for mt in range(2)], deps=[v_sq, last])
            last = p.do("vector", lambda e, nb=nb: e.tensor_copy(
                srseg[:, nb * 512: nb * 512 + 512],
                pM[1][0:1, 0:512]), deps=[t_r])
            t_r2 = mm_group(p, pM[1][0:1, 0:512],
                            [(sonescol[:],
                              sDF[:, mt * LP + PAD + c0: mt * LP + PAD + c0 + 512])
                             for mt in range(2)], deps=[last])
            last = p.do("vector", lambda e, nb=nb: e.tensor_copy(
                sstt[:, nb * 512: nb * 512 + 512], pM[1][0:1, 0:512]),
                deps=[t_r2])
        v1 = p.do("vector", lambda e: e.tensor_scalar(
            sstt[:, 0:SEG], sstt[:, 0:SEG], 1.0 / DV, None, ALU.mult),
            deps=[last])
        v2 = p.do("vector", lambda e: e.tensor_tensor(
            sstt[:, SEG:2 * SEG], sstt[:, 0:SEG], sstt[:, 0:SEG], ALU.mult),
            deps=[v1])
        v3 = p.do("vector", lambda e: e.scalar_tensor_tensor(
            sstt[:, SEG:2 * SEG], srseg[:, 0:SEG], 1.0 / DV,
            sstt[:, SEG:2 * SEG], ALU.mult, ALU.subtract), deps=[v2])
        v4 = p.do("vector", lambda e: e.tensor_scalar(
            sstt[:, SEG:2 * SEG], sstt[:, SEG:2 * SEG], 1e-30, None,
            ALU.max), deps=[v3])
        v5 = p.do("scalar", lambda e: e.activation(
            sstt[:, SEG:2 * SEG], sstt[:, SEG:2 * SEG], AF.Sqrt), deps=[v4])
        p.dma("sync", o_stats[4:5, g0:g0 + SEG], sstt[0:1, 0:SEG], lo,
              deps=[v5])
        last_st = p.dma("sync", o_stats[5:6, g0:g0 + SEG],
                        sstt[0:1, SEG:2 * SEG], lo, deps=[v5])
    p.wait("sync", (lo, lo.n), (lqu, lqu.n), (lqk, lqk.n), (lqw, lqw.n),
           (lqv, lqv.n))
    scope4.close()
    scope_late.close()
    return p.finish()



def build_router():
    p = Prog("router")
    xn1 = p.din("xn1", [D, TPC], BF16)
    st24 = p.din("st24", [24, TPC])
    Wr1 = p.din("Wr1", [1048, 2048])
    br1c = p.din("br1c", [128, 16])
    Wr2p = p.din("Wr2p", [2048, 12])
    br2p = p.din("br2p", [4, 3])
    ltg4 = p.din("ltg4", [4, 1])
    Wid = p.din("Wid", [D, 4])
    bid4 = p.din("bid4", [4, 1])
    alpha4 = p.din("alpha4", [4, 1])

    o_p = [p.dout(f"p{e}", [4, TPC]) for e in range(3)]
    o_ids = p.dout("ids", [4, TPC])

    R_EPS = 0.025

    sxn = p.sb("sxn", [128, 8 * TPC], BF16)
    sstb = p.sb("sstb", [24, TPC], BF16)
    sWr1 = p.sb("sWr1", [128, 8 * 2048], BF16)
    sWr1s = p.sb("sWr1s", [24, 2048], BF16)
    sbr1 = p.sb("sbr1", [128, 16])
    sW2 = p.sb("sW2", [128, 16 * 12], BF16)
    sbr2 = p.sb("sbr2", [4, 3])
    sltg = p.sb("sltg", [4, 1])
    sitau = p.sb("sitau", [4, 1])
    sWid = p.sb("sWid", [128, 8 * 4], BF16)
    sbid = p.sb("sbid", [4, 1])
    salpha = p.sb("salpha", [4, 1])
    shm = p.sb("shm", [128, 16 * TPC], BF16)
    slog = [p.sb(f"slog{e}", [4, TPC]) for e in range(3)]
    sexp = slog
    ssum = p.sb("ssump", [4, TPC])
    sids = p.sb("sids", [4, TPC])

    pk = [p.ps("pk0", [128, 512]), p.ps("pk1", [128, 512])]
    pe2 = [p.ps("pe20", [4, 512]), p.ps("pe21", [4, 512])]

    ld = p.ctr_new("ld")
    lg = p.ctr_new("lg")
    st = p.ctr_new("st")

    p.dma("sync", s8(sxn, 8), r8(xn1), ld)

    p.dma("sync", sbr1[:], br1c, ld)
    p.dma("sync", sbr2[:], br2p, ld)
    p.dma("sync", sltg[:], ltg4, ld)
    p.dma("sync", sbid[:], bid4, ld)
    p.dma("sync", salpha[:], alpha4, ld)
    p.dma("gpsimd", sstb[:], st24, lg)
    p.dma("gpsimd", s8(sWr1, 8), r8(Wr1[0:1024, :]), lg)
    p.dma("gpsimd", sWr1s[:], Wr1[1024:1048, :], lg)
    p.dma("gpsimd", sW2[:].rearrange("p (a t) -> p a t", a=16), r8(Wr2p), lg)
    p.dma("gpsimd", sWid[:].rearrange("p (a t) -> p a t", a=8), r8(Wid), lg)
    lds = [(ld, ld.n), (lg, lg.n)]

    v_stb = None

    ppA = PP(pk)
    s_hm = None
    for mt in range(16):
        for nb in range(4):
            c0 = nb * 512
            i2, pkt, freeat = ppA.acq()
            pairs = [(sWr1[:, kt * 2048 + mt * 128: kt * 2048 + mt * 128 + 128],
                      sxn[:, kt * TPC + c0: kt * TPC + c0 + 512])
                     for kt in range(8)]
            pairs.append((sWr1s[0:24, mt * 128:(mt + 1) * 128],
                          sstb[0:24, c0:c0 + 512]))
            t_mm = mm_group(p, pkt[:], pairs,
                            deps=[v_stb, freeat] + (lds if mt == 0 and nb == 0
                                                    else []))
            s_hm = p.do("scalar", lambda e, mt=mt, c0=c0, pkt=pkt:
                        e.activation(
                shm[:, mt * TPC + c0: mt * TPC + c0 + 512], pkt[:], AF.Gelu,
                bias=sbr1[:, mt:mt + 1]), deps=[t_mm])
            ppA.rel(i2, s_hm)

    ppB = PP(pe2)
    v_log = None
    for e_ in range(3):
        for nb in range(4):
            c0 = nb * 512
            i2, pet, freeat = ppB.acq()
            t_mm = mm_group(
                p, pet[:],
                [(sW2[:, kt * 12 + e_ * 4: kt * 12 + e_ * 4 + 4],
                  shm[:, kt * TPC + c0: kt * TPC + c0 + 512])
                 for kt in range(16)], deps=[s_hm, freeat])
            v_log = p.do("vector", lambda e, e_=e_, c0=c0, pet=pet:
                         e.tensor_scalar(
                slog[e_][:, c0:c0 + 512], pet[:], sbr2[:, e_:e_ + 1], None,
                ALU.add), deps=[t_mm])
            ppB.rel(i2, v_log)

    s_tau = p.do("scalar", lambda e: e.activation(sitau[:], sltg[:], AF.Exp),
                 deps=lds)
    v_itau = p.do("vector", lambda e: e.reciprocal(sitau[:], sitau[:]),
                  deps=[s_tau])
    s_ex = None
    for e_ in range(3):
        v_sc = p.do("vector", lambda e, e_=e_: e.tensor_scalar(
            slog[e_][:], slog[e_][:], sitau[:], None, ALU.mult),
            deps=[v_log, v_itau] if e_ == 0 else ())
        s_ex = p.do("scalar", lambda e, e_=e_: e.activation(
            sexp[e_][:], slog[e_][:], AF.Exp), deps=[v_sc])
    v_sm = p.do("vector", lambda e: e.tensor_tensor(
        ssum[:], sexp[0][:], sexp[1][:], ALU.add), deps=[s_ex])
    v_sm = p.do("vector", lambda e: e.tensor_tensor(
        ssum[:], ssum[:], sexp[2][:], ALU.add), deps=[v_sm])
    v_sm = p.do("vector", lambda e: e.reciprocal(ssum[:], ssum[:]),
                deps=[v_sm])
    v_pe = None
    for e_ in range(3):
        v_p = p.do("vector", lambda e, e_=e_: e.tensor_tensor(
            sexp[e_][:], sexp[e_][:], ssum[:], ALU.mult), deps=[v_sm])
        v_pe = p.do("vector", lambda e, e_=e_: e.tensor_scalar(
            sexp[e_][:], sexp[e_][:], 1.0 - 3.0 * R_EPS, R_EPS, ALU.mult,
            ALU.add), deps=[v_p])
        p.dma("sync", o_p[e_], sexp[e_][:], st, deps=[v_pe])

    v_id = None
    for nb in range(4):
        c0 = nb * 512
        i2, pet, freeat = ppB.acq()
        t_mm = mm_group(p, pet[:],
                        [(sWid[:, kt * 4:(kt + 1) * 4],
                          sxn[:, kt * TPC + c0: kt * TPC + c0 + 512])
                         for kt in range(8)], deps=[freeat])
        v_id = p.do("vector", lambda e, c0=c0, pet=pet: e.tensor_scalar(
            sids[:, c0:c0 + 512], pet[:], sbid[:], None, ALU.add),
            deps=[t_mm])
        ppB.rel(i2, v_id)
    s_sig = p.do("scalar", lambda e: e.activation(
        sids[:], sids[:], AF.Sigmoid), deps=[v_id])
    s_sa = p.do("scalar", lambda e: e.activation(
        salpha[:], salpha[:], AF.Sigmoid), deps=[s_sig])
    v_ids = p.do("vector", lambda e: e.tensor_scalar(
        sids[:], sids[:], salpha[:], 0.06, ALU.mult, ALU.add), deps=[s_sa])
    p.dma("sync", o_ids, sids[:], st, deps=[v_ids])
    p.wait("sync", (st, st.n))
    return p.finish()



def build_mix():
    p = Prog("mix")
    vF = p.din("vF", [DV, L], BF16)
    fsF = p.din("fsF", [DV, L], BF16)
    flF = p.din("flF", [DV, L], BF16)
    dF = p.din("dF", [DV, L], BF16)
    p4 = p.din("p4", [4, L])
    onwc = p.din("onwc", [128, 2])
    onesr = p.din("onesr", [1, 128])
    onescl = p.din("onescl", [128, 1])

    o_oN = p.dout("oN", [DV, L], BF16)

    sV = p.sb("sV", [128, 2 * L], BF16)
    sFs = p.sb("sFs", [128, 2 * L], BF16)
    sFl = p.sb("sFl", [128, 2 * L], BF16)
    sDl = p.sb("sDl", [128, 2 * L], BF16)
    sON = p.sb("sON", [128, 2 * 512], BF16)
    sp3 = p.sb("sp3", [1, 4 * 512], BF16)
    sonw = p.sb("sonw", [128, 2])
    sonesrb = p.sb("sonesrb", [1, 128], BF16)
    sonesf2 = p.sb("sonesf2", [1, 128])
    sonescol = p.sb("sonescol", [128, 1], BF16)
    sbcs = [p.sb(f"sbcs{i}", [128, 512], BF16) for i in range(4)]
    so = p.sb("so", [128, 2 * 512])
    sosq = p.sb("sosq", [128, 2 * 512], BF16)
    srow = p.sb("srowm", [1, 2 * 512])
    srstdb = p.sb("srstdb", [128, 512], BF16)

    pb = [p.ps("pb0", [128, 512]), p.ps("pb1", [128, 512])]
    pr = p.ps("prm", [1, 512])

    ld = p.ctr_new("ld")
    lg = p.ctr_new("lg")
    st = p.ctr_new("st")

    p.dma("sync", s8(sV, 2), r8(vF), ld)
    p.dma("sync", s8(sFs, 2), r8(fsF), ld)
    p.dma("sync", s8(sFl, 2), r8(flF), ld)
    p.dma("sync", s8(sDl, 2), r8(dF), ld)
    p.dma("sync", sonw[:], onwc, ld)
    p.dma("sync", sonesf2[:], onesr, ld)
    p.dma("gpsimd", sonesrb[:], onesr, lg)
    p.dma("gpsimd", sonescol[:], onescl, lg)
    lds = [(ld, ld.n), (lg, lg.n)]

    pbp = PP(pb)
    v_on = None
    st_on = None
    v_bc_all = None
    lp = p.ctr_new("lp")
    for nb in range(L // 512):
        c0 = nb * 512
        ptok = p.dma("gpsimd", sp3[:].rearrange("p (a t) -> p a t", a=4),
                     p4[:, c0:c0 + 512], lp, deps=[v_bc_all])
        v_bc = None
        for bi in range(4):
            i2, pbt, freeat = pbp.acq()
            src = sp3[0:1, bi * 512: bi * 512 + 512]
            t_bc = p.do("tensor", lambda e, src=src, pbt=pbt: e.matmul(
                pbt[:], sonesrb[:], src, start=True, stop=True),
                deps=[freeat, ptok] + (lds if nb == 0 and bi == 0 else []))
            v_bc = p.do("vector", lambda e, bi=bi, pbt=pbt: e.tensor_copy(
                sbcs[bi][:], pbt[:]), deps=[t_bc])
            pbp.rel(i2, v_bc)
        v_bc_all = ("tensor", p.ctr["tensor"].n)
        v_mix = None
        for mt in range(2):
            o_sl = so[:, mt * 512:(mt + 1) * 512]
            t_sl = sosq[:, mt * 512:(mt + 1) * 512]
            v_mix = p.do("vector", lambda e, mt=mt, c0=c0, o_sl=o_sl: e.tensor_tensor(
                o_sl, sFs[:, mt * L + c0: mt * L + c0 + 512], sbcs[0][:],
                ALU.mult), deps=[v_bc, v_on])
            for bi, ten in ((1, sFl), (2, sDl), (3, sV)):
                v_mix = p.do("vector", lambda e, mt=mt, c0=c0, bi=bi, ten=ten,
                             t_sl=t_sl: e.tensor_tensor(
                    t_sl, ten[:, mt * L + c0: mt * L + c0 + 512],
                    sbcs[bi][:], ALU.mult), deps=[v_mix])
                v_mix = p.do("vector", lambda e, o_sl=o_sl, t_sl=t_sl:
                             e.tensor_tensor(o_sl, o_sl, t_sl, ALU.add),
                             deps=[v_mix])
        v_sq = None
        for mt in range(2):
            v_sq = p.do("vector", lambda e, mt=mt: e.tensor_tensor(
                sosq[:, mt * 512:(mt + 1) * 512],
                so[:, mt * 512:(mt + 1) * 512],
                so[:, mt * 512:(mt + 1) * 512], ALU.mult), deps=[v_mix])
        t_r = mm_group(p, pr[:], [(sonescol[:], sosq[:, mt * 512:(mt + 1) * 512])
                                  for mt in range(2)], deps=[v_sq])
        v_r1 = p.do("vector", lambda e: e.tensor_scalar(
            srow[:, 0:512], pr[:], 1.0 / DV, 1e-5, ALU.mult, ALU.add),
            deps=[t_r])
        s_r = p.do("scalar", lambda e: e.activation(
            srow[:, 512:1024], srow[:, 0:512], AF.Sqrt), deps=[v_r1])
        v_r2 = p.do("vector", lambda e: e.reciprocal(
            srow[:, 0:512], srow[:, 512:1024]), deps=[s_r])
        i2, pbt, freeat = pbp.acq()
        t_bc = p.do("tensor", lambda e, pbt=pbt: e.matmul(
            pbt[:], sonesf2[:], srow[:, 0:512], start=True, stop=True),
            deps=[v_r2, freeat])
        v_rb = p.do("vector", lambda e, pbt=pbt: e.tensor_copy(
            srstdb[:], pbt[:]), deps=[t_bc])
        pbp.rel(i2, v_rb)
        for mt in range(2):
            v_on = p.do("vector", lambda e, mt=mt: e.scalar_tensor_tensor(
                sON[:, mt * 512:(mt + 1) * 512],
                so[:, mt * 512:(mt + 1) * 512], sonw[:, mt:mt + 1],
                srstdb[:], ALU.mult, ALU.mult),
                deps=[v_rb, st_on if mt == 0 else None])
        st_on = p.dma("sync", r8(o_oN)[:, :, c0:c0 + 512], s8(sON, 2), st,
                      deps=[v_on])
    p.wait("sync", (st, st.n))
    return p.finish()



def build_l1out():
    p = Prog("l1out")
    oN = p.din("oN", [D, TPC], BF16)
    Wo = p.din("Wo", [D, D])
    x1F = p.din("x1F", [D, TPC])
    o_acc = p.dout("acc", [D, TPC])

    soN = p.sb("soN", [128, 8 * TPC], BF16)
    sWo = p.sb("sWo", [128, 8 * D], BF16)
    sx1 = p.sb("sx1", [128, 8 * TPC])
    sacc = p.sb("sacc2", [128, 8 * TPC])

    pk = [p.ps("pk0", [128, 512]), p.ps("pk1", [128, 512])]

    ld = p.ctr_new("ld")
    lg = p.ctr_new("lg")
    st = p.ctr_new("st")

    p.dma("sync", s8(soN, 8), r8(oN), ld)
    p.dma("sync", s8(sx1, 8), r8(x1F), ld)
    p.dma("gpsimd", s8(sWo, 8), r8(Wo), lg)
    lds = [(ld, ld.n), (lg, lg.n)]

    pp = PP(pk)
    v_last = None
    for nb in range(4):
        c0 = nb * 512
        for mt in range(8):
            i2, pkt, freeat = pp.acq()
            t_mm = mm_group(
                p, pkt[:],
                [(sWo[:, kt * D + mt * 128: kt * D + mt * 128 + 128],
                  soN[:, kt * TPC + c0: kt * TPC + c0 + 512])
                 for kt in range(8)],
                deps=[freeat] + (lds if nb == 0 and mt == 0 else []))
            v_last = p.do("vector", lambda e, mt=mt, c0=c0, pkt=pkt:
                          e.tensor_tensor(
                sacc[:, mt * TPC + c0: mt * TPC + c0 + 512], pkt[:],
                sx1[:, mt * TPC + c0: mt * TPC + c0 + 512], ALU.add),
                deps=[t_mm])
            pp.rel(i2, v_last)
    p.dma("sync", r8(o_acc), s8(sacc, 8), st, deps=[v_last])
    p.wait("sync", (st, st.n))
    return p.finish()



def _consts():
    return {
        "onesr": np.ones((1, 128), np.float32),
        "onescl": np.ones((128, 1), np.float32),
        "identc": np.eye(128, dtype=np.float32),
        "ones128": np.ones((128, 128), np.float32),
    }


def _get_nc(name, builder):
    if name not in _nc_cache:
        _nc_cache[name] = builder()
    return _nc_cache[name]


def _run(name, builder, in_maps):
    nc = _get_nc(name, builder)
    res = run_bass_kernel_spmd(nc, in_maps, core_ids=list(range(NCORE)))
    return res.results


def _bias_tile(b, ntiles=8):
    return np.ascontiguousarray(
        np.asarray(b).reshape(ntiles, 128).T.astype(np.float32))


def run_l0(inputs):
    C = _consts()
    x = np.asarray(inputs["L0_attn_x"])
    W1 = np.asarray(inputs["L0_attn_W1"], np.float32)
    W2 = np.asarray(inputs["L0_attn_W2"], np.float32)
    W3 = np.asarray(inputs["L0_attn_W3"], np.float32)
    W4 = np.asarray(inputs["L0_attn_W4"], np.float32)
    b1c = _bias_tile(inputs["L0_attn_b1"])
    b2c = _bias_tile(inputs["L0_attn_b2"])
    b3c = _bias_tile(inputs["L0_attn_b3"])
    b4c = _bias_tile(inputs["L0_attn_b4"])

    xFs = []
    for c in range(NCORE):
        bc, sc = c // 4, c % 4
        xFs.append(np.ascontiguousarray(
            x[bc, sc * TPC:(sc + 1) * TPC, :].T.astype(np.float32)))

    in_maps = [{"xF": xFs[c], "onesr": C["onesr"], "onescl": C["onescl"],
                "identc": C["identc"]} for c in range(NCORE)]
    rn = _run("l0norm", build_l0norm, in_maps)
    in_maps = [{"xnF": rn[c]["xnF"], "W1": W1, "W2": W2, "b1c": b1c,
                "b2c": b2c, "onescl": C["onescl"]} for c in range(NCORE)]
    rq = _run("l0qk", build_l0qk, in_maps)
    ra = [{**rn[c], **rq[c]} for c in range(NCORE)]

    in_maps_b = []
    groups = {}
    for g in range(2):
        kT_full = np.concatenate([ra[4 * g + i]["kT"] for i in range(4)], axis=1)
        kTt = np.ascontiguousarray(
            kT_full.reshape(8, 128, 64, 128).transpose(2, 0, 1, 3))
        xnTf = np.concatenate([ra[4 * g + i]["xnT"] for i in range(4)], axis=0)
        xnTt = np.ascontiguousarray(
            xnTf.reshape(64, 128, 8, 128).transpose(2, 0, 1, 3))
        groups[g] = (kTt, xnTt)
    for c in range(NCORE):
        g = c // 4
        kTt, xnTt = groups[g]
        in_maps_b.append({
            "qT": ra[c]["qT"], "brow": ra[c]["brow"], "kTt": kTt, "xnTt": xnTt,
            "W3": W3, "W4": W4, "b3c": b3c, "b4c": b4c, "xF": xFs[c],
            "onesr": C["onesr"], "ones128": C["ones128"]})
    rb = _run("l0flash", build_l0flash, in_maps_b)

    return {"ra": ra, "rb": rb, "xFs": xFs}


def run_wq(inputs):
    C = _consts()
    w10 = np.asarray(inputs["L0_ffn_w1"], np.float32)
    w20 = np.asarray(inputs["L0_ffn_w2"], np.float32)
    w11 = np.asarray(inputs["L1_ffn_w1"], np.float32)
    w21 = np.asarray(inputs["L1_ffn_w2"], np.float32)

    def slices(c):
        return {
            "wsl0": np.ascontiguousarray(w10[c * 512:(c + 1) * 512]),
            "wsl1": np.ascontiguousarray(w20[c * 128:(c + 1) * 128]),
            "wsl2": np.ascontiguousarray(w11[c * 512:(c + 1) * 512]),
            "wsl3": np.ascontiguousarray(w21[c * 128:(c + 1) * 128]),
        }

    in_maps = [{**slices(c), "onescl": C["onescl"]} for c in range(NCORE)]
    r0 = _run("wstat", build_wstat, in_maps)
    parts = np.concatenate([r0[c]["part"] for c in range(NCORE)], axis=0)
    in_maps = [{**slices(c), "parts": parts, "identc": C["identc"],
                "onesr": C["onesr"]} for c in range(NCORE)]
    r1 = _run("wquant", build_wquant, in_maps)

    w1qT0 = np.concatenate([r1[c]["wq0"] for c in range(NCORE)], axis=1)
    w2qT0 = np.concatenate([r1[c]["wq1"] for c in range(NCORE)], axis=1)
    w1qT1 = np.concatenate([r1[c]["wq2"] for c in range(NCORE)], axis=1)
    w2qT1 = np.concatenate([r1[c]["wq3"] for c in range(NCORE)], axis=1)
    invw = r1[0]["invw"]

    def tile_w2(w2qT):
        return np.ascontiguousarray(
            w2qT.reshape(32, 128, 8, 128).transpose(2, 0, 1, 3))

    return {
        "w1qT0": np.ascontiguousarray(w1qT0), "w2qTt0": tile_w2(w2qT0),
        "w1qT1": np.ascontiguousarray(w1qT1), "w2qTt1": tile_w2(w2qT1),
        "invw": invw,
    }


def run_ffn(layer, xFs, accs, wq, inputs):
    C = _consts()
    cw = np.ascontiguousarray(
        np.asarray(inputs[f"L{layer}_ffn_conv_w"], np.float32).reshape(DH, 3))
    cb = np.ascontiguousarray(
        np.asarray(inputs[f"L{layer}_ffn_conv_b"], np.float32)
        .reshape(32, 128).T)
    w1qT = wq[f"w1qT{layer}"]
    w2qTt = wq[f"w2qTt{layer}"]
    invw = wq["invw"]
    iv1 = np.ascontiguousarray(invw[:, 2 * layer:2 * layer + 1])
    iv2 = np.ascontiguousarray(invw[:, 2 * layer + 1:2 * layer + 2])
    zero = np.zeros((D, 1), np.float32)
    in_maps = []
    for c in range(NCORE):
        sc = c % 4
        xhl = zero if sc == 0 else np.ascontiguousarray(xFs[c - 1][:, -1:])
        xhr = zero if sc == 3 else np.ascontiguousarray(xFs[c + 1][:, 0:1])
        in_maps.append({
            "xF": xFs[c], "xhl": xhl, "xhr": xhr, "accIn": accs[c],
            "w1qT": w1qT, "w2qTt": w2qTt, "invw1": iv1, "invw2": iv2,
            "cwc": cw, "cbc": cb, "onesr": C["onesr"], "onescl": C["onescl"],
            "identc": C["identc"]})
    rd = _run("ffn", build_ffn, in_maps)
    return [rd[c]["xout"] for c in range(NCORE)]


def _l1_consts():
    tri = np.tril(np.ones((32, 32), np.float32), -1)
    return {
        "negI": -np.eye(32, dtype=np.float32),
        "mslneg": np.ascontiguousarray(-np.tile(tri, (1, 16))),
        "mtriu": np.ascontiguousarray(
            np.tile(np.triu(np.ones((32, 32), np.float32), 0), (1, 16))),
    }


def run_l1head(xn1_full_g, inputs, debug=False):
    C = _consts()
    C1 = _l1_consts()
    Wq = np.asarray(inputs["L1_attn_Wq"], np.float32)
    Wk = np.asarray(inputs["L1_attn_Wk"], np.float32)
    Wv = np.asarray(inputs["L1_attn_Wv"], np.float32)
    Wb = np.asarray(inputs["L1_attn_Wb"], np.float32)
    cq = np.asarray(inputs["L1_attn_conv_q"], np.float32)
    ck = np.asarray(inputs["L1_attn_conv_k"], np.float32)
    cv = np.asarray(inputs["L1_attn_conv_v"], np.float32)
    fs = np.asarray(inputs["L1_attn_fir_short"], np.float32)
    fl = np.asarray(inputs["L1_attn_fir_long"], np.float32)
    in_maps = []
    for c in range(NCORE):
        g, h = c // 4, c % 4
        sl = slice(h * 256, (h + 1) * 256)
        in_maps.append({
            "xn1": xn1_full_g[g],
            "Wq": np.ascontiguousarray(Wq[:, sl]),
            "Wk": np.ascontiguousarray(Wk[:, sl]),
            "Wv": np.ascontiguousarray(Wv[:, sl]),
            "Wb": np.ascontiguousarray(Wb[:, h:h + 1]),
            "cq": np.ascontiguousarray(cq[sl]),
            "ck": np.ascontiguousarray(ck[sl]),
            "cv": np.ascontiguousarray(cv[sl]),
            "firs": np.ascontiguousarray(fs[h]),
            "firl": np.ascontiguousarray(fl[h]),
            "negI": C1["negI"], "mslneg": C1["mslneg"], "mtriu": C1["mtriu"],
            "onesr": C["onesr"], "onescl": C["onescl"], "identc": C["identc"],
        })
    if debug:
        return _run("l1head_dbg", lambda: build_l1head(debug=True), in_maps)
    return _run("l1head", build_l1head, in_maps)


def run_norm(xFs):
    C = _consts()
    in_maps = [{"xF": xFs[c], "onesr": C["onesr"], "onescl": C["onescl"],
                "identc": C["identc"]} for c in range(NCORE)]
    return _run("l0norm", build_l0norm, in_maps)


def run_router(xn1Fs, statrows, inputs):
    C = _consts()
    Wr1 = np.asarray(inputs["L1_attn_Wr1"], np.float32)
    br1 = np.asarray(inputs["L1_attn_br1"], np.float32)
    Wr2 = np.asarray(inputs["L1_attn_Wr2"], np.float32)
    br2 = np.asarray(inputs["L1_attn_br2"], np.float32)
    ltg = np.asarray(inputs["L1_attn_log_tau_group"], np.float32)
    Wid = np.asarray(inputs["L1_attn_Wid"], np.float32)
    bid = np.asarray(inputs["L1_attn_bid"], np.float32)
    alpha = np.asarray(inputs["L1_attn_alpha_id"], np.float32)

    perm = [(r % 4) * 3 + r // 4 for r in range(12)]
    Wr2p = np.ascontiguousarray(Wr2[:, perm])
    br2p = np.ascontiguousarray(br2[perm].reshape(3, 4).T)
    br1c = _bias_tile(br1, 16)
    ltg4 = np.ascontiguousarray(ltg[[0, 0, 1, 1]].reshape(4, 1))
    bid4 = np.ascontiguousarray(bid.reshape(4, 1))
    alpha4 = np.ascontiguousarray(alpha.reshape(4, 1))

    in_maps = []
    for c in range(NCORE):
        g, sc = c // 4, c % 4
        cols = slice(sc * TPC, (sc + 1) * TPC)
        st24 = np.ascontiguousarray(np.concatenate(
            [np.stack([statrows[4 * g + h][tt] for h in range(4)])
             for tt in range(6)], axis=0)[:, cols])
        in_maps.append({
            "xn1": xn1Fs[c], "st24": st24, "Wr1": Wr1, "br1c": br1c,
            "Wr2p": Wr2p, "br2p": br2p, "ltg4": ltg4, "Wid": Wid,
            "bid4": bid4, "alpha4": alpha4})
    return _run("router", build_router, in_maps)


def run_mix(rf, rg, inputs):
    C = _consts()
    onw = np.asarray(inputs["L1_attn_o_norm_w"], np.float32)
    onwc = np.ascontiguousarray(onw.reshape(2, 128).T)
    in_maps = []
    for c in range(NCORE):
        g, h = c // 4, c % 4
        rows = [np.concatenate([rg[4 * g + sc][f"p{e}"][h:h + 1, :]
                                for sc in range(4)], axis=1) for e in range(3)]
        rows.append(np.concatenate([rg[4 * g + sc]["ids"][h:h + 1, :]
                                    for sc in range(4)], axis=1))
        p4 = np.ascontiguousarray(np.concatenate(rows, axis=0))
        in_maps.append({
            "vF": rf[c]["vF"], "fsF": rf[c]["fsF"], "flF": rf[c]["flF"],
            "dF": rf[c]["dF"], "p4": p4, "onwc": onwc,
            "onesr": C["onesr"], "onescl": C["onescl"]})
    return _run("mix", build_mix, in_maps)


def run_l1out(rh, x1Fs, inputs):
    Wo = np.asarray(inputs["L1_attn_Wo"], np.float32)
    in_maps = []
    for c in range(NCORE):
        g, sc = c // 4, c % 4
        cols = slice(sc * TPC, (sc + 1) * TPC)
        oN = np.ascontiguousarray(np.concatenate(
            [rh[4 * g + h]["oN"][:, cols] for h in range(4)], axis=0))
        in_maps.append({"oN": oN, "Wo": Wo, "x1F": x1Fs[c]})
    return _run("l1out", build_l1out, in_maps)


def kernel(**inputs):
    x = np.asarray(inputs["L0_attn_x"])

    wq = run_wq(inputs)

    r0 = run_l0(inputs)
    xFs = r0["xFs"]
    acc1s = [r0["rb"][c]["acc"] for c in range(NCORE)]
    x1Fs = run_ffn(0, xFs, acc1s, wq, inputs)

    rn1 = run_norm(x1Fs)
    xn1Fs = [rn1[c]["xnF"] for c in range(NCORE)]
    xn1g = {g: np.ascontiguousarray(np.concatenate(
        [xn1Fs[4 * g + i] for i in range(4)], axis=1)) for g in range(2)}

    rf = run_l1head(xn1g, inputs)
    statrows = [rf[c]["stats"] for c in range(NCORE)]
    rg = run_router(xn1Fs, statrows, inputs)
    rh = run_mix(rf, rg, inputs)
    ri = run_l1out(rh, x1Fs, inputs)
    acc2s = [ri[c]["acc"] for c in range(NCORE)]

    x2Fs = run_ffn(1, x1Fs, acc2s, wq, inputs)

    out = np.empty((B, S, D), np.float32)
    for c in range(NCORE):
        bc, sc = c // 4, c % 4
        out[bc, sc * TPC:(sc + 1) * TPC, :] = x2Fs[c].T
    return out


# version 05
# speedup vs baseline: 1.0028x; 1.0028x over previous
import contextlib
import math
import numpy as np
import ml_dtypes
import concourse.bass as bass
import concourse.mybir as mybir
from concourse.bass_utils import run_bass_kernel_spmd

F32 = mybir.dt.float32
BF16 = mybir.dt.bfloat16
I32 = mybir.dt.int32
AF = mybir.ActivationFunctionType
ALU = mybir.AluOpType
AX = mybir.AxisListType
BF = ml_dtypes.bfloat16

B, S, D, H = 2, 8192, 1024, 4
DK = DV = 256
CH = 32
TPC = 2048
DH = 4096
NCORE = 8

ENGS = ("sync", "gpsimd", "tensor", "vector", "scalar")

_nc_cache = {}


class Ctr:
    def __init__(self, p, name):
        self.s = p.stack.enter_context(p.nc.semaphore(name))
        self.n = 0
        self.name = name


class Prog:
    def __init__(self, name):
        self.nc = bass.Bass("TRN2", target_bir_lowering=False)
        self.stack = contextlib.ExitStack()
        self.name = name
        self.sch = []
        self.ctr = {e: Ctr(self, "c_" + e) for e in ENGS}

    def sb(self, name, shape, dt=F32):
        return self.stack.enter_context(self.nc.sbuf_tensor(name, shape, dt))

    def ps(self, name, shape, dt=F32):
        return self.stack.enter_context(self.nc.psum_tensor(name, shape, dt))

    def din(self, name, shape, dt=F32):
        return self.nc.dram_tensor(name, shape, dt, kind="ExternalInput").ap()

    def dout(self, name, shape, dt=F32):
        return self.nc.dram_tensor(name, shape, dt, kind="ExternalOutput").ap()

    def ctr_new(self, name):
        return Ctr(self, name)

    def _emit_deps(self, eng, deps):
        for d in deps:
            if d is None:
                continue
            dc, tgt = d
            if isinstance(dc, str):
                dc = self.ctr[dc]
            if tgt is None or tgt <= 0:
                continue
            self.sch.append((eng, lambda e, s=dc.s, t=tgt: e.wait_ge(s, t)))

    def do(self, eng, fn, deps=()):
        c = self.ctr[eng]
        self._emit_deps(eng, deps)
        c.n += 1
        tgt = c.n
        self.sch.append((eng, lambda e, fn=fn, s=c.s: fn(e).then_inc(s, 1)))
        return (eng, tgt)

    def dma(self, eng, dst, src, ctr, deps=(), slow=False):
        self._emit_deps(eng, deps)
        kw = {"allow_slow_non_contiguous": True} if slow else {}
        ctr.n += 16
        self.sch.append((eng, lambda e, d=dst, s=src: e.dma_start(
            d, s, **kw).then_inc(ctr.s, 16)))
        return (ctr, ctr.n)

    def wait(self, eng, *deps):
        self._emit_deps(eng, deps)

    def finish(self):
        with self.stack:
            with self.nc.Block() as block:
                for engname in ENGS:
                    if not any(e == engname for e, _ in self.sch):
                        continue

                    def make(engname):
                        def body(eng):
                            for e, fn in self.sch:
                                if e == engname:
                                    fn(eng)
                        return body

                    getattr(block, engname)(make(engname))
        return self.nc


def r8(x, p=128):
    return x.rearrange("(a p) t -> p a t", p=p)


def s8(t, ntiles):
    return t[:].rearrange("p (a t) -> p a t", a=ntiles)


class PP:

    def __init__(self, tiles):
        self.t = tiles
        self.free_at = [None] * len(tiles)
        self.i = 0

    def acq(self):
        i = self.i % len(self.t)
        self.i += 1
        return i, self.t[i], self.free_at[i]

    def rel(self, i, tok):
        self.free_at[i] = tok


def mm_group(p, psum_ap, pairs, deps=()):
    n = len(pairs)
    tok = None
    for idx, (lhsT, rhs) in enumerate(pairs):
        d = deps if idx == 0 else ()
        tok = p.do("tensor", lambda e, l=lhsT, r=rhs, idx=idx: e.matmul(
            psum_ap, l, r, start=(idx == 0), stop=(idx == n - 1),
            skip_group_check=True), deps=d)
    return tok



def build_l0norm():
    p = Prog("l0norm")
    xF = p.din("xF", [D, TPC])
    onesr = p.din("onesr", [1, 128])
    onescl = p.din("onescl", [128, 1])
    identc = p.din("identc", [128, 128])

    o_xnT = p.dout("xnT", [TPC, D], BF16)
    o_xnF = p.dout("xnF", [D, TPC], BF16)

    sxF = p.sb("sxF", [128, 8 * TPC])
    sxn = p.sb("sxn", [128, 8 * TPC], BF16)
    sxb = p.sb("sxb", [128, 8 * 512], BF16)
    sxsq = p.sb("sxsq", [128, 8 * 512], BF16)
    sonesf = p.sb("sonesf", [1, 128])
    sonescol = p.sb("sonescol", [128, 1], BF16)
    sident = p.sb("sident", [128, 128], BF16)
    srow = p.sb("srow", [1, 3 * 512])
    smean = p.sb("smean", [1, 512])
    srstd = p.sb("srstd", [1, 512])
    sxnTb = [p.sb(f"sxnTb{i}", [128, D], BF16) for i in range(2)]
    stmp = p.sb("stmp", [128, 8 * 512])

    pr1 = p.ps("pr1", [1, 512])
    pr2 = p.ps("pr2", [1, 512])
    pbc1 = p.ps("pbc1", [128, 512])
    pbc2 = p.ps("pbc2", [128, 512])
    ptr0 = p.ps("ptr0", [128, 128], BF16)
    ptr1 = p.ps("ptr1", [128, 128], BF16)

    ld = p.ctr_new("ld")
    lg = p.ctr_new("lg")
    stt = [p.ctr_new(f"stt{i}") for i in range(2)]
    stf = p.ctr_new("stf")

    p.dma("sync", s8(sxF, 8), r8(xF), ld)
    p.dma("sync", sonesf[:], onesr, ld)
    p.dma("gpsimd", sident[:], identc, lg)
    p.dma("gpsimd", sonescol[:], onescl, lg)
    lds = [(ld, ld.n), (lg, lg.n)]

    NB = TPC // 512
    t_stats_prev = None
    v_apply_last = None
    for nb in range(NB):
        c0 = nb * 512
        v_cast = None
        s_sq = None
        for kt in range(8):
            v_cast = p.do("vector", lambda e, kt=kt, c0=c0: e.tensor_copy(
                sxb[:, kt * 512:(kt + 1) * 512],
                sxF[:, kt * TPC + c0: kt * TPC + c0 + 512]),
                deps=(lds + [t_stats_prev]) if kt == 0 else ())
        for kt in range(8):
            s_sq = p.do("scalar", lambda e, kt=kt, c0=c0: e.activation(
                sxsq[:, kt * 512:(kt + 1) * 512],
                sxF[:, kt * TPC + c0: kt * TPC + c0 + 512], AF.Square),
                deps=(lds + [t_stats_prev]) if kt == 0 else ())
        mm_group(p, pr1[:], [(sonescol[:], sxb[:, kt * 512:(kt + 1) * 512])
                             for kt in range(8)],
                 deps=[v_cast, s_sq, v_apply_last] + lds)
        t_stats = mm_group(p, pr2[:],
                           [(sonescol[:], sxsq[:, kt * 512:(kt + 1) * 512])
                            for kt in range(8)])
        t_stats_prev = t_stats

        v_mean = p.do("vector", lambda e: e.tensor_scalar(
            smean[:], pr1[:], 1.0 / D, None, ALU.mult), deps=[t_stats])
        v_msq = p.do("vector", lambda e: e.tensor_tensor(
            srow[:, 0:512], smean[:], smean[:], ALU.mult), deps=[v_mean])
        v_var = p.do("vector", lambda e: e.scalar_tensor_tensor(
            srow[:, 512:1024], pr2[:], 1.0 / D, srow[:, 0:512], ALU.mult,
            ALU.subtract), deps=[v_msq])
        v_eps = p.do("vector", lambda e: e.tensor_scalar(
            srow[:, 512:1024], srow[:, 512:1024], 1e-5, None, ALU.add),
            deps=[v_var])
        s_std = p.do("scalar", lambda e: e.activation(
            srow[:, 1024:1536], srow[:, 512:1024], AF.Sqrt), deps=[v_eps])
        v_rst = p.do("vector", lambda e: e.reciprocal(
            srstd[:], srow[:, 1024:1536]), deps=[s_std])

        p.do("tensor", lambda e: e.matmul(
            pbc1[:], sonesf[:], smean[:], start=True, stop=True),
            deps=[v_rst, v_apply_last])
        t_bc = p.do("tensor", lambda e: e.matmul(
            pbc2[:], sonesf[:], srstd[:], start=True, stop=True))

        v_apply_last = None
        for kt in range(8):
            v_sub = p.do("vector", lambda e, kt=kt, c0=c0: e.tensor_tensor(
                stmp[:, kt * 512:(kt + 1) * 512],
                sxF[:, kt * TPC + c0: kt * TPC + c0 + 512], pbc1[:],
                ALU.subtract), deps=[t_bc] if kt == 0 else ())
            v_apply_last = p.do("vector", lambda e, kt=kt, c0=c0: e.tensor_tensor(
                sxn[:, kt * TPC + c0: kt * TPC + c0 + 512],
                stmp[:, kt * 512:(kt + 1) * 512], pbc2[:], ALU.mult),
                deps=[v_sub])
    v_xn = v_apply_last

    p.dma("sync", r8(o_xnF), s8(sxn, 8), stf, deps=[v_xn])

    ptr = PP([ptr0, ptr1])
    sbuf_free = [None, None]
    for nt in range(16):
        j = nt % 2
        v_tr = None
        for kt in range(8):
            i, pt, freeat = ptr.acq()
            t_tr = p.do("tensor", lambda e, nt=nt, kt=kt, pt=pt: e.transpose(
                pt[:], sxn[:, kt * TPC + nt * 128: kt * TPC + nt * 128 + 128],
                sident[:]), deps=[v_xn, freeat])
            v_tr = p.do("vector", lambda e, kt=kt, j=j, pt=pt: e.tensor_copy(
                sxnTb[j][:, kt * 128:(kt + 1) * 128], pt[:]),
                deps=[t_tr] + ([sbuf_free[j]] if kt == 0 else []))
            ptr.rel(i, v_tr)
        sbuf_free[j] = p.dma("sync", r8(o_xnT)[:, nt:nt + 1, :],
                             sxnTb[j][:].rearrange("p (a t) -> p a t", a=1),
                             stt[j], deps=[v_tr])
    p.wait("sync", (stf, stf.n), (stt[0], stt[0].n), (stt[1], stt[1].n))
    return p.finish()



def build_l0qk():
    p = Prog("l0qk")
    xnF = p.din("xnF", [D, TPC], BF16)
    W1 = p.din("W1", [D, D])
    W2 = p.din("W2", [D, D])
    b1c = p.din("b1c", [128, 8])
    b2c = p.din("b2c", [128, 8])
    onescl = p.din("onescl", [128, 1])

    o_qT = p.dout("qT", [D, TPC], BF16)
    o_kT = p.dout("kT", [D, TPC], BF16)
    o_brow = p.dout("brow", [1, TPC])

    sxn = p.sb("sxn", [128, 8 * TPC], BF16)
    sW1 = p.sb("sW1", [128, 8 * D], BF16)
    sW2 = p.sb("sW2", [128, 8 * D], BF16)
    sb1 = p.sb("sb1", [128, 8])
    sb2 = p.sb("sb2", [128, 8])
    sonescol = p.sb("sonescol", [128, 1], BF16)
    sqk = p.sb("sqk", [128, 8 * TPC], BF16)
    ssq = p.sb("ssq", [128, 8 * 512], BF16)
    sqss = p.sb("sqss", [1, TPC])
    skss = p.sb("skss", [1, TPC])
    skmax = p.sb("skmax", [1, 1])
    sbrow = p.sb("sbrow", [1, TPC])

    pr1 = p.ps("pr1", [1, 512])
    pk0 = p.ps("pk0", [128, 512])
    pk1 = p.ps("pk1", [128, 512])

    ld = p.ctr_new("ld")
    lg = p.ctr_new("lg")
    st = p.ctr_new("st")

    p.dma("sync", s8(sxn, 8), r8(xnF), ld)
    p.dma("sync", sb1[:], b1c, ld)
    p.dma("sync", sb2[:], b2c, ld)
    p.dma("gpsimd", s8(sW1, 8), r8(W1), lg)
    p.dma("gpsimd", s8(sW2, 8), r8(W2), lg)
    p.dma("gpsimd", sonescol[:], onescl, lg)
    lds = [(ld, ld.n), (lg, lg.n)]

    NB = TPC // 512
    pp = PP([pk0, pk1])
    st_qk = {}
    v_pr_cons = None
    for which, (sW, sbias, srow_out, o_out) in enumerate(
            ((sW1, sb1, sqss, o_qT), (sW2, sb2, skss, o_kT))):
        t_ssq_cons = None
        v_store_dep = None
        for nb in range(NB):
            c0 = nb * 512
            for mt in range(8):
                i, pkt, freeat = pp.acq()
                deps = [freeat]
                if nb == 0 and mt == 0:
                    deps += lds if which == 0 else [st_qk[0]]
                t_mm = mm_group(
                    p, pkt[:],
                    [(sW[:, kt * D + mt * 128: kt * D + mt * 128 + 128],
                      sxn[:, kt * TPC + c0: kt * TPC + c0 + 512])
                     for kt in range(8)], deps=deps)
                v_ev = p.do("vector", lambda e, mt=mt, c0=c0, sbias=sbias,
                            pkt=pkt: e.tensor_scalar(
                    sqk[:, mt * TPC + c0: mt * TPC + c0 + 512], pkt[:],
                    sbias[:, mt:mt + 1], None, ALU.add), deps=[t_mm])
                pp.rel(i, v_ev)
                v_store_dep = v_ev
            v_sq = None
            for kt in range(8):
                v_sq = p.do("vector", lambda e, kt=kt, c0=c0: e.tensor_tensor(
                    ssq[:, kt * 512:(kt + 1) * 512],
                    sqk[:, kt * TPC + c0: kt * TPC + c0 + 512],
                    sqk[:, kt * TPC + c0: kt * TPC + c0 + 512], ALU.mult),
                    deps=[t_ssq_cons, v_store_dep] if kt == 0 else ())
            t_ssq_cons = mm_group(
                p, pr1[:], [(sonescol[:], ssq[:, kt * 512:(kt + 1) * 512])
                            for kt in range(8)], deps=[v_sq, v_pr_cons])
            v_pr_cons = p.do("vector", lambda e, c0=c0, srow_out=srow_out:
                             e.tensor_copy(srow_out[:, c0:c0 + 512], pr1[:]),
                             deps=[t_ssq_cons])
        st_qk[which] = p.dma("sync", r8(o_out), s8(sqk, 8), st,
                             deps=[v_store_dep, v_pr_cons])

    v_red = p.do("vector", lambda e: e.tensor_reduce(
        skmax[:], skss[:], AX.X, ALU.max), deps=[v_pr_cons])
    v_brow = p.do("vector", lambda e: e.tensor_scalar(
        sbrow[:], sqss[:], skmax[:], -0.5, ALU.add, ALU.mult), deps=[v_red])
    p.dma("sync", o_brow, sbrow[:], st, deps=[v_brow])
    p.wait("sync", (st, st.n))
    return p.finish()



def build_l0flash():
    p = Prog("l0flash")
    qT = p.din("qT", [D, TPC], BF16)
    browi = p.din("brow", [1, TPC])
    kTt = p.din("kTt", [64, 8, 128, 128], BF16)
    xnTt = p.din("xnTt", [8, 64, 128, 128], BF16)
    W3 = p.din("W3", [D, D])
    W4 = p.din("W4", [D, D])
    b3c = p.din("b3c", [128, 8])
    b4c = p.din("b4c", [128, 8])
    xF = p.din("xF", [D, TPC])
    onesr = p.din("onesr", [1, 128])
    ones128 = p.din("ones128", [128, 128])

    o_acc = p.dout("acc", [D, TPC])

    sqT = p.sb("sqT", [128, 8 * TPC], BF16)
    sxnTe = [p.sb(f"sxnTe{i}", [128, 64 * 128], BF16) for i in range(2)]
    skc = [p.sb(f"skc{i}", [128, 8 * 128], BF16) for i in range(3)]
    sW3 = p.sb("sW3", [128, 8 * D], BF16)
    sW4 = p.sb("sW4", [128, 8 * D], BF16)
    sb3 = p.sb("sb3", [128, 8])
    sb4 = p.sb("sb4", [128, 8])
    sbrow = p.sb("sbrow", [1, TPC], BF16)
    sonesrb = p.sb("sonesrb", [1, 128], BF16)
    sones128 = p.sb("sones128", [128, 128], BF16)
    sexp = p.sb("sexp", [128, 64 * 256], BF16)
    sbrc = p.sb("sbrc", [128, 256])
    sZn = p.sb("sZn", [128, 8 * 256], BF16)
    so3 = p.sb("so3", [128, 8 * 256], BF16)
    sxFb = p.sb("sxFb", [128, 8 * 256])
    sacc = p.sb("sacc", [128, 8 * 256])

    psc = [p.ps("psc0", [128, 512]), p.ps("psc1", [128, 512])]
    pden = p.ps("pden", [128, 512])
    pz = [p.ps("pz0", [128, 512]), p.ps("pz1", [128, 512])]
    po = [p.ps("po0", [128, 512]), p.ps("po1", [128, 512])]

    ld = p.ctr_new("ld")
    lg = p.ctr_new("lg")
    lk = [p.ctr_new(f"lk{i}") for i in range(3)]
    le = [p.ctr_new(f"le{i}") for i in range(2)]
    lf = p.ctr_new("lf")
    st = p.ctr_new("st")

    p.dma("sync", s8(sqT, 8), r8(qT), ld)
    p.dma("sync", sb3[:], b3c, ld)
    p.dma("sync", sb4[:], b4c, ld)
    p.dma("gpsimd", s8(sW3, 8), r8(W3), lg)
    p.dma("gpsimd", s8(sW4, 8), r8(W4), lg)
    p.dma("gpsimd", sbrow[:], browi, lg)
    p.dma("gpsimd", sonesrb[:], onesr, lg)
    p.dma("gpsimd", sones128[:], ones128, lg)
    lds = [(ld, ld.n), (lg, lg.n)]

    NKT = S // 128
    NQB = TPC // 256

    k_loaded = {}
    t_kdone = {}
    s_expdone = {}
    t_z_last = None
    pzpp = PP(pz)
    opp = PP(po)
    xnbuf_free = [None, None]
    xn_le = [None, None]
    v_zn_last = None
    v_rec_prev = None
    v_acc_prev = None
    st_acc_prev = None

    for qb in range(NQB):
        q0 = qb * 256
        lf_prev = p.dma("sync", s8(sxFb, 8), r8(xF)[:, :, q0:q0 + 256], lf,
                        deps=[v_acc_prev])

        for kt in range(NKT):
            idx = qb * NKT + kt
            j = idx % 3
            k_loaded[idx] = p.dma(
                "gpsimd", s8(skc[j], 8), kTt[kt].rearrange("a p t -> p a t"),
                lk[j], deps=[t_kdone.get(idx - 3)])

            par = idx % 2
            deps = [k_loaded[idx], s_expdone.get(idx - 2)]
            if idx == 0:
                deps += lds
            pairs = [(sonesrb[:], sbrow[:, q0:q0 + 256])]
            pairs += [(skc[j][:, et * 128:(et + 1) * 128],
                       sqT[:, et * TPC + q0: et * TPC + q0 + 256])
                      for et in range(8)]
            t_sc = mm_group(p, psc[par][:, 0:256], pairs, deps=deps)
            t_kdone[idx] = t_sc

            s_expdone[idx] = p.do("scalar", lambda e, kt=kt, par=par: e.activation(
                sexp[:, kt * 256:(kt + 1) * 256], psc[par][:, 0:256], AF.Exp,
                scale=1.0 / 32.0),
                deps=[t_sc] + ([t_z_last] if kt == 0 else []))

        t_den = mm_group(p, pden[:, 0:256],
                         [(sones128[:], sexp[:, kt * 256:(kt + 1) * 256])
                          for kt in range(NKT)],
                         deps=[s_expdone[qb * NKT + NKT - 1], v_rec_prev])
        v_rec = p.do("vector", lambda e: e.reciprocal(sbrc[:], pden[:, 0:256]),
                     deps=[t_den, v_zn_last])
        v_rec_prev = v_rec

        for et in range(2):
            jj = et % 2
            xn_le[jj] = p.dma("sync", s8(sxnTe[jj], 64),
                              xnTt[et].rearrange("a p t -> p a t"), le[jj],
                              deps=[xnbuf_free[jj]])
        for et in range(8):
            jj = et % 2
            i, pzt, freeat = pzpp.acq()
            t_z = mm_group(p, pzt[:, 0:256],
                           [(sxnTe[jj][:, kt * 128:(kt + 1) * 128],
                             sexp[:, kt * 256:(kt + 1) * 256])
                            for kt in range(NKT)],
                           deps=[xn_le[jj], freeat])
            t_z_last = t_z
            xnbuf_free[jj] = t_z
            if et + 2 < 8:
                xn_le[jj] = p.dma("sync", s8(sxnTe[jj], 64),
                                  xnTt[et + 2].rearrange("a p t -> p a t"),
                                  le[jj], deps=[t_z])
            v_zn_last = p.do("vector", lambda e, et=et, pzt=pzt: e.tensor_tensor(
                sZn[:, et * 256:(et + 1) * 256], pzt[:, 0:256], sbrc[:],
                ALU.mult), deps=[t_z, v_rec])
            pzpp.rel(i, v_zn_last)

        v_so3_last = None
        for mt in range(8):
            i, pot, freeat = opp.acq()
            t_mm = mm_group(p, pot[:, 0:256],
                            [(sW3[:, kt * D + mt * 128: kt * D + mt * 128 + 128],
                              sZn[:, kt * 256:(kt + 1) * 256])
                             for kt in range(8)], deps=[v_zn_last, freeat])
            v_so3_last = p.do("vector", lambda e, mt=mt, pot=pot: e.tensor_scalar(
                so3[:, mt * 256:(mt + 1) * 256], pot[:, 0:256],
                sb3[:, mt:mt + 1], None, ALU.add), deps=[t_mm])
            opp.rel(i, v_so3_last)

        for mt in range(8):
            i, pot, freeat = pzpp.acq()
            t_mm = mm_group(p, pot[:, 0:256],
                            [(sW4[:, kt * D + mt * 128: kt * D + mt * 128 + 128],
                              so3[:, kt * 256:(kt + 1) * 256])
                             for kt in range(8)], deps=[v_so3_last, freeat])
            v_b4 = p.do("vector", lambda e, mt=mt, pot=pot: e.tensor_scalar(
                sacc[:, mt * 256:(mt + 1) * 256], pot[:, 0:256],
                sb4[:, mt:mt + 1], None, ALU.add),
                deps=[t_mm, lf_prev, st_acc_prev])
            v_acc_prev = p.do("vector", lambda e, mt=mt: e.tensor_tensor(
                sacc[:, mt * 256:(mt + 1) * 256],
                sacc[:, mt * 256:(mt + 1) * 256],
                sxFb[:, mt * 256:(mt + 1) * 256], ALU.add), deps=[v_b4])
            pzpp.rel(i, v_acc_prev)

        st_acc_prev = p.dma("sync", r8(o_acc)[:, :, q0:q0 + 256], s8(sacc, 8),
                            st, deps=[v_acc_prev])

    p.wait("sync", (st, st.n))
    return p.finish()



def build_wstat():
    p = Prog("wstat")
    wins = [p.din("wsl0", [512, 1024]), p.din("wsl1", [128, 4096]),
            p.din("wsl2", [512, 1024]), p.din("wsl3", [128, 4096])]
    onescl = p.din("onescl", [128, 1])
    o_part = p.dout("part", [1, 4])

    sw = p.sb("sw", [128, 4 * 4096])
    scol = p.sb("scol", [128, 4])
    sacc = p.sb("sacc", [128, 1])
    sonescf = p.sb("sonescf", [128, 1])
    srow = p.sb("srow", [1, 32])
    sout = p.sb("sout", [1, 4])
    ptr = p.ps("ptr", [1, 512])

    ld = p.ctr_new("ld")
    lg = p.ctr_new("lg")
    st = p.ctr_new("st")

    p.dma("sync", sonescf[:], onescl, ld)
    ldtoks = []
    for j, w in enumerate(wins):
        na = 4 if w.tensor.shape[0] == 512 else 1
        ldtoks.append(p.dma(
            "sync", sw[:, j * 4096:(j + 1) * 4096].rearrange(
                "p (a t) -> p a t", a=na), r8(w), ld))

    prev = None
    for j, w in enumerate(wins):
        na = 4 if w.tensor.shape[0] == 512 else 1
        base = j * 4096
        width = 4096 // na
        r = None
        for a in range(na):
            r = p.do("vector", lambda e, a=a, base=base, width=width, j=j:
                     e.tensor_reduce(
                scol[:, a:a + 1], sw[:, base + a * width: base + (a + 1) * width],
                AX.X, ALU.add, apply_absolute_value=True),
                deps=[(ld, ld.n), prev] if a == 0 else ())
        if na == 4:
            r = p.do("vector", lambda e: e.tensor_tensor(
                scol[:, 0:1], scol[:, 0:1], scol[:, 1:2], ALU.add), deps=[r])
            r = p.do("vector", lambda e: e.tensor_tensor(
                scol[:, 2:3], scol[:, 2:3], scol[:, 3:4], ALU.add), deps=[r])
            r = p.do("vector", lambda e: e.tensor_tensor(
                sacc[:], scol[:, 0:1], scol[:, 2:3], ALU.add), deps=[r])
        else:
            r = p.do("vector", lambda e: e.tensor_copy(
                sacc[:], scol[:, 0:1]), deps=[r])
        t = p.do("tensor", lambda e: e.matmul(
            ptr[0:1, 0:1], sonescf[:], sacc[:], start=True, stop=True),
            deps=[r, prev])
        prev = p.do("vector", lambda e, j=j: e.tensor_copy(
            sout[:, j:j + 1], ptr[0:1, 0:1]), deps=[t])
    p.dma("sync", o_part, sout[:], st, deps=[prev])
    p.wait("sync", (st, st.n))
    return p.finish()



def build_wquant():
    p = Prog("wquant")
    wins = [p.din("wsl0", [512, 1024]), p.din("wsl1", [128, 4096]),
            p.din("wsl2", [512, 1024]), p.din("wsl3", [128, 4096])]
    parts = p.din("parts", [8, 4])
    identc = p.din("identc", [128, 128])
    onesr = p.din("onesr", [1, 128])

    outs = [p.dout("wq0", [1024, 512], BF16), p.dout("wq1", [4096, 128], BF16),
            p.dout("wq2", [1024, 512], BF16), p.dout("wq3", [4096, 128], BF16)]
    o_invw = p.dout("invw", [1, 4])

    sw = p.sb("sw", [128, 4 * 4096])
    swq = p.sb("swq", [128, 4096], BF16)
    swi = p.sb("swi", [128, 4096], I32)
    swt = p.sb("swt", [128, 32 * 128], BF16)
    sparts = p.sb("sparts", [8, 4])
    sident = p.sb("sident", [128, 128])
    sidentb = p.sb("sidentb", [128, 128], BF16)
    sonesf = p.sb("sonesf", [1, 128])
    sr4 = p.sb("sr4", [4, 8])
    sc4 = p.sb("sc4", [4, 1])
    ssc = p.sb("ssc", [1, 4])
    sinv = p.sb("sinv", [1, 4])
    sscol = p.sb("sscol", [128, 4])

    ptr = p.ps("ptr", [1, 512])
    ptrb = [p.ps(f"ptrb{i}", [128, 128], BF16) for i in range(2)]
    pbc = p.ps("pbc", [128, 512])

    ld = p.ctr_new("ld")
    lg = p.ctr_new("lg")
    lw = p.ctr_new("lw")
    st = p.ctr_new("st")

    p.dma("sync", sparts[:], parts, ld)
    p.dma("sync", sonesf[:], onesr, ld)
    p.dma("gpsimd", sident[:], identc, lg)
    p.dma("gpsimd", sidentb[:], identc, lg)
    lds = [(ld, ld.n), (lg, lg.n)]

    t = p.do("tensor", lambda e: e.transpose(pbc[0:4, 0:8], sparts[:],
                                             sident[0:8, 0:8]), deps=lds)
    v = p.do("vector", lambda e: e.tensor_copy(sr4[:], pbc[0:4, 0:8]), deps=[t])
    v = p.do("vector", lambda e: e.tensor_reduce(
        sc4[:], sr4[:], AX.X, ALU.add), deps=[v])
    v = p.do("vector", lambda e: e.tensor_scalar(
        sc4[:], sc4[:], 1.0 / (DH * D), 1e-5, ALU.mult, ALU.max), deps=[v])
    t2 = p.do("tensor", lambda e: e.transpose(pbc[0:1, 0:4], sc4[:],
                                              sident[0:4, 0:4]), deps=[v])
    v_inv = p.do("vector", lambda e: e.tensor_copy(sinv[:], pbc[0:1, 0:4]),
                 deps=[t2])
    p.dma("sync", o_invw, sinv[:], st, deps=[v_inv])
    v_rec = p.do("vector", lambda e: e.reciprocal(ssc[:], sinv[:]),
                 deps=[v_inv])
    t3 = p.do("tensor", lambda e: e.matmul(
        pbc[:, 0:4], sonesf[:], ssc[:], start=True, stop=True), deps=[v_rec])
    v_sc = p.do("vector", lambda e: e.tensor_copy(sscol[:], pbc[:, 0:4]),
                deps=[t3])

    wtoks = []
    for j, w in enumerate(wins):
        na = 4 if w.tensor.shape[0] == 512 else 1
        wtoks.append(p.dma("sync", sw[:, j * 4096:(j + 1) * 4096].rearrange(
            "p (a t) -> p a t", a=na), r8(w), lw))

    ptrpp = PP(ptrb)
    v_tr = None
    st_prev = None
    for j, (w, o) in enumerate(zip(wins, outs)):
        tall = w.tensor.shape[0] == 512
        na = 4 if tall else 1
        ncol = 4096 // na
        base = j * 4096
        for a in range(na):
            v_q = p.do("vector", lambda e, a=a, base=base, ncol=ncol, j=j:
                       e.tensor_scalar(
                swi[:, 0:ncol], sw[:, base + a * ncol: base + (a + 1) * ncol],
                sscol[:, j:j + 1], None, ALU.mult),
                deps=[(lw, lw.n), v_sc, v_tr])
            v_q = p.do("vector", lambda e, ncol=ncol: e.tensor_scalar(
                swq[:, 0:ncol], swi[:, 0:ncol], 1.0, -1.0, ALU.min, ALU.max),
                deps=[v_q])
            for cc in range(ncol // 128):
                i, pt, freeat = ptrpp.acq()
                t_tr = p.do("tensor", lambda e, cc=cc, pt=pt: e.transpose(
                    pt[:], swq[:, cc * 128:(cc + 1) * 128], sidentb[:]),
                    deps=[v_q, freeat, (lg, lg.n)])
                v_tr = p.do("vector", lambda e, cc=cc, a=a, na=na, pt=pt:
                            e.tensor_copy(
                    swt[:, (cc * na + a) * 128:(cc * na + a) * 128 + 128],
                    pt[:]), deps=[t_tr, st_prev if (a == 0 and cc == 0) else None])
                ptrpp.rel(i, v_tr)
        ntc = ncol // 128
        st_prev = p.dma("sync", r8(o), swt[:, 0:ntc * na * 128].rearrange(
            "p (a t) -> p a t", a=ntc), st, deps=[v_tr])
    p.wait("sync", (st, st.n))
    return p.finish()



def build_ffn():
    p = Prog("ffn")
    xF = p.din("xF", [D, TPC])
    xhl = p.din("xhl", [D, 1])
    xhr = p.din("xhr", [D, 1])
    accIn = p.din("accIn", [D, TPC])
    w1qT = p.din("w1qT", [D, DH], BF16)
    w2qTt = p.din("w2qTt", [8, 32, 128, 128], BF16)
    invw1 = p.din("invw1", [1, 1])
    invw2 = p.din("invw2", [1, 1])
    cwc = p.din("cwc", [DH, 3])
    cbc = p.din("cbc", [128, 32])
    onesr = p.din("onesr", [1, 128])
    onescl = p.din("onescl", [128, 1])
    identc = p.din("identc", [128, 128])

    o_x = p.dout("xout", [D, TPC])

    NBL = 8
    W = 258

    sw1 = p.sb("sw1", [128, 8 * DH], BF16)
    sw2b = [p.sb(f"sw2b{i}", [128, 32 * 128], BF16) for i in range(2)]
    sdiag = p.sb("sdiag", [128, 96 * 128], BF16)
    scw = p.sb("scw", [128, 96])
    scb = p.sb("scb", [128, 32])
    sidentb = p.sb("sidentb", [128, 128], BF16)
    sident = p.sb("sident", [128, 128])
    sonesf = p.sb("sonesf", [1, 128])
    sonescol = p.sb("sonescol", [128, 1], BF16)
    sinv1 = p.sb("sinv1", [1, 1])
    sinv2 = p.sb("sinv2", [1, 1])
    sxfb = p.sb("sxfb", [128, 8 * W])
    sacb = p.sb("sacb", [128, 8 * 256])
    sxbb = p.sb("sxbb", [128, 8 * W], BF16)
    sxsqb = p.sb("sxsqb", [128, 8 * W], BF16)
    sxnb = p.sb("sxnb", [128, 8 * W])
    sxq = p.sb("sxq", [128, 8 * W], BF16)
    swiq = p.sb("swiq", [128, W], I32)
    shh = p.sb("shh", [128, 2 * W], BF16)
    sy = p.sb("sy", [128, 32 * 256], BF16)
    sxq2 = p.sb("sxq2", [128, 32 * 256], BF16)
    swiq2 = p.sb("swiq2", [128, 256], I32)
    sam = p.sb("sam", [128, W])
    sfold = p.sb("sfold", [128, W])
    samrow = p.sb("samrow", [1, 4 * 512])
    srows = p.sb("srows", [1, 4 * 512])
    sB = p.sb("sB", [128, W])
    dB = p.sb("dB", [128, W])
    s2B = p.sb("s2B", [128, 256])
    d2B = p.sb("d2B", [128, 256])
    sxo = p.sb("sxo", [128, 8 * 256])

    pm = [p.ps("pm0", [128, 512]), p.ps("pm1", [128, 512])]
    pe_ = [p.ps("pe0", [128, 512]), p.ps("pe1", [128, 512])]
    pbc1 = p.ps("pbc1", [128, 512])
    pbc2 = p.ps("pbc2", [128, 512])
    prow = p.ps("prow", [1, 512])
    prow2 = p.ps("prow2", [128, 512])

    ld = p.ctr_new("ld")
    lg = p.ctr_new("lg")
    lf = p.ctr_new("lf")
    la = p.ctr_new("la")
    lw2 = [p.ctr_new(f"lw2{i}") for i in range(2)]
    st = p.ctr_new("st")

    p.dma("sync", sonesf[:], onesr, ld)
    p.dma("sync", scw[:].rearrange("p (a t) -> p a t", a=32),
          r8(cwc), ld)
    p.dma("sync", scb[:], cbc, ld)
    p.dma("sync", sident[:], identc, ld)
    p.dma("sync", sinv1[:], invw1, ld)
    p.dma("sync", sinv2[:], invw2, ld)
    p.dma("gpsimd", s8(sw1, 8), r8(w1qT), lg)
    p.dma("gpsimd", sidentb[:], identc, lg)
    p.dma("gpsimd", sonescol[:], onescl, lg)
    lds = [(ld, ld.n), (lg, lg.n)]

    v_diag = None
    for mt in range(32):
        for k in range(3):
            v_diag = p.do("vector", lambda e, mt=mt, k=k: e.tensor_scalar(
                sdiag[:, (mt * 3 + k) * 128:(mt * 3 + k) * 128 + 128],
                sidentb[:], scw[:, mt * 3 + k: mt * 3 + k + 1], None,
                ALU.mult), deps=lds if (mt == 0 and k == 0) else ())

    v_apply = None
    s_sq_prev = None
    s_silu_last = None
    v_xo_last = None
    st_prev = None
    t_row_cons = None
    v_row_cons = None
    t_conv_prev = [None, None]
    s_silu_prev = [None, None]
    v_pe_cons = [None, None]
    w2_tok = [None, None]
    t_w2_cons = [None, None]

    for nb in range(NBL):
        t0 = nb * 256
        fdeps = [v_apply, s_sq_prev]
        if nb == 0:
            lf_t = p.dma("sync", s8(sxfb, 8)[:, :, 1:258],
                         r8(xF)[:, :, 0:257], lf, deps=fdeps)
            lf_t = p.dma("sync", s8(sxfb, 8)[:, :, 0:1], r8(xhl), lf, slow=True)
        elif nb == NBL - 1:
            lf_t = p.dma("sync", s8(sxfb, 8)[:, :, 0:257],
                         r8(xF)[:, :, t0 - 1:TPC], lf, deps=fdeps)
            lf_t = p.dma("sync", s8(sxfb, 8)[:, :, 257:258], r8(xhr), lf, slow=True)
        else:
            lf_t = p.dma("sync", s8(sxfb, 8), r8(xF)[:, :, t0 - 1:t0 + 257],
                         lf, deps=fdeps)
        la_t = p.dma("sync", s8(sacb, 8), r8(accIn)[:, :, t0:t0 + 256], la,
                     deps=[v_xo_last])

        v_cast = None
        s_sq = None
        for kt in range(8):
            v_cast = p.do("vector", lambda e, kt=kt: e.tensor_copy(
                sxbb[:, kt * W:(kt + 1) * W], sxfb[:, kt * W:(kt + 1) * W]),
                deps=([(lf, lf.n), t_row_cons] if kt == 0 else ()))
        for kt in range(8):
            s_sq = p.do("scalar", lambda e, kt=kt: e.activation(
                sxsqb[:, kt * W:(kt + 1) * W], sxfb[:, kt * W:(kt + 1) * W],
                AF.Square), deps=([(lf, lf.n), t_row_cons] if kt == 0 else ()))
        t_sum = mm_group(p, prow[0:1, 0:W],
                         [(sonescol[:], sxbb[:, kt * W:(kt + 1) * W])
                          for kt in range(8)], deps=[v_cast, s_sq, v_row_cons])
        v_mean = p.do("vector", lambda e: e.tensor_scalar(
            srows[:, 0:W], prow[0:1, 0:W], 1.0 / D, None, ALU.mult),
            deps=[t_sum])
        t_ssq = mm_group(p, prow[0:1, 0:W],
                         [(sonescol[:], sxsqb[:, kt * W:(kt + 1) * W])
                          for kt in range(8)], deps=[v_mean])
        v_msq = p.do("vector", lambda e: e.tensor_tensor(
            srows[:, 512:512 + W], srows[:, 0:W], srows[:, 0:W], ALU.mult),
            deps=[t_ssq])
        v_var = p.do("vector", lambda e: e.scalar_tensor_tensor(
            srows[:, 1024:1024 + W], prow[0:1, 0:W], 1.0 / D,
            srows[:, 512:512 + W], ALU.mult, ALU.subtract), deps=[v_msq])
        v_eps = p.do("vector", lambda e: e.tensor_scalar(
            srows[:, 1024:1024 + W], srows[:, 1024:1024 + W], 1e-5, None,
            ALU.add), deps=[v_var])
        s_std = p.do("scalar", lambda e: e.activation(
            srows[:, 1536:1536 + W], srows[:, 1024:1024 + W], AF.Sqrt),
            deps=[v_eps])
        v_rst = p.do("vector", lambda e: e.reciprocal(
            srows[:, 512:512 + W], srows[:, 1536:1536 + W]), deps=[s_std])
        t_row_cons = t_ssq
        s_sq_prev = s_sq
        v_row_cons = v_var

        t_bc1 = p.do("tensor", lambda e: e.matmul(
            pbc1[:, 0:W], sonesf[:], srows[:, 0:W], start=True, stop=True),
            deps=[v_rst])
        t_bc2 = p.do("tensor", lambda e: e.matmul(
            pbc2[:, 0:W], sonesf[:], srows[:, 512:512 + W], start=True,
            stop=True))
        v_apply = None
        for kt in range(8):
            v_s = p.do("vector", lambda e, kt=kt: e.tensor_tensor(
                sxnb[:, kt * W:(kt + 1) * W], sxfb[:, kt * W:(kt + 1) * W],
                pbc1[:, 0:W], ALU.subtract),
                deps=[t_bc2] if kt == 0 else ())
            v_apply = p.do("vector", lambda e, kt=kt: e.tensor_tensor(
                sxnb[:, kt * W:(kt + 1) * W], sxnb[:, kt * W:(kt + 1) * W],
                pbc2[:, 0:W], ALU.mult), deps=[v_s])

        v_am = p.do("vector", lambda e: e.scalar_tensor_tensor(
            sam[:], sxnb[:, 0:W], -1.0, sxnb[:, 0:W], ALU.mult, ALU.max),
            deps=[v_apply])
        for kt in range(1, 8):
            v_am = p.do("vector", lambda e, kt=kt: e.scalar_tensor_tensor(
                sfold[:, 0:W], sxnb[:, kt * W:(kt + 1) * W], -1.0,
                sxnb[:, kt * W:(kt + 1) * W], ALU.mult, ALU.max),
                deps=[v_am])
            v_am = p.do("vector", lambda e: e.tensor_tensor(
                sam[:], sam[:], sfold[:, 0:W], ALU.max), deps=[v_am])
        v_am = p.do("vector", lambda e: e.tensor_copy(
            sfold[0:64, 0:W], sam[64:128, :]), deps=[v_am])
        v_am = p.do("vector", lambda e: e.tensor_tensor(
            sam[0:64, :], sam[0:64, :], sfold[0:64, 0:W], ALU.max),
            deps=[v_am])
        v_am = p.do("vector", lambda e: e.tensor_copy(
            sfold[0:32, 0:W], sam[32:64, :]), deps=[v_am])
        v_am = p.do("vector", lambda e: e.tensor_tensor(
            sam[0:32, :], sam[0:32, :], sfold[0:32, 0:W], ALU.max),
            deps=[v_am])
        v_row = None
        for cc, wid in ((0, 128), (1, 128), (2, 2)):
            t_t1 = p.do("tensor", lambda e, cc=cc, wid=wid: e.transpose(
                prow2[0:wid, 0:32], sam[0:32, cc * 128:cc * 128 + wid],
                sident[0:32, 0:32]), deps=[v_am, v_row])
            v_c1 = p.do("vector", lambda e, cc=cc, wid=wid: e.tensor_reduce(
                sam[0:wid, 0:1], prow2[0:wid, 0:32], AX.X, ALU.max),
                deps=[t_t1])
            t_t2 = p.do("tensor", lambda e, cc=cc, wid=wid: e.transpose(
                prow2[0:1, 0:wid], sam[0:wid, 0:1], sident[0:wid, 0:wid]),
                deps=[v_c1])
            v_row = p.do("vector", lambda e, cc=cc, wid=wid: e.tensor_copy(
                samrow[0:1, cc * 128:cc * 128 + wid], prow2[0:1, 0:wid]),
                deps=[t_t2])
        v_r1 = p.do("vector", lambda e: e.tensor_scalar(
            samrow[0:1, 512:512 + W], samrow[0:1, 0:W], 1e-5, None, ALU.max),
            deps=[v_row])
        v_r2 = p.do("vector", lambda e: e.reciprocal(
            samrow[0:1, 1024:1024 + W], samrow[0:1, 512:512 + W]),
            deps=[v_r1])
        v_r3 = p.do("vector", lambda e: e.tensor_scalar(
            samrow[0:1, 1024:1024 + W], samrow[0:1, 1024:1024 + W], 127.0,
            None, ALU.mult), deps=[v_r2])
        v_r4 = p.do("vector", lambda e: e.tensor_scalar(
            samrow[0:1, 1536:1536 + W], samrow[0:1, 512:512 + W],
            sinv1[0:1, 0:1], 1.0 / 127.0, ALU.mult, ALU.mult), deps=[v_r3])
        t_b1 = p.do("tensor", lambda e: e.matmul(
            pbc1[:, 0:W], sonesf[:], samrow[0:1, 1024:1024 + W], start=True,
            stop=True), deps=[v_r4, v_apply])
        t_b2 = p.do("tensor", lambda e: e.matmul(
            pbc2[:, 0:W], sonesf[:], samrow[0:1, 1536:1536 + W], start=True,
            stop=True))
        v_sB = p.do("vector", lambda e: e.tensor_copy(sB[:], pbc1[:, 0:W]),
                    deps=[t_b2])
        v_dB = p.do("vector", lambda e: e.tensor_copy(dB[:], pbc2[:, 0:W]),
                    deps=[v_sB])

        v_xq = None
        for kt in range(8):
            v_i = p.do("vector", lambda e, kt=kt: e.tensor_tensor(
                swiq[:], sxnb[:, kt * W:(kt + 1) * W], sB[:], ALU.mult),
                deps=[v_dB] if kt == 0 else [v_xq])
            v_xq = p.do("vector", lambda e, kt=kt: e.tensor_tensor(
                sxq[:, kt * W:(kt + 1) * W], swiq[:], dB[:], ALU.mult),
                deps=[v_i])

        for mt in range(32):
            par = mt % 2
            t_mm1 = mm_group(
                p, pm[par][:, 0:W],
                [(sw1[:, kt * DH + mt * 128: kt * DH + mt * 128 + 128],
                  sxq[:, kt * W:(kt + 1) * W]) for kt in range(8)],
                deps=[v_xq, s_silu_prev[par]])
            v_h = p.do("vector", lambda e, par=par: e.tensor_copy(
                shh[:, par * W:par * W + W], pm[par][:, 0:W]),
                deps=[t_mm1, t_conv_prev[par]])
            t_conv = mm_group(
                p, pm[par][:, 0:256],
                [(sdiag[:, (mt * 3 + k) * 128:(mt * 3 + k) * 128 + 128],
                  shh[:, par * W + k: par * W + k + 256]) for k in range(3)],
                deps=[v_h, v_diag])
            t_conv_prev[par] = t_conv
            s_silu = p.do("scalar", lambda e, mt=mt, par=par: e.activation(
                sy[:, mt * 256:(mt + 1) * 256], pm[par][:, 0:256], AF.Silu,
                bias=scb[:, mt:mt + 1]), deps=[t_conv])
            s_silu_prev[par] = s_silu
            s_silu_last = s_silu

        v_am2 = p.do("vector", lambda e: e.scalar_tensor_tensor(
            sam[:, 0:256], sy[:, 0:256], -1.0, sy[:, 0:256], ALU.mult,
            ALU.max), deps=[s_silu_last])
        for mt in range(1, 32):
            v_am2 = p.do("vector", lambda e, mt=mt: e.scalar_tensor_tensor(
                sfold[:, 0:256], sy[:, mt * 256:(mt + 1) * 256], -1.0,
                sy[:, mt * 256:(mt + 1) * 256], ALU.mult, ALU.max),
                deps=[v_am2])
            v_am2 = p.do("vector", lambda e: e.tensor_tensor(
                sam[:, 0:256], sam[:, 0:256], sfold[:, 0:256], ALU.max),
                deps=[v_am2])
        v_am2 = p.do("vector", lambda e: e.tensor_copy(
            sfold[0:64, 0:256], sam[64:128, 0:256]), deps=[v_am2])
        v_am2 = p.do("vector", lambda e: e.tensor_tensor(
            sam[0:64, 0:256], sam[0:64, 0:256], sfold[0:64, 0:256], ALU.max),
            deps=[v_am2])
        v_am2 = p.do("vector", lambda e: e.tensor_copy(
            sfold[0:32, 0:256], sam[32:64, 0:256]), deps=[v_am2])
        v_am2 = p.do("vector", lambda e: e.tensor_tensor(
            sam[0:32, 0:256], sam[0:32, 0:256], sfold[0:32, 0:256], ALU.max),
            deps=[v_am2])
        v_row2 = None
        for cc in range(2):
            t_t1 = p.do("tensor", lambda e, cc=cc: e.transpose(
                prow2[0:128, 0:32], sam[0:32, cc * 128:(cc + 1) * 128],
                sident[0:32, 0:32]), deps=[v_am2, v_row2])
            v_c1 = p.do("vector", lambda e: e.tensor_reduce(
                sam[0:128, 0:1], prow2[0:128, 0:32], AX.X, ALU.max),
                deps=[t_t1])
            t_t2 = p.do("tensor", lambda e: e.transpose(
                prow2[0:1, 0:128], sam[0:128, 0:1], sident[:]), deps=[v_c1])
            v_row2 = p.do("vector", lambda e, cc=cc: e.tensor_copy(
                samrow[0:1, cc * 128:(cc + 1) * 128], prow2[0:1, 0:128]),
                deps=[t_t2])
        v_r1 = p.do("vector", lambda e: e.tensor_scalar(
            samrow[0:1, 512:512 + 256], samrow[0:1, 0:256], 1e-5, None,
            ALU.max), deps=[v_row2])
        v_r2 = p.do("vector", lambda e: e.reciprocal(
            samrow[0:1, 1024:1024 + 256], samrow[0:1, 512:512 + 256]),
            deps=[v_r1])
        v_r3 = p.do("vector", lambda e: e.tensor_scalar(
            samrow[0:1, 1024:1024 + 256], samrow[0:1, 1024:1024 + 256],
            127.0, None, ALU.mult), deps=[v_r2])
        v_r4 = p.do("vector", lambda e: e.tensor_scalar(
            samrow[0:1, 1536:1536 + 256], samrow[0:1, 512:512 + 256],
            sinv2[0:1, 0:1], 1.0 / 127.0, ALU.mult, ALU.mult), deps=[v_r3])
        t_b1 = p.do("tensor", lambda e: e.matmul(
            pbc1[:, 0:256], sonesf[:], samrow[0:1, 1024:1024 + 256],
            start=True, stop=True), deps=[v_r4, v_xq])
        t_b2 = p.do("tensor", lambda e: e.matmul(
            pbc2[:, 0:256], sonesf[:], samrow[0:1, 1536:1536 + 256],
            start=True, stop=True))
        v_s2B = p.do("vector", lambda e: e.tensor_copy(s2B[:], pbc1[:, 0:256]),
                     deps=[t_b2])
        v_d2B = p.do("vector", lambda e: e.tensor_copy(d2B[:], pbc2[:, 0:256]),
                     deps=[v_s2B])

        v_xq2 = None
        for mt in range(32):
            v_i = p.do("vector", lambda e, mt=mt: e.tensor_tensor(
                swiq2[:], sy[:, mt * 256:(mt + 1) * 256], s2B[:], ALU.mult),
                deps=[v_d2B] if mt == 0 else [v_xq2])
            v_xq2 = p.do("vector", lambda e, mt=mt: e.tensor_copy(
                sxq2[:, mt * 256:(mt + 1) * 256], swiq2[:]), deps=[v_i])

        for et in range(2):
            jj = et % 2
            w2_tok[jj] = p.dma("gpsimd", s8(sw2b[jj], 32),
                               w2qTt[et].rearrange("a p t -> p a t"),
                               lw2[jj], deps=[t_w2_cons[jj]])
        for et in range(8):
            jj = et % 2
            par = et % 2
            t_mm2 = mm_group(
                p, pe_[par][:, 0:256],
                [(sw2b[jj][:, kt * 128:(kt + 1) * 128],
                  sxq2[:, kt * 256:(kt + 1) * 256]) for kt in range(32)],
                deps=[v_xq2, w2_tok[jj], v_pe_cons[par]])
            t_w2_cons[jj] = t_mm2
            if et + 2 < 8:
                w2_tok[jj] = p.dma("gpsimd", s8(sw2b[jj], 32),
                                   w2qTt[et + 2].rearrange("a p t -> p a t"),
                                   lw2[jj], deps=[t_mm2])
            v_o1 = p.do("vector", lambda e, et=et, par=par: e.tensor_tensor(
                sxo[:, et * 256:(et + 1) * 256], pe_[par][:, 0:256], d2B[:],
                ALU.mult), deps=[t_mm2, (la, la.n), st_prev])
            v_xo_last = p.do("vector", lambda e, et=et: e.tensor_tensor(
                sxo[:, et * 256:(et + 1) * 256],
                sxo[:, et * 256:(et + 1) * 256],
                sacb[:, et * 256:(et + 1) * 256], ALU.add), deps=[v_o1])
            v_pe_cons[par] = v_xo_last

        st_prev = p.dma("sync", r8(o_x)[:, :, t0:t0 + 256], s8(sxo, 8), st,
                        deps=[v_xo_last])

    p.wait("sync", (st, st.n))
    return p.finish()



L = S
NCH = L // CH
NTB = 16


def build_l1head(debug=False):
    p = Prog("l1head")
    nc = p.nc
    xn1 = p.din("xn1", [D, L], BF16)
    Wq = p.din("Wq", [D, DK])
    Wk = p.din("Wk", [D, DK])
    Wv = p.din("Wv", [D, DV])
    Wb = p.din("Wb", [D, 1])
    cq = p.din("cq", [DK, 4])
    ck = p.din("ck", [DK, 4])
    cv = p.din("cv", [DV, 4])
    firs = p.din("firs", [DV, 3])
    firl = p.din("firl", [DV, 31])
    negI = p.din("negI", [32, 32])
    mslneg = p.din("mslneg", [32, 512])
    mtriu = p.din("mtriu", [32, 512])
    onesr = p.din("onesr", [1, 128])
    onescl = p.din("onescl", [128, 1])
    identc = p.din("identc", [128, 128])

    o_v = p.dout("vF", [DV, L], BF16)
    o_fs = p.dout("fsF", [DV, L], BF16)
    o_fl = p.dout("flF", [DV, L], BF16)
    o_dF = p.dout("dF", [DV, L], BF16)
    o_stats = p.dout("stats", [6, L])
    kind = "ExternalOutput" if debug else "Internal"
    uTs = nc.dram_tensor("uTs", [L, DV], BF16, kind=kind).ap()
    kTs = nc.dram_tensor("kTs", [L, DK], BF16, kind=kind).ap()
    vTs = nc.dram_tensor("vTs", [L, DV], BF16, kind=kind).ap()
    wTs = nc.dram_tensor("wTs", [NCH // 4, 128, 256], BF16, kind=kind).ap()
    if debug:
        o_TT = p.dout("dbg_TT", [32, NCH * 32], BF16)
        o_ATd = p.dout("dbg_AT", [32, NCH * 32], BF16)
        o_qs = p.dout("dbg_qs", [DK, L], BF16)
        o_ks = p.dout("dbg_ks", [DK, L], BF16)

    PAD = 32
    LP = L + 2 * PAD
    SEG = 1024
    NSEG = L // SEG

    sq = p.sb("sq", [128, 2 * L], BF16)
    sk = p.sb("sk", [128, 2 * L], BF16)
    spre = p.sb("spre", [128, 2 * LP], BF16)
    sWbc = p.sb("sWbc", [128, 8], BF16)
    scc = p.sb("scc", [128, 2 * 4])
    sfir = p.sb("sfir", [128, 2 * 34])
    snegI = p.sb("snegI", [32, 32], BF16)
    smsl = p.sb("smsl", [32, 512], BF16)
    smtr = p.sb("smtr", [32, 512], BF16)
    sonesf = p.sb("sonesf", [1, 128])
    sonesrb = p.sb("sonesrb", [1, 128], BF16)
    sonescol = p.sb("sonescol", [128, 1], BF16)
    sident = p.sb("sident", [128, 128])
    sidentb = p.sb("sidentb", [128, 128], BF16)
    sbeta = p.sb("sbeta", [1, L], BF16)
    srseg = p.sb("srseg", [1, 2 * SEG])
    sstt = p.sb("sstt", [1, 2 * SEG])
    sbc = p.sb("sbc", [128, 512], BF16)
    ssq2 = p.sb("ssq2", [128, 2 * 512], BF16)
    schk = p.sb("schk", [32, 256], BF16)
    schk2 = p.sb("schk2", [32, 256], BF16)
    subat = p.sb("subat", [32, 4 * 256], BF16)
    skbat = p.sb("skbat", [32, 4 * 256], BF16)
    sSb = [p.sb(f"sSb{i}", [128, 256], BF16) for i in range(2)]
    su2 = p.sb("su2", [32, 256], BF16)
    sDel = p.sb("sDel", [32, 256], BF16)

    pA = p.ps("pA", [128, 512])
    pB = p.ps("pB", [128, 512])
    pM = [p.ps("pM0", [32, 512]), p.ps("pM1", [32, 512])]
    ptb = [p.ps("ptb0", [128, 256], BF16), p.ps("ptb1", [128, 256], BF16)]
    pS = [p.ps("pS0", [128, 512]), p.ps("pS1", [128, 512])]

    ld = p.ctr_new("ld")
    lg = p.ctr_new("lg")
    lx = [p.ctr_new(f"lx{i}") for i in range(2)]
    lo = p.ctr_new("lo")
    lu = [p.ctr_new(f"lu{i}") for i in range(2)]
    lk2 = [p.ctr_new(f"lk2{i}") for i in range(2)]
    lv2 = [p.ctr_new(f"lv2{i}") for i in range(2)]
    lc = p.ctr_new("lc")
    lwp = p.ctr_new("lwp")
    lqu = p.ctr_new("lqu")
    lqk = p.ctr_new("lqk")
    lqw = p.ctr_new("lqw")
    lqv = p.ctr_new("lqv")
    lsh = p.ctr_new("lsh")

    p.dma("sync", sonesf[:], onesr, ld)
    p.dma("sync", sident[:], identc, ld)

    p.dma("sync", sfir[:, 0:6].rearrange("p (a t) -> p a t", a=2), r8(firs), ld)
    p.dma("sync", sfir[:, 6:68].rearrange("p (a t) -> p a t", a=2), r8(firl), ld)
    p.dma("gpsimd", smsl[:], mslneg, lg)
    p.dma("gpsimd", smtr[:], mtriu, lg)
    p.dma("gpsimd", sidentb[:], identc, lg)
    p.dma("gpsimd", sonesrb[:], onesr, lg)
    p.dma("gpsimd", sonescol[:], onescl, lg)
    p.dma("gpsimd", snegI[:], negI, lg)
    p.dma("gpsimd", sWbc[:].rearrange("p (a t) -> p a t", a=8), r8(Wb), lg, slow=True)
    lds = [(ld, ld.n), (lg, lg.n)]

    tpp = PP(ptb)

    scope1 = contextlib.ExitStack()
    sv = scope1.enter_context(nc.sbuf_tensor("sv", [128, 2 * LP], BF16))
    sxnb = [scope1.enter_context(nc.sbuf_tensor(f"sxnb{i}", [128, 8 * 512],
                                                BF16)) for i in range(2)]
    sWp = scope1.enter_context(nc.sbuf_tensor("sWp", [128, 8 * 2 * 128],
                                              BF16))

    v_pad = None
    for t_, w_ in ((sv, LP), (spre, LP)):
        for mt in range(2):
            v_pad = p.do("vector", lambda e, t_=t_, w_=w_, mt=mt: e.memset(
                t_[:, mt * w_: mt * w_ + PAD], 0.0), deps=[v_pad])
            v_pad = p.do("vector", lambda e, t_=t_, w_=w_, mt=mt: e.memset(
                t_[:, (mt + 1) * w_ - PAD: (mt + 1) * w_], 0.0), deps=[v_pad])

    ppA = PP([pA, pB])
    xguard = [None, None]
    beta_done = None
    prev_mm_last = None
    prev_silu = None
    for ti, (Wt, ct, dst, off, width) in enumerate((
            (Wq, cq, sq, 0, L), (Wk, ck, sk, 0, L), (Wv, cv, sv, PAD, LP))):
        wtok = p.dma("gpsimd", sWp[:].rearrange("p (a t) -> p a t", a=8),
                     r8(Wt), lwp, deps=[prev_mm_last])
        ctok = p.dma("sync", scc[:].rearrange("p (a t) -> p a t", a=2),
                     r8(ct), lc, deps=[prev_silu])
        v_evl = None
        for nb in range(NTB):
            c0 = nb * 512
            jj = nb % 2
            xtok = p.dma("sync", s8(sxnb[jj], 8), r8(xn1)[:, :, c0:c0 + 512],
                         lx[jj], deps=[xguard[jj]])
            for mt in range(2):
                i2, pkt, freeat = ppA.acq()
                t_mm = mm_group(
                    p, pkt[:],
                    [(sWp[:, kt * 256 + mt * 128: kt * 256 + mt * 128 + 128],
                      sxnb[jj][:, kt * 512:(kt + 1) * 512])
                     for kt in range(8)],
                    deps=[wtok, xtok, freeat] + (lds if (ti == 0 and nb == 0
                                                         and mt == 0) else []))
                v_evl = p.do("vector", lambda e, mt=mt, c0=c0, pkt=pkt:
                             e.tensor_copy(
                    spre[:, mt * LP + PAD + c0: mt * LP + PAD + c0 + 512],
                    pkt[:]), deps=[t_mm] + ([v_pad] if (ti == 0 and nb == 0)
                                            else []))
                ppA.rel(i2, v_evl)
                prev_mm_last = t_mm
            xguard[jj] = prev_mm_last
            if ti == 0:
                t_b = mm_group(p, pM[0][0:1, 0:512],
                               [(sWbc[:, kt:kt + 1],
                                 sxnb[jj][:, kt * 512:(kt + 1) * 512])
                                for kt in range(8)],
                               deps=[(lg, lg.n), beta_done])
                beta_done = p.do("vector", lambda e, c0=c0: e.tensor_copy(
                    sbeta[:, c0:c0 + 512], pM[0][0:1, 0:512]), deps=[t_b])
        s_silu = None
        for mt in range(2):
            po = mt * LP + PAD
            do = mt * width + off
            v_cv = p.do("vector", lambda e, po=po, do=do, dst=dst, mt=mt:
                        e.tensor_scalar(
                dst[:, do: do + L], spre[:, po - 3: po - 3 + L],
                scc[:, mt * 4: mt * 4 + 1], None, ALU.mult),
                deps=[v_evl, ctok, s_silu])
            for k in range(1, 4):
                v_cv = p.do("vector", lambda e, po=po, do=do, k=k, dst=dst,
                            mt=mt: e.scalar_tensor_tensor(
                    dst[:, do: do + L], spre[:, po - 3 + k: po - 3 + k + L],
                    scc[:, mt * 4 + k: mt * 4 + k + 1],
                    dst[:, do: do + L], ALU.mult, ALU.add), deps=[v_cv])
            s_silu = p.do("scalar", lambda e, do=do, dst=dst: e.activation(
                dst[:, do: do + L], dst[:, do: do + L], AF.Silu),
                deps=[v_cv])
        prev_silu = s_silu

    s_qkv = prev_silu
    s_beta = p.do("scalar", lambda e: e.activation(
        sbeta[:], sbeta[:], AF.Sigmoid), deps=[beta_done])

    v_scale = None
    for which, src in ((0, sq), (1, sk)):
        for sg in range(NSEG):
            g0 = sg * SEG
            last = None
            for nb in range(SEG // 512):
                c0 = g0 + nb * 512
                v_sq = None
                for mt in range(2):
                    v_sq = p.do("vector", lambda e, mt=mt, c0=c0, src=src:
                                e.tensor_tensor(
                        ssq2[:, mt * 512:(mt + 1) * 512],
                        src[:, mt * L + c0: mt * L + c0 + 512],
                        src[:, mt * L + c0: mt * L + c0 + 512], ALU.mult),
                        deps=[s_qkv if (which == 0 and sg == 0 and nb == 0)
                              else None, last, v_scale] if mt == 0 else ())
                t_r = mm_group(p, pM[1][0:1, 0:512],
                               [(sonescol[:], ssq2[:, mt * 512:(mt + 1) * 512])
                                for mt in range(2)], deps=[v_sq, last])
                last = p.do("vector", lambda e, nb=nb: e.tensor_copy(
                    srseg[:, nb * 512: nb * 512 + 512], pM[1][0:1, 0:512]),
                    deps=[t_r])
            v_e = p.do("vector", lambda e: e.tensor_scalar(
                srseg[:, 0:SEG], srseg[:, 0:SEG], 1e-6, None, ALU.add),
                deps=[last])
            s_s = p.do("scalar", lambda e: e.activation(
                srseg[:, SEG:2 * SEG], srseg[:, 0:SEG], AF.Sqrt), deps=[v_e])
            v_r = p.do("vector", lambda e: e.reciprocal(
                srseg[:, 0:SEG], srseg[:, SEG:2 * SEG]), deps=[s_s])
            for nb in range(SEG // 512):
                c0 = g0 + nb * 512
                t_bc = p.do("tensor", lambda e, nb=nb: e.matmul(
                    pA[:], sonesf[:], srseg[:, nb * 512: nb * 512 + 512],
                    start=True, stop=True), deps=[v_r, v_scale])
                v_bc = p.do("vector", lambda e: e.tensor_copy(
                    sbc[:, 0:512], pA[:]), deps=[t_bc])
                for mt in range(2):
                    v_scale = p.do("vector", lambda e, mt=mt, c0=c0, src=src:
                                   e.tensor_tensor(
                        src[:, mt * L + c0: mt * L + c0 + 512],
                        src[:, mt * L + c0: mt * L + c0 + 512],
                        sbc[:, 0:512], ALU.mult), deps=[v_bc])
    v_qkscaled = v_scale

    v_store = p.dma("sync", r8(o_v),
                    sv[:].rearrange("p (m t) -> p m t", m=2)[:, :, PAD:PAD + L],
                    lo, deps=[s_qkv], slow=True)
    stat_toks = []
    v_fir_prev = v_qkscaled
    for oo, (o_t, foff, ntap) in enumerate(((o_fs, 0, 3), (o_fl, 6, 31))):
        v_f = None
        for mt in range(2):
            po = mt * LP + PAD
            v_f = p.do("vector", lambda e, po=po, foff=foff, mt=mt,
                       ntap=ntap: e.tensor_scalar(
                spre[:, po: po + L],
                sv[:, po - ntap + 1: po - ntap + 1 + L],
                sfir[:, foff + mt * ntap: foff + mt * ntap + 1], None,
                ALU.mult), deps=[v_fir_prev] if mt == 0 else [v_f])
            for k in range(1, ntap):
                v_f = p.do("vector", lambda e, po=po, foff=foff, k=k,
                           mt=mt, ntap=ntap: e.scalar_tensor_tensor(
                    spre[:, po: po + L],
                    sv[:, po - ntap + 1 + k: po - ntap + 1 + k + L],
                    sfir[:, foff + mt * ntap + k: foff + mt * ntap + k + 1],
                    spre[:, po: po + L], ALU.mult, ALU.add), deps=[v_f])
        st_f = p.dma("sync", r8(o_t),
                     spre[:].rearrange("p (m t) -> p m t", m=2)
                     [:, :, PAD:PAD + L], lo, deps=[v_f], slow=True)
        last_st = None
        for sg in range(NSEG):
            g0 = sg * SEG
            last = None
            for nb in range(SEG // 512):
                c0 = g0 + nb * 512
                v_sq = None
                for mt in range(2):
                    v_sq = p.do("vector", lambda e, mt=mt, c0=c0:
                                e.tensor_tensor(
                        ssq2[:, mt * 512:(mt + 1) * 512],
                        spre[:, mt * LP + PAD + c0: mt * LP + PAD + c0 + 512],
                        spre[:, mt * LP + PAD + c0: mt * LP + PAD + c0 + 512],
                        ALU.mult), deps=[v_f, last] if mt == 0 else ())
                t_r = mm_group(p, pM[1][0:1, 0:512],
                               [(sonescol[:], ssq2[:, mt * 512:(mt + 1) * 512])
                                for mt in range(2)], deps=[v_sq, last])
                last = p.do("vector", lambda e, nb=nb: e.tensor_copy(
                    srseg[:, nb * 512: nb * 512 + 512],
                    pM[1][0:1, 0:512]), deps=[t_r])
                t_r2 = mm_group(p, pM[1][0:1, 0:512],
                                [(sonescol[:],
                                  spre[:, mt * LP + PAD + c0: mt * LP + PAD + c0 + 512])
                                 for mt in range(2)], deps=[last])
                last = p.do("vector", lambda e, nb=nb: e.tensor_copy(
                    sstt[:, nb * 512: nb * 512 + 512], pM[1][0:1, 0:512]),
                    deps=[t_r2])
            v1 = p.do("vector", lambda e: e.tensor_scalar(
                sstt[:, 0:SEG], sstt[:, 0:SEG], 1.0 / DV, None, ALU.mult),
                deps=[last])
            v2 = p.do("vector", lambda e: e.tensor_tensor(
                sstt[:, SEG:2 * SEG], sstt[:, 0:SEG], sstt[:, 0:SEG],
                ALU.mult), deps=[v1])
            v3 = p.do("vector", lambda e: e.scalar_tensor_tensor(
                sstt[:, SEG:2 * SEG], srseg[:, 0:SEG],
                1.0 / DV, sstt[:, SEG:2 * SEG], ALU.mult, ALU.subtract),
                deps=[v2])
            v4 = p.do("vector", lambda e: e.tensor_scalar(
                sstt[:, SEG:2 * SEG], sstt[:, SEG:2 * SEG], 1e-30, None,
                ALU.max), deps=[v3])
            v5 = p.do("scalar", lambda e: e.activation(
                sstt[:, SEG:2 * SEG], sstt[:, SEG:2 * SEG], AF.Sqrt),
                deps=[v4])
            p.dma("sync", o_stats[2 * oo:2 * oo + 1, g0:g0 + SEG],
                  sstt[0:1, 0:SEG], lo, deps=[v5])
            last_st = p.dma("sync", o_stats[2 * oo + 1:2 * oo + 2, g0:g0 + SEG],
                            sstt[0:1, SEG:2 * SEG], lo, deps=[v5])
        stat_toks.append(last_st)
        v_fir_prev = (lo, lo.n)
    v_vb = None
    for sg in range(NSEG):
        g0 = sg * SEG
        for nb in range(SEG // 512):
            c0 = g0 + nb * 512
            t_bc = p.do("tensor", lambda e, c0=c0: e.matmul(
                pA[:], sonesrb[:], sbeta[:, c0:c0 + 512],
                start=True, stop=True),
                deps=[s_beta, v_vb,
                      (lo, lo.n) if sg == 0 and nb == 0 else None,
                      v_fir_prev if sg == 0 and nb == 0 else None])
            v_bc = p.do("vector", lambda e: e.tensor_copy(
                sbc[:, 0:512], pA[:]), deps=[t_bc])
            for mt in range(2):
                v_vb = p.do("vector", lambda e, mt=mt, c0=c0: e.tensor_tensor(
                    sv[:, mt * LP + PAD + c0: mt * LP + PAD + c0 + 512],
                    sv[:, mt * LP + PAD + c0: mt * LP + PAD + c0 + 512],
                    sbc[:, 0:512], ALU.mult), deps=[v_bc])

    lv_prev = None
    v_vt_ev = None
    for c in range(NCH):
        cb = c % 4
        for mt in range(2):
            i2, pt, freeat = tpp.acq()
            t_tr = p.do("tensor", lambda e, c=c, mt=mt, pt=pt: e.transpose(
                pt[0:32, 0:128],
                sv[:, mt * LP + PAD + c * 32: mt * LP + PAD + c * 32 + 32],
                sidentb[:]), deps=[v_vb if c == 0 and mt == 0 else None,
                                   freeat])
            v_vt_ev = p.do("vector", lambda e, mt=mt, cb=cb, pt=pt:
                           e.tensor_copy(
                subat[0:32, cb * 256 + mt * 128: cb * 256 + mt * 128 + 128],
                pt[0:32, 0:128]),
                deps=[t_tr, lv_prev if (cb == 0 and mt == 0) else None])
            tpp.rel(i2, v_vt_ev)
        if cb == 3:
            c4 = c // 4
            lv_prev = p.dma(
                "sync", vTs[c4 * 128:(c4 + 1) * 128, :].rearrange(
                    "(c i) d -> i c d", i=32), s8(subat, 4), lqv,
                deps=[v_vt_ev])
    v_vt_done = lv_prev

    scope1.close()
    scope_late = contextlib.ExitStack()
    sAT = scope_late.enter_context(nc.sbuf_tensor("sAT", [32, NCH * 32], BF16))
    swbat = scope_late.enter_context(nc.sbuf_tensor("swbat", [128, 256], BF16))
    swstr = [scope_late.enter_context(nc.sbuf_tensor(f"swstr{i}", [128, 256],
                                                     BF16)) for i in range(2)]
    scopeT = contextlib.ExitStack()
    sTT = scopeT.enter_context(nc.sbuf_tensor("sTT", [32, NCH * 32], BF16))

    v_kb = None
    for sg in range(NSEG):
        g0 = sg * SEG
        for nb in range(SEG // 512):
            c0 = g0 + nb * 512
            t_bc = p.do("tensor", lambda e, c0=c0: e.matmul(
                pA[:], sonesrb[:], sbeta[:, c0:c0 + 512],
                start=True, stop=True),
                deps=[s_beta, v_kb, (lo, lo.n) if sg == 0 and nb == 0 else None,
                      v_vt_done if sg == 0 and nb == 0 else None])
            v_bc = p.do("vector", lambda e: e.tensor_copy(
                sbc[:, 0:512], pA[:]), deps=[t_bc])
            for mt in range(2):
                v_kb = p.do("vector", lambda e, mt=mt, c0=c0: e.tensor_tensor(
                    spre[:, mt * LP + PAD + c0: mt * LP + PAD + c0 + 512],
                    sk[:, mt * L + c0: mt * L + c0 + 512], sbc[:, 0:512],
                    ALU.mult), deps=[v_bc])
    skb = spre

    scope2 = contextlib.ExitStack()
    sMev = scope2.enter_context(nc.sbuf_tensor("sMev", [32, NCH * 32], BF16))
    sTch = [scope2.enter_context(nc.sbuf_tensor(f"sTch{i}", [128, 1024], F32))
            for i in range(2)]
    sTchb = scope2.enter_context(nc.sbuf_tensor("sTchb", [128, 1024], BF16))

    v_mev = None
    for bat in range(NCH // 16):
        par = bat % 2
        t_mev_prev = None
        for ci in range(16):
            c = bat * 16 + ci
            t_mev_prev = mm_group(
                p, pM[par][0:32, ci * 32:(ci + 1) * 32],
                [(skb[:, mt * LP + PAD + c * 32: mt * LP + PAD + c * 32 + 32],
                  sk[:, mt * L + c * 32: mt * L + c * 32 + 32])
                 for mt in range(2)],
                deps=[v_kb, v_mev] if ci == 0 else ())
        v_mev = p.do("vector", lambda e, bat=bat, par=par: e.tensor_tensor(
            sMev[:, bat * 512:(bat + 1) * 512], pM[par][0:32, 0:512],
            smsl[:], ALU.mult), deps=[t_mev_prev])
    for i in range(32):
        for half in range(2):
            p.dma("gpsimd",
                  sTch[half][:, i * 32:(i + 1) * 32],
                  sMev[i:i + 1, :].rearrange("p (c j) -> p c j", j=32)
                  [:, half * 128:(half + 1) * 128, :],
                  lsh, deps=[v_mev] if (i == 0 and half == 0) else ())
    v_t = None
    for half in range(2):
        for i in range(1, 32):
            for jjj in range(i):
                v_t = p.do("vector", lambda e, half=half, i=i, jjj=jjj:
                           e.scalar_tensor_tensor(
                    sTch[half][:, i * 32:(i + 1) * 32],
                    sTch[half][:, jjj * 32:(jjj + 1) * 32],
                    sTch[half][:, i * 32 + jjj: i * 32 + jjj + 1],
                    sTch[half][:, i * 32:(i + 1) * 32], ALU.mult, ALU.add),
                    deps=[v_t] if v_t is not None else [(lsh, lsh.n)])
    for half in range(2):
        v_t = p.do("vector", lambda e, half=half: e.tensor_scalar(
            sTch[half][:, 0:1024:33], sTch[half][:, 0:1024:33], 1.0, None,
            ALU.add), deps=[v_t])
    v_tt = None
    v_cast = None
    for half in range(2):
        v_cast = p.do("vector", lambda e, half=half: e.tensor_copy(
            sTchb[:], sTch[half][:]), deps=[v_t, v_tt])
        for i in range(32):
            ii, pt, freeat = tpp.acq()
            t_tr = p.do("tensor", lambda e, i=i, pt=pt: e.transpose(
                pt[0:32, 0:128], sTchb[:, i * 32:(i + 1) * 32], sidentb[:]),
                deps=[v_cast, freeat])
            v_tt = p.do("vector", lambda e, i=i, half=half, pt=pt:
                        e.tensor_copy(
                sTT[0:32, half * 4096 + i: half * 4096 + i + 4065: 32],
                pt[0:32, 0:128]), deps=[t_tr])
            tpp.rel(ii, v_tt)

    scope2.close()
    scope3 = contextlib.ExitStack()
    svstr = [scope3.enter_context(nc.sbuf_tensor(f"svstr{i}", [32, 4 * 256],
                                                 BF16)) for i in range(2)]

    vstr_tok = [None, None]
    v_u_ev = None
    v_w_ev = None
    v_at_ev = None
    lq_u_prev = None
    lq_k_prev = None
    lq_w_prev = None
    for g in range(2):
        vstr_tok[g] = p.dma("sync", s8(svstr[g], 4),
                            vTs[g * 128:(g + 1) * 128, :].rearrange(
                                "(c i) d -> i c d", i=32), lv2[g],
                            deps=[v_vt_done, v_tt] if g == 0 else [v_vt_done])
    for c in range(NCH):
        g = c // 4
        j = g % 2
        cb = c % 4
        t_u = mm_group(p, pM[0][0:32, 0:256],
                       [(sTT[0:32, c * 32:(c + 1) * 32],
                         svstr[j][0:32, cb * 256:(cb + 1) * 256])],
                       deps=[vstr_tok[j], v_tt if c == 0 else None,
                             v_u_ev])
        v_u_ev = p.do("vector", lambda e, cb=cb: e.tensor_copy(
            subat[0:32, cb * 256:(cb + 1) * 256], pM[0][0:32, 0:256]),
            deps=[t_u, lq_u_prev if cb == 0 else None, v_vt_done if c == 0 else None])
        if cb == 3:
            c4 = c // 4
            lq_u_prev = p.dma(
                "sync", uTs[c4 * 128:(c4 + 1) * 128, :].rearrange(
                    "(c i) d -> i c d", i=32), s8(subat, 4), lqu,
                deps=[v_u_ev])
            if g + 2 < NCH // 4:
                vstr_tok[j] = p.dma("sync", s8(svstr[j], 4),
                                    vTs[(g + 2) * 128:(g + 3) * 128, :]
                                    .rearrange("(c i) d -> i c d", i=32),
                                    lv2[j], deps=[t_u])
        v_ch2 = None
        for mt in range(2):
            i2, pt, freeat = tpp.acq()
            t_tr = p.do("tensor", lambda e, c=c, mt=mt, pt=pt: e.transpose(
                pt[0:32, 0:128],
                skb[:, mt * LP + PAD + c * 32: mt * LP + PAD + c * 32 + 32],
                sidentb[:]), deps=[freeat])
            v_ch2 = p.do("vector", lambda e, mt=mt, pt=pt: e.tensor_copy(
                schk2[0:32, mt * 128:(mt + 1) * 128], pt[0:32, 0:128]),
                deps=[t_tr, v_w_ev if mt == 0 else None])
            tpp.rel(i2, v_ch2)
        for mt in range(2):
            t_w = mm_group(p, pA[0:128, mt * 32:(mt + 1) * 32],
                           [(schk2[0:32, mt * 128:(mt + 1) * 128],
                             sTT[0:32, c * 32:(c + 1) * 32])],
                           deps=[v_ch2, v_w_ev])
            v_w_ev = p.do("vector", lambda e, cb=cb, mt=mt: e.tensor_copy(
                swbat[:, cb * 64 + mt * 32: cb * 64 + mt * 32 + 32],
                pA[0:128, mt * 32:(mt + 1) * 32]),
                deps=[t_w, lq_w_prev if (cb == 0 and mt == 0) else None])
        if cb == 3:
            lq_w_prev = p.dma("sync", wTs[c // 4], swbat[:], lqw,
                              deps=[v_w_ev])
        v_k2 = None
        for mt in range(2):
            i2, pt, freeat = tpp.acq()
            t_tr = p.do("tensor", lambda e, c=c, mt=mt, pt=pt: e.transpose(
                pt[0:32, 0:128], sk[:, mt * L + c * 32: mt * L + c * 32 + 32],
                sidentb[:]), deps=[freeat])
            v_k2 = p.do("vector", lambda e, mt=mt, cb=cb, pt=pt: e.tensor_copy(
                skbat[0:32, cb * 256 + mt * 128: cb * 256 + mt * 128 + 128],
                pt[0:32, 0:128]),
                deps=[t_tr, lq_k_prev if (cb == 0 and mt == 0) else None])
            tpp.rel(i2, v_k2)
        if cb == 3:
            c4 = c // 4
            lq_k_prev = p.dma(
                "sync", kTs[c4 * 128:(c4 + 1) * 128, :].rearrange(
                    "(c i) d -> i c d", i=32), s8(skbat, 4), lqk,
                deps=[v_k2])
        t_at = mm_group(p, pM[1][0:32, (c % 16) * 32:(c % 16) * 32 + 32],
                        [(sk[:, mt * L + c * 32: mt * L + c * 32 + 32],
                          sq[:, mt * L + c * 32: mt * L + c * 32 + 32])
                         for mt in range(2)],
                        deps=[v_at_ev if c % 16 == 0 else None,
                              v_qkscaled if c == 0 else None])
        if c % 16 == 15:
            v_at_ev = p.do("vector", lambda e, c=c: e.tensor_tensor(
                sAT[0:32, (c - 15) * 32:(c + 1) * 32], pM[1][0:32, 0:512],
                smtr[:], ALU.mult), deps=[t_at])

    if debug:
        p.dma("sync", o_TT, sTT[:], lo, deps=[v_tt])
        p.dma("sync", o_ATd, sAT[:], lo, deps=[v_at_ev])
        p.dma("sync", r8(o_qs), s8(sq, 2), lo, deps=[v_qkscaled])
        p.dma("sync", r8(o_ks), s8(sk, 2), lo, deps=[v_qkscaled])
    scope3.close()
    scopeT.close()
    scope4 = contextlib.ExitStack()
    sustr = [scope4.enter_context(nc.sbuf_tensor(f"sustr{i}", [32, 4 * 256],
                                                 BF16)) for i in range(2)]
    skstr = [scope4.enter_context(nc.sbuf_tensor(f"skstr{i}", [32, 4 * 256],
                                                 BF16)) for i in range(2)]
    v_s0 = p.do("vector", lambda e: e.memset(sSb[0][:], 0.0), deps=[v_at_ev])
    v_s0 = p.do("vector", lambda e: e.memset(sSb[1][:], 0.0), deps=[v_s0])
    ustr_tok = [None, None]
    kstr_tok = [None, None]
    wstr_tok = [None, None]
    lw2 = [p.ctr_new(f"lw2{i}") for i in range(2)]
    for g in range(2):
        ustr_tok[g] = p.dma("sync", s8(sustr[g], 4),
                            uTs[g * 128:(g + 1) * 128, :].rearrange(
                                "(c i) d -> i c d", i=32), lu[g],
                            deps=[lq_u_prev])
        kstr_tok[g] = p.dma("sync", s8(skstr[g], 4),
                            kTs[g * 128:(g + 1) * 128, :].rearrange(
                                "(c i) d -> i c d", i=32), lk2[g],
                            deps=[lq_k_prev])
        wstr_tok[g] = p.dma("sync", swstr[g][:], wTs[g], lw2[g],
                            deps=[lq_w_prev])
    v_sb_ref = [v_s0, v_s0]
    v_u2 = None
    v_del_ev = None
    sDF = spre
    for c in range(NCH):
        g = c // 4
        j = g % 2
        cb = c % 4
        pairs = [(snegI[:], sustr[j][0:32, cb * 256:(cb + 1) * 256])]
        pairs += [(swstr[j][:, cb * 64 + mt * 32: cb * 64 + mt * 32 + 32],
                   sSb[mt][:]) for mt in range(2)]
        t_p1 = mm_group(p, pM[0][0:32, 0:256], pairs,
                        deps=[ustr_tok[j], wstr_tok[j], v_sb_ref[0],
                              v_sb_ref[1], v_u2])
        v_u2 = p.do("vector", lambda e: e.tensor_scalar(
            su2[:], pM[0][0:32, 0:256], -1.0, None, ALU.mult), deps=[t_p1])
        pairs = [(sq[:, mt * L + c * 32: mt * L + c * 32 + 32], sSb[mt][:])
                 for mt in range(2)]
        pairs += [(sAT[0:32, c * 32:(c + 1) * 32], su2[:])]
        t_p2 = mm_group(p, pM[1][0:32, 0:256], pairs,
                        deps=[v_u2, v_del_ev])
        t_ds = None
        for mt in range(2):
            t_ds = p.do("tensor", lambda e, mt=mt, j=j, cb=cb, c=c: e.matmul(
                pS[mt][:, 0:256],
                skstr[j][0:32, cb * 256 + mt * 128: cb * 256 + mt * 128 + 128],
                su2[:], start=(c == 0), stop=(c == NCH - 1),
                skip_group_check=True),
                deps=[kstr_tok[j]] if mt == 0 else ())
        if cb == 3 and g + 2 < NCH // 4:
            ustr_tok[j] = p.dma("sync", s8(sustr[j], 4),
                                uTs[(g + 2) * 128:(g + 3) * 128, :].rearrange(
                                    "(c i) d -> i c d", i=32), lu[j],
                                deps=[t_p1])
            kstr_tok[j] = p.dma("sync", s8(skstr[j], 4),
                                kTs[(g + 2) * 128:(g + 3) * 128, :].rearrange(
                                    "(c i) d -> i c d", i=32), lk2[j],
                                deps=[t_ds])
            wstr_tok[j] = p.dma("sync", swstr[j][:], wTs[g + 2], lw2[j],
                                deps=[t_p1])
        v_sb_ref = [
            p.do("vector", lambda e: e.tensor_copy(sSb[0][:], pS[0][:, 0:256]),
                 deps=[t_ds]),
            p.do("vector", lambda e: e.tensor_copy(sSb[1][:], pS[1][:, 0:256]),
                 deps=[t_ds]),
        ]
        v_del = p.do("vector", lambda e: e.tensor_copy(
            sDel[:], pM[1][0:32, 0:256]),
            deps=[t_p2, v_kb if c == 0 else None])
        v_del_ev = v_del
        for mt in range(2):
            i2, pt, freeat = tpp.acq()
            t_tr = p.do("tensor", lambda e, mt=mt, pt=pt: e.transpose(
                pt[0:128, 0:32], sDel[0:32, mt * 128:(mt + 1) * 128],
                sidentb[0:32, 0:32]), deps=[v_del, freeat])
            v_df = p.do("vector", lambda e, c=c, mt=mt, pt=pt: e.tensor_copy(
                sDF[:, mt * LP + PAD + c * 32: mt * LP + PAD + c * 32 + 32],
                pt[0:128, 0:32]), deps=[t_tr])
            tpp.rel(i2, v_df)
            v_del_ev = v_df
    v_scan_done = v_del_ev

    st_df = p.dma("sync", r8(o_dF),
                  sDF[:].rearrange("p (m t) -> p m t", m=2)[:, :, PAD:PAD + L],
                  lo, deps=[v_scan_done], slow=True)
    last_st = None
    for sg in range(NSEG):
        g0 = sg * SEG
        last = None
        for nb in range(SEG // 512):
            c0 = g0 + nb * 512
            v_sq = None
            for mt in range(2):
                v_sq = p.do("vector", lambda e, mt=mt, c0=c0: e.tensor_tensor(
                    ssq2[:, mt * 512:(mt + 1) * 512],
                    sDF[:, mt * LP + PAD + c0: mt * LP + PAD + c0 + 512],
                    sDF[:, mt * LP + PAD + c0: mt * LP + PAD + c0 + 512],
                    ALU.mult), deps=[v_scan_done, last] if mt == 0 else ())
            t_r = mm_group(p, pM[1][0:1, 0:512],
                           [(sonescol[:], ssq2[:, mt * 512:(mt + 1) * 512])
                            for mt in range(2)], deps=[v_sq, last])
            last = p.do("vector", lambda e, nb=nb: e.tensor_copy(
                srseg[:, nb * 512: nb * 512 + 512],
                pM[1][0:1, 0:512]), deps=[t_r])
            t_r2 = mm_group(p, pM[1][0:1, 0:512],
                            [(sonescol[:],
                              sDF[:, mt * LP + PAD + c0: mt * LP + PAD + c0 + 512])
                             for mt in range(2)], deps=[last])
            last = p.do("vector", lambda e, nb=nb: e.tensor_copy(
                sstt[:, nb * 512: nb * 512 + 512], pM[1][0:1, 0:512]),
                deps=[t_r2])
        v1 = p.do("vector", lambda e: e.tensor_scalar(
            sstt[:, 0:SEG], sstt[:, 0:SEG], 1.0 / DV, None, ALU.mult),
            deps=[last])
        v2 = p.do("vector", lambda e: e.tensor_tensor(
            sstt[:, SEG:2 * SEG], sstt[:, 0:SEG], sstt[:, 0:SEG], ALU.mult),
            deps=[v1])
        v3 = p.do("vector", lambda e: e.scalar_tensor_tensor(
            sstt[:, SEG:2 * SEG], srseg[:, 0:SEG], 1.0 / DV,
            sstt[:, SEG:2 * SEG], ALU.mult, ALU.subtract), deps=[v2])
        v4 = p.do("vector", lambda e: e.tensor_scalar(
            sstt[:, SEG:2 * SEG], sstt[:, SEG:2 * SEG], 1e-30, None,
            ALU.max), deps=[v3])
        v5 = p.do("scalar", lambda e: e.activation(
            sstt[:, SEG:2 * SEG], sstt[:, SEG:2 * SEG], AF.Sqrt), deps=[v4])
        p.dma("sync", o_stats[4:5, g0:g0 + SEG], sstt[0:1, 0:SEG], lo,
              deps=[v5])
        last_st = p.dma("sync", o_stats[5:6, g0:g0 + SEG],
                        sstt[0:1, SEG:2 * SEG], lo, deps=[v5])
    p.wait("sync", (lo, lo.n), (lqu, lqu.n), (lqk, lqk.n), (lqw, lqw.n),
           (lqv, lqv.n))
    scope4.close()
    scope_late.close()
    return p.finish()



def build_router():
    p = Prog("router")
    xn1 = p.din("xn1", [D, TPC], BF16)
    st24 = p.din("st24", [24, TPC])
    Wr1 = p.din("Wr1", [1048, 2048])
    br1c = p.din("br1c", [128, 16])
    Wr2p = p.din("Wr2p", [2048, 12])
    br2p = p.din("br2p", [4, 3])
    ltg4 = p.din("ltg4", [4, 1])
    Wid = p.din("Wid", [D, 4])
    bid4 = p.din("bid4", [4, 1])
    alpha4 = p.din("alpha4", [4, 1])

    o_p = [p.dout(f"p{e}", [4, TPC]) for e in range(3)]
    o_ids = p.dout("ids", [4, TPC])

    R_EPS = 0.025

    sxn = p.sb("sxn", [128, 8 * TPC], BF16)
    sstb = p.sb("sstb", [24, TPC], BF16)
    sWr1 = p.sb("sWr1", [128, 8 * 2048], BF16)
    sWr1s = p.sb("sWr1s", [24, 2048], BF16)
    sbr1 = p.sb("sbr1", [128, 16])
    sW2 = p.sb("sW2", [128, 16 * 12], BF16)
    sbr2 = p.sb("sbr2", [4, 3])
    sltg = p.sb("sltg", [4, 1])
    sitau = p.sb("sitau", [4, 1])
    sWid = p.sb("sWid", [128, 8 * 4], BF16)
    sbid = p.sb("sbid", [4, 1])
    salpha = p.sb("salpha", [4, 1])
    shm = p.sb("shm", [128, 16 * TPC], BF16)
    slog = [p.sb(f"slog{e}", [4, TPC]) for e in range(3)]
    sexp = slog
    ssum = p.sb("ssump", [4, TPC])
    sids = p.sb("sids", [4, TPC])

    pk = [p.ps("pk0", [128, 512]), p.ps("pk1", [128, 512])]
    pe2 = [p.ps("pe20", [4, 512]), p.ps("pe21", [4, 512])]

    ld = p.ctr_new("ld")
    lg = p.ctr_new("lg")
    st = p.ctr_new("st")

    p.dma("sync", s8(sxn, 8), r8(xn1), ld)

    p.dma("sync", sbr1[:], br1c, ld)
    p.dma("sync", sbr2[:], br2p, ld)
    p.dma("sync", sltg[:], ltg4, ld)
    p.dma("sync", sbid[:], bid4, ld)
    p.dma("sync", salpha[:], alpha4, ld)
    p.dma("gpsimd", sstb[:], st24, lg)
    p.dma("gpsimd", s8(sWr1, 8), r8(Wr1[0:1024, :]), lg)
    p.dma("gpsimd", sWr1s[:], Wr1[1024:1048, :], lg)
    p.dma("gpsimd", sW2[:].rearrange("p (a t) -> p a t", a=16), r8(Wr2p), lg)
    p.dma("gpsimd", sWid[:].rearrange("p (a t) -> p a t", a=8), r8(Wid), lg)
    lds = [(ld, ld.n), (lg, lg.n)]

    v_stb = None

    ppA = PP(pk)
    s_hm = None
    for mt in range(16):
        for nb in range(4):
            c0 = nb * 512
            i2, pkt, freeat = ppA.acq()
            pairs = [(sWr1[:, kt * 2048 + mt * 128: kt * 2048 + mt * 128 + 128],
                      sxn[:, kt * TPC + c0: kt * TPC + c0 + 512])
                     for kt in range(8)]
            pairs.append((sWr1s[0:24, mt * 128:(mt + 1) * 128],
                          sstb[0:24, c0:c0 + 512]))
            t_mm = mm_group(p, pkt[:], pairs,
                            deps=[v_stb, freeat] + (lds if mt == 0 and nb == 0
                                                    else []))
            s_hm = p.do("scalar", lambda e, mt=mt, c0=c0, pkt=pkt:
                        e.activation(
                shm[:, mt * TPC + c0: mt * TPC + c0 + 512], pkt[:], AF.Gelu,
                bias=sbr1[:, mt:mt + 1]), deps=[t_mm])
            ppA.rel(i2, s_hm)

    ppB = PP(pe2)
    v_log = None
    for e_ in range(3):
        for nb in range(4):
            c0 = nb * 512
            i2, pet, freeat = ppB.acq()
            t_mm = mm_group(
                p, pet[:],
                [(sW2[:, kt * 12 + e_ * 4: kt * 12 + e_ * 4 + 4],
                  shm[:, kt * TPC + c0: kt * TPC + c0 + 512])
                 for kt in range(16)], deps=[s_hm, freeat])
            v_log = p.do("vector", lambda e, e_=e_, c0=c0, pet=pet:
                         e.tensor_scalar(
                slog[e_][:, c0:c0 + 512], pet[:], sbr2[:, e_:e_ + 1], None,
                ALU.add), deps=[t_mm])
            ppB.rel(i2, v_log)

    s_tau = p.do("scalar", lambda e: e.activation(sitau[:], sltg[:], AF.Exp),
                 deps=lds)
    v_itau = p.do("vector", lambda e: e.reciprocal(sitau[:], sitau[:]),
                  deps=[s_tau])
    s_ex = None
    for e_ in range(3):
        v_sc = p.do("vector", lambda e, e_=e_: e.tensor_scalar(
            slog[e_][:], slog[e_][:], sitau[:], None, ALU.mult),
            deps=[v_log, v_itau] if e_ == 0 else ())
        s_ex = p.do("scalar", lambda e, e_=e_: e.activation(
            sexp[e_][:], slog[e_][:], AF.Exp), deps=[v_sc])
    v_sm = p.do("vector", lambda e: e.tensor_tensor(
        ssum[:], sexp[0][:], sexp[1][:], ALU.add), deps=[s_ex])
    v_sm = p.do("vector", lambda e: e.tensor_tensor(
        ssum[:], ssum[:], sexp[2][:], ALU.add), deps=[v_sm])
    v_sm = p.do("vector", lambda e: e.reciprocal(ssum[:], ssum[:]),
                deps=[v_sm])
    v_pe = None
    for e_ in range(3):
        v_p = p.do("vector", lambda e, e_=e_: e.tensor_tensor(
            sexp[e_][:], sexp[e_][:], ssum[:], ALU.mult), deps=[v_sm])
        v_pe = p.do("vector", lambda e, e_=e_: e.tensor_scalar(
            sexp[e_][:], sexp[e_][:], 1.0 - 3.0 * R_EPS, R_EPS, ALU.mult,
            ALU.add), deps=[v_p])
        p.dma("sync", o_p[e_], sexp[e_][:], st, deps=[v_pe])

    v_id = None
    for nb in range(4):
        c0 = nb * 512
        i2, pet, freeat = ppB.acq()
        t_mm = mm_group(p, pet[:],
                        [(sWid[:, kt * 4:(kt + 1) * 4],
                          sxn[:, kt * TPC + c0: kt * TPC + c0 + 512])
                         for kt in range(8)], deps=[freeat])
        v_id = p.do("vector", lambda e, c0=c0, pet=pet: e.tensor_scalar(
            sids[:, c0:c0 + 512], pet[:], sbid[:], None, ALU.add),
            deps=[t_mm])
        ppB.rel(i2, v_id)
    s_sig = p.do("scalar", lambda e: e.activation(
        sids[:], sids[:], AF.Sigmoid), deps=[v_id])
    s_sa = p.do("scalar", lambda e: e.activation(
        salpha[:], salpha[:], AF.Sigmoid), deps=[s_sig])
    v_ids = p.do("vector", lambda e: e.tensor_scalar(
        sids[:], sids[:], salpha[:], 0.06, ALU.mult, ALU.add), deps=[s_sa])
    p.dma("sync", o_ids, sids[:], st, deps=[v_ids])
    p.wait("sync", (st, st.n))
    return p.finish()



def build_mix():
    p = Prog("mix")
    vF = p.din("vF", [DV, L], BF16)
    fsF = p.din("fsF", [DV, L], BF16)
    flF = p.din("flF", [DV, L], BF16)
    dF = p.din("dF", [DV, L], BF16)
    p4 = p.din("p4", [4, L])
    onwc = p.din("onwc", [128, 2])
    onesr = p.din("onesr", [1, 128])
    onescl = p.din("onescl", [128, 1])

    o_oN = p.dout("oN", [DV, L], BF16)

    sV = p.sb("sV", [128, 2 * L], BF16)
    sFs = p.sb("sFs", [128, 2 * L], BF16)
    sFl = p.sb("sFl", [128, 2 * L], BF16)
    sDl = p.sb("sDl", [128, 2 * L], BF16)
    sON = p.sb("sON", [128, 2 * 512], BF16)
    sp3 = p.sb("sp3", [1, 4 * 512], BF16)
    sonw = p.sb("sonw", [128, 2])
    sonesrb = p.sb("sonesrb", [1, 128], BF16)
    sonesf2 = p.sb("sonesf2", [1, 128])
    sonescol = p.sb("sonescol", [128, 1], BF16)
    sbcs = [p.sb(f"sbcs{i}", [128, 512], BF16) for i in range(4)]
    so = p.sb("so", [128, 2 * 512])
    sosq = p.sb("sosq", [128, 2 * 512], BF16)
    srow = p.sb("srowm", [1, 2 * 512])
    srstdb = p.sb("srstdb", [128, 512], BF16)

    pb = [p.ps("pb0", [128, 512]), p.ps("pb1", [128, 512])]
    pr = p.ps("prm", [1, 512])

    ld = p.ctr_new("ld")
    lg = p.ctr_new("lg")
    st = p.ctr_new("st")

    p.dma("sync", s8(sV, 2), r8(vF), ld)
    p.dma("sync", s8(sFs, 2), r8(fsF), ld)
    p.dma("sync", s8(sFl, 2), r8(flF), ld)
    p.dma("sync", s8(sDl, 2), r8(dF), ld)
    p.dma("sync", sonw[:], onwc, ld)
    p.dma("sync", sonesf2[:], onesr, ld)
    p.dma("gpsimd", sonesrb[:], onesr, lg)
    p.dma("gpsimd", sonescol[:], onescl, lg)
    lds = [(ld, ld.n), (lg, lg.n)]

    pbp = PP(pb)
    v_on = None
    st_on = None
    v_bc_all = None
    lp = p.ctr_new("lp")
    for nb in range(L // 512):
        c0 = nb * 512
        ptok = p.dma("gpsimd", sp3[:].rearrange("p (a t) -> p a t", a=4),
                     p4[:, c0:c0 + 512], lp, deps=[v_bc_all])
        v_bc = None
        for bi in range(4):
            i2, pbt, freeat = pbp.acq()
            src = sp3[0:1, bi * 512: bi * 512 + 512]
            t_bc = p.do("tensor", lambda e, src=src, pbt=pbt: e.matmul(
                pbt[:], sonesrb[:], src, start=True, stop=True),
                deps=[freeat, ptok] + (lds if nb == 0 and bi == 0 else []))
            v_bc = p.do("vector", lambda e, bi=bi, pbt=pbt: e.tensor_copy(
                sbcs[bi][:], pbt[:]), deps=[t_bc])
            pbp.rel(i2, v_bc)
        v_bc_all = ("tensor", p.ctr["tensor"].n)
        v_mix = None
        for mt in range(2):
            o_sl = so[:, mt * 512:(mt + 1) * 512]
            t_sl = sosq[:, mt * 512:(mt + 1) * 512]
            v_mix = p.do("vector", lambda e, mt=mt, c0=c0, o_sl=o_sl: e.tensor_tensor(
                o_sl, sFs[:, mt * L + c0: mt * L + c0 + 512], sbcs[0][:],
                ALU.mult), deps=[v_bc, v_on])
            for bi, ten in ((1, sFl), (2, sDl), (3, sV)):
                v_mix = p.do("vector", lambda e, mt=mt, c0=c0, bi=bi, ten=ten,
                             t_sl=t_sl: e.tensor_tensor(
                    t_sl, ten[:, mt * L + c0: mt * L + c0 + 512],
                    sbcs[bi][:], ALU.mult), deps=[v_mix])
                v_mix = p.do("vector", lambda e, o_sl=o_sl, t_sl=t_sl:
                             e.tensor_tensor(o_sl, o_sl, t_sl, ALU.add),
                             deps=[v_mix])
        v_sq = None
        for mt in range(2):
            v_sq = p.do("vector", lambda e, mt=mt: e.tensor_tensor(
                sosq[:, mt * 512:(mt + 1) * 512],
                so[:, mt * 512:(mt + 1) * 512],
                so[:, mt * 512:(mt + 1) * 512], ALU.mult), deps=[v_mix])
        t_r = mm_group(p, pr[:], [(sonescol[:], sosq[:, mt * 512:(mt + 1) * 512])
                                  for mt in range(2)], deps=[v_sq])
        v_r1 = p.do("vector", lambda e: e.tensor_scalar(
            srow[:, 0:512], pr[:], 1.0 / DV, 1e-5, ALU.mult, ALU.add),
            deps=[t_r])
        s_r = p.do("scalar", lambda e: e.activation(
            srow[:, 512:1024], srow[:, 0:512], AF.Sqrt), deps=[v_r1])
        v_r2 = p.do("vector", lambda e: e.reciprocal(
            srow[:, 0:512], srow[:, 512:1024]), deps=[s_r])
        i2, pbt, freeat = pbp.acq()
        t_bc = p.do("tensor", lambda e, pbt=pbt: e.matmul(
            pbt[:], sonesf2[:], srow[:, 0:512], start=True, stop=True),
            deps=[v_r2, freeat])
        v_rb = p.do("vector", lambda e, pbt=pbt: e.tensor_copy(
            srstdb[:], pbt[:]), deps=[t_bc])
        pbp.rel(i2, v_rb)
        for mt in range(2):
            v_on = p.do("vector", lambda e, mt=mt: e.scalar_tensor_tensor(
                sON[:, mt * 512:(mt + 1) * 512],
                so[:, mt * 512:(mt + 1) * 512], sonw[:, mt:mt + 1],
                srstdb[:], ALU.mult, ALU.mult),
                deps=[v_rb, st_on if mt == 0 else None])
        st_on = p.dma("sync", r8(o_oN)[:, :, c0:c0 + 512], s8(sON, 2), st,
                      deps=[v_on])
    p.wait("sync", (st, st.n))
    return p.finish()



def build_l1out():
    p = Prog("l1out")
    oN = p.din("oN", [D, TPC], BF16)
    Wo = p.din("Wo", [D, D])
    x1F = p.din("x1F", [D, TPC])
    o_acc = p.dout("acc", [D, TPC])

    soN = p.sb("soN", [128, 8 * TPC], BF16)
    sWo = p.sb("sWo", [128, 8 * D], BF16)
    sx1 = p.sb("sx1", [128, 8 * TPC])
    sacc = p.sb("sacc2", [128, 8 * TPC])

    pk = [p.ps("pk0", [128, 512]), p.ps("pk1", [128, 512])]

    ld = p.ctr_new("ld")
    lg = p.ctr_new("lg")
    st = p.ctr_new("st")

    p.dma("sync", s8(soN, 8), r8(oN), ld)
    p.dma("sync", s8(sx1, 8), r8(x1F), ld)
    p.dma("gpsimd", s8(sWo, 8), r8(Wo), lg)
    lds = [(ld, ld.n), (lg, lg.n)]

    pp = PP(pk)
    v_last = None
    for nb in range(4):
        c0 = nb * 512
        for mt in range(8):
            i2, pkt, freeat = pp.acq()
            t_mm = mm_group(
                p, pkt[:],
                [(sWo[:, kt * D + mt * 128: kt * D + mt * 128 + 128],
                  soN[:, kt * TPC + c0: kt * TPC + c0 + 512])
                 for kt in range(8)],
                deps=[freeat] + (lds if nb == 0 and mt == 0 else []))
            v_last = p.do("vector", lambda e, mt=mt, c0=c0, pkt=pkt:
                          e.tensor_tensor(
                sacc[:, mt * TPC + c0: mt * TPC + c0 + 512], pkt[:],
                sx1[:, mt * TPC + c0: mt * TPC + c0 + 512], ALU.add),
                deps=[t_mm])
            pp.rel(i2, v_last)
    p.dma("sync", r8(o_acc), s8(sacc, 8), st, deps=[v_last])
    p.wait("sync", (st, st.n))
    return p.finish()



def _consts():
    return {
        "onesr": np.ones((1, 128), np.float32),
        "onescl": np.ones((128, 1), np.float32),
        "identc": np.eye(128, dtype=np.float32),
        "ones128": np.ones((128, 128), np.float32),
    }


def _get_nc(name, builder):
    if name not in _nc_cache:
        _nc_cache[name] = builder()
    return _nc_cache[name]


def _run(name, builder, in_maps):
    nc = _get_nc(name, builder)
    res = run_bass_kernel_spmd(nc, in_maps, core_ids=list(range(NCORE)))
    return res.results


def _bias_tile(b, ntiles=8):
    return np.ascontiguousarray(
        np.asarray(b).reshape(ntiles, 128).T.astype(np.float32))


def run_l0(inputs):
    C = _consts()
    x = np.asarray(inputs["L0_attn_x"])
    W1 = np.asarray(inputs["L0_attn_W1"], np.float32)
    W2 = np.asarray(inputs["L0_attn_W2"], np.float32)
    W3 = np.asarray(inputs["L0_attn_W3"], np.float32)
    W4 = np.asarray(inputs["L0_attn_W4"], np.float32)
    b1c = _bias_tile(inputs["L0_attn_b1"])
    b2c = _bias_tile(inputs["L0_attn_b2"])
    b3c = _bias_tile(inputs["L0_attn_b3"])
    b4c = _bias_tile(inputs["L0_attn_b4"])

    xFs = []
    for c in range(NCORE):
        bc, sc = c // 4, c % 4
        xFs.append(np.ascontiguousarray(
            x[bc, sc * TPC:(sc + 1) * TPC, :].T.astype(np.float32)))

    in_maps = [{"xF": xFs[c], "onesr": C["onesr"], "onescl": C["onescl"],
                "identc": C["identc"]} for c in range(NCORE)]
    rn = _run("l0norm", build_l0norm, in_maps)
    in_maps = [{"xnF": rn[c]["xnF"], "W1": W1, "W2": W2, "b1c": b1c,
                "b2c": b2c, "onescl": C["onescl"]} for c in range(NCORE)]
    rq = _run("l0qk", build_l0qk, in_maps)
    ra = [{**rn[c], **rq[c]} for c in range(NCORE)]

    in_maps_b = []
    groups = {}
    for g in range(2):
        kT_full = np.concatenate([ra[4 * g + i]["kT"] for i in range(4)], axis=1)
        kTt = np.ascontiguousarray(
            kT_full.reshape(8, 128, 64, 128).transpose(2, 0, 1, 3))
        xnTf = np.concatenate([ra[4 * g + i]["xnT"] for i in range(4)], axis=0)
        xnTt = np.ascontiguousarray(
            xnTf.reshape(64, 128, 8, 128).transpose(2, 0, 1, 3))
        groups[g] = (kTt, xnTt)
    for c in range(NCORE):
        g = c // 4
        kTt, xnTt = groups[g]
        in_maps_b.append({
            "qT": ra[c]["qT"], "brow": ra[c]["brow"], "kTt": kTt, "xnTt": xnTt,
            "W3": W3, "W4": W4, "b3c": b3c, "b4c": b4c, "xF": xFs[c],
            "onesr": C["onesr"], "ones128": C["ones128"]})
    rb = _run("l0flash", build_l0flash, in_maps_b)

    return {"ra": ra, "rb": rb, "xFs": xFs}


def run_wq(inputs):
    C = _consts()
    w10 = np.asarray(inputs["L0_ffn_w1"], np.float32)
    w20 = np.asarray(inputs["L0_ffn_w2"], np.float32)
    w11 = np.asarray(inputs["L1_ffn_w1"], np.float32)
    w21 = np.asarray(inputs["L1_ffn_w2"], np.float32)

    def slices(c):
        return {
            "wsl0": np.ascontiguousarray(w10[c * 512:(c + 1) * 512]),
            "wsl1": np.ascontiguousarray(w20[c * 128:(c + 1) * 128]),
            "wsl2": np.ascontiguousarray(w11[c * 512:(c + 1) * 512]),
            "wsl3": np.ascontiguousarray(w21[c * 128:(c + 1) * 128]),
        }

    in_maps = [{**slices(c), "onescl": C["onescl"]} for c in range(NCORE)]
    r0 = _run("wstat", build_wstat, in_maps)
    parts = np.concatenate([r0[c]["part"] for c in range(NCORE)], axis=0)
    in_maps = [{**slices(c), "parts": parts, "identc": C["identc"],
                "onesr": C["onesr"]} for c in range(NCORE)]
    r1 = _run("wquant", build_wquant, in_maps)

    w1qT0 = np.concatenate([r1[c]["wq0"] for c in range(NCORE)], axis=1)
    w2qT0 = np.concatenate([r1[c]["wq1"] for c in range(NCORE)], axis=1)
    w1qT1 = np.concatenate([r1[c]["wq2"] for c in range(NCORE)], axis=1)
    w2qT1 = np.concatenate([r1[c]["wq3"] for c in range(NCORE)], axis=1)
    invw = r1[0]["invw"]

    def tile_w2(w2qT):
        return np.ascontiguousarray(
            w2qT.reshape(32, 128, 8, 128).transpose(2, 0, 1, 3))

    return {
        "w1qT0": np.ascontiguousarray(w1qT0), "w2qTt0": tile_w2(w2qT0),
        "w1qT1": np.ascontiguousarray(w1qT1), "w2qTt1": tile_w2(w2qT1),
        "invw": invw,
    }


def run_ffn(layer, xFs, accs, wq, inputs):
    C = _consts()
    cw = np.ascontiguousarray(
        np.asarray(inputs[f"L{layer}_ffn_conv_w"], np.float32).reshape(DH, 3))
    cb = np.ascontiguousarray(
        np.asarray(inputs[f"L{layer}_ffn_conv_b"], np.float32)
        .reshape(32, 128).T)
    w1qT = wq[f"w1qT{layer}"]
    w2qTt = wq[f"w2qTt{layer}"]
    invw = wq["invw"]
    iv1 = np.ascontiguousarray(invw[:, 2 * layer:2 * layer + 1])
    iv2 = np.ascontiguousarray(invw[:, 2 * layer + 1:2 * layer + 2])
    zero = np.zeros((D, 1), np.float32)
    in_maps = []
    for c in range(NCORE):
        sc = c % 4
        xhl = zero if sc == 0 else np.ascontiguousarray(xFs[c - 1][:, -1:])
        xhr = zero if sc == 3 else np.ascontiguousarray(xFs[c + 1][:, 0:1])
        in_maps.append({
            "xF": xFs[c], "xhl": xhl, "xhr": xhr, "accIn": accs[c],
            "w1qT": w1qT, "w2qTt": w2qTt, "invw1": iv1, "invw2": iv2,
            "cwc": cw, "cbc": cb, "onesr": C["onesr"], "onescl": C["onescl"],
            "identc": C["identc"]})
    rd = _run("ffn", build_ffn, in_maps)
    return [rd[c]["xout"] for c in range(NCORE)]


def _l1_consts():
    tri = np.tril(np.ones((32, 32), np.float32), -1)
    return {
        "negI": -np.eye(32, dtype=np.float32),
        "mslneg": np.ascontiguousarray(-np.tile(tri, (1, 16))),
        "mtriu": np.ascontiguousarray(
            np.tile(np.triu(np.ones((32, 32), np.float32), 0), (1, 16))),
    }


def run_l1head(xn1_full_g, inputs, debug=False):
    C = _consts()
    C1 = _l1_consts()
    Wq = np.asarray(inputs["L1_attn_Wq"], np.float32)
    Wk = np.asarray(inputs["L1_attn_Wk"], np.float32)
    Wv = np.asarray(inputs["L1_attn_Wv"], np.float32)
    Wb = np.asarray(inputs["L1_attn_Wb"], np.float32)
    cq = np.asarray(inputs["L1_attn_conv_q"], np.float32)
    ck = np.asarray(inputs["L1_attn_conv_k"], np.float32)
    cv = np.asarray(inputs["L1_attn_conv_v"], np.float32)
    fs = np.asarray(inputs["L1_attn_fir_short"], np.float32)
    fl = np.asarray(inputs["L1_attn_fir_long"], np.float32)
    in_maps = []
    for c in range(NCORE):
        g, h = c // 4, c % 4
        sl = slice(h * 256, (h + 1) * 256)
        in_maps.append({
            "xn1": xn1_full_g[g],
            "Wq": np.ascontiguousarray(Wq[:, sl]),
            "Wk": np.ascontiguousarray(Wk[:, sl]),
            "Wv": np.ascontiguousarray(Wv[:, sl]),
            "Wb": np.ascontiguousarray(Wb[:, h:h + 1]),
            "cq": np.ascontiguousarray(cq[sl]),
            "ck": np.ascontiguousarray(ck[sl]),
            "cv": np.ascontiguousarray(cv[sl]),
            "firs": np.ascontiguousarray(fs[h]),
            "firl": np.ascontiguousarray(fl[h]),
            "negI": C1["negI"], "mslneg": C1["mslneg"], "mtriu": C1["mtriu"],
            "onesr": C["onesr"], "onescl": C["onescl"], "identc": C["identc"],
        })
    if debug:
        return _run("l1head_dbg", lambda: build_l1head(debug=True), in_maps)
    return _run("l1head", build_l1head, in_maps)


def run_norm(xFs):
    C = _consts()
    in_maps = [{"xF": xFs[c], "onesr": C["onesr"], "onescl": C["onescl"],
                "identc": C["identc"]} for c in range(NCORE)]
    return _run("l0norm", build_l0norm, in_maps)


def run_router(xn1Fs, statrows, inputs):
    C = _consts()
    Wr1 = np.asarray(inputs["L1_attn_Wr1"], np.float32)
    br1 = np.asarray(inputs["L1_attn_br1"], np.float32)
    Wr2 = np.asarray(inputs["L1_attn_Wr2"], np.float32)
    br2 = np.asarray(inputs["L1_attn_br2"], np.float32)
    ltg = np.asarray(inputs["L1_attn_log_tau_group"], np.float32)
    Wid = np.asarray(inputs["L1_attn_Wid"], np.float32)
    bid = np.asarray(inputs["L1_attn_bid"], np.float32)
    alpha = np.asarray(inputs["L1_attn_alpha_id"], np.float32)

    perm = [(r % 4) * 3 + r // 4 for r in range(12)]
    Wr2p = np.ascontiguousarray(Wr2[:, perm])
    br2p = np.ascontiguousarray(br2[perm].reshape(3, 4).T)
    br1c = _bias_tile(br1, 16)
    ltg4 = np.ascontiguousarray(ltg[[0, 0, 1, 1]].reshape(4, 1))
    bid4 = np.ascontiguousarray(bid.reshape(4, 1))
    alpha4 = np.ascontiguousarray(alpha.reshape(4, 1))

    in_maps = []
    for c in range(NCORE):
        g, sc = c // 4, c % 4
        cols = slice(sc * TPC, (sc + 1) * TPC)
        st24 = np.ascontiguousarray(np.concatenate(
            [np.stack([statrows[4 * g + h][tt] for h in range(4)])
             for tt in range(6)], axis=0)[:, cols])
        in_maps.append({
            "xn1": xn1Fs[c], "st24": st24, "Wr1": Wr1, "br1c": br1c,
            "Wr2p": Wr2p, "br2p": br2p, "ltg4": ltg4, "Wid": Wid,
            "bid4": bid4, "alpha4": alpha4})
    return _run("router", build_router, in_maps)


def run_mix(rf, rg, inputs):
    C = _consts()
    onw = np.asarray(inputs["L1_attn_o_norm_w"], np.float32)
    onwc = np.ascontiguousarray(onw.reshape(2, 128).T)
    in_maps = []
    for c in range(NCORE):
        g, h = c // 4, c % 4
        rows = [np.concatenate([rg[4 * g + sc][f"p{e}"][h:h + 1, :]
                                for sc in range(4)], axis=1) for e in range(3)]
        rows.append(np.concatenate([rg[4 * g + sc]["ids"][h:h + 1, :]
                                    for sc in range(4)], axis=1))
        p4 = np.ascontiguousarray(np.concatenate(rows, axis=0))
        in_maps.append({
            "vF": rf[c]["vF"], "fsF": rf[c]["fsF"], "flF": rf[c]["flF"],
            "dF": rf[c]["dF"], "p4": p4, "onwc": onwc,
            "onesr": C["onesr"], "onescl": C["onescl"]})
    return _run("mix", build_mix, in_maps)


def run_l1out(rh, x1Fs, inputs):
    Wo = np.asarray(inputs["L1_attn_Wo"], np.float32)
    in_maps = []
    for c in range(NCORE):
        g, sc = c // 4, c % 4
        cols = slice(sc * TPC, (sc + 1) * TPC)
        oN = np.ascontiguousarray(np.concatenate(
            [rh[4 * g + h]["oN"][:, cols] for h in range(4)], axis=0))
        in_maps.append({"oN": oN, "Wo": Wo, "x1F": x1Fs[c]})
    return _run("l1out", build_l1out, in_maps)


def kernel(**inputs):
    x = np.asarray(inputs["L0_attn_x"])

    wq = run_wq(inputs)

    r0 = run_l0(inputs)
    xFs = r0["xFs"]
    acc1s = [r0["rb"][c]["acc"] for c in range(NCORE)]
    x1Fs = run_ffn(0, xFs, acc1s, wq, inputs)

    rn1 = run_norm(x1Fs)
    xn1Fs = [rn1[c]["xnF"] for c in range(NCORE)]
    xn1g = {g: np.ascontiguousarray(np.concatenate(
        [xn1Fs[4 * g + i] for i in range(4)], axis=1)) for g in range(2)}

    rf = run_l1head(xn1g, inputs)
    statrows = [rf[c]["stats"] for c in range(NCORE)]
    rg = run_router(xn1Fs, statrows, inputs)
    rh = run_mix(rf, rg, inputs)
    ri = run_l1out(rh, x1Fs, inputs)
    acc2s = [ri[c]["acc"] for c in range(NCORE)]

    x2Fs = run_ffn(1, x1Fs, acc2s, wq, inputs)

    out = np.empty((B, S, D), np.float32)
    for c in range(NCORE):
        bc, sc = c // 4, c % 4
        out[bc, sc * TPC:(sc + 1) * TPC, :] = x2Fs[c].T
    return out
